# version 10
# baseline (speedup 1.0000x reference)
import numpy as np
from contextlib import ExitStack

import concourse.bass as bass
import concourse.mybir as mybir
from concourse.bass_utils import run_bass_kernel_spmd

F32 = mybir.dt.float32
BF16 = mybir.dt.bfloat16
AF = mybir.ActivationFunctionType
ALU = mybir.AluOpType
AX = mybir.AxisListType

P = 128
B, S, D = 32, 2048, 1024
NCORES = 8
BL = B // NCORES
NT = S // P
NC = D // P
NG = S // 512
H, DK, DV, SEG, NSEGS = 8, 128, 128, 512, 4
LN_EPS = 1e-5
B_EPS = 1e-6

ENGS = ("sync", "gpsimd", "tensor", "vector", "scalar")


class Plan:

    def __init__(self):
        self.ops = {e: [] for e in ENGS}
        self.cnt = {s: 0 for s in ("sPE", "sDV", "sAC", "sGP", "sXD", "sGD", "sST")}
        self.seen = {e: {} for e in ENGS}
        self.eng_sem = {"tensor": "sPE", "vector": "sDV", "scalar": "sAC",
                        "gpsimd": "sGP"}

    def add(self, eng, fn, waits=(), inc=None, n=1, tag="", attach=True):
        w = []
        for wv in waits:
            if wv is None:
                continue
            (s, v) = wv
            if v <= 0:
                continue
            if self.seen[eng].get(s, 0) >= v:
                continue
            self.seen[eng][s] = v
            w.append((s, v))
        incs = []
        if inc is not None:
            self.cnt[inc] += n
            incs.append((inc, n, attach))
        self.ops[eng].append((fn, w, incs, tag))
        return (inc, self.cnt[inc]) if inc is not None else None

    def cur(self, s):
        return (s, self.cnt[s])

    def barrier_waits(self, *sems):
        return [(s, self.cnt[s]) for s in sems]

    def simulate(self):
        pos = {e: 0 for e in ENGS}
        val = {s: 0 for s in self.cnt}
        progress = True
        while progress:
            progress = False
            for e in ENGS:
                while pos[e] < len(self.ops[e]):
                    fn, w, incs, tag = self.ops[e][pos[e]]
                    if any(val[s] < v for (s, v) in w):
                        break
                    for (s, v, _a) in incs:
                        val[s] += v
                    pos[e] += 1
                    progress = True
        stuck = {e: pos[e] for e in ENGS if pos[e] < len(self.ops[e])}
        if stuck:
            msg = []
            for e, p_ in stuck.items():
                fn, w, incs, tag = self.ops[e][p_]
                msg.append(f"{e}@{p_}/{len(self.ops[e])} tag={tag} waits={w} "
                           f"sems={[(s, val[s]) for s, _ in w]}")
            raise RuntimeError("PLAN DEADLOCK:\n" + "\n".join(msg))


FFN_LEVEL = 3
DBG_SEQ = 0

def build_nc(n_sub=4, dbg=False, ffn_level=None):
    if ffn_level is None:
        ffn_level = FFN_LEVEL
    nc = bass.Bass("TRN2", target_bir_lowering=False,
                   dynamic_dma_scratch_size=8192)

    x_d = nc.dram_tensor("x", [BL, S, D], F32, kind="ExternalInput").ap()
    out_d = nc.dram_tensor("out", [BL, S, D], F32, kind="ExternalOutput").ap()
    wqaT_d = nc.dram_tensor("wqaT", [D, D], F32, kind="ExternalInput").ap()
    wkaT_d = nc.dram_tensor("wkaT", [D, D], F32, kind="ExternalInput").ap()
    wrT_d = nc.dram_tensor("wrT", [D, D], F32, kind="ExternalInput").ap()
    wkfT_d = nc.dram_tensor("wkfT", [D, D], F32, kind="ExternalInput").ap()
    wvfT_d = nc.dram_tensor("wvfT", [D, D], F32, kind="ExternalInput").ap()
    wqb_d = nc.dram_tensor("wqb", [D, D], F32, kind="ExternalInput").ap()
    wkb_d = nc.dram_tensor("wkb", [D, D], F32, kind="ExternalInput").ap()
    wvb_d = nc.dram_tensor("wvb", [D, D], F32, kind="ExternalInput").ap()
    smalls_d = nc.dram_tensor("smalls", [P, 8, NC], F32, kind="ExternalInput").ap()
    if dbg:
        dbg_d = {k: nc.dram_tensor(f"dbg_{k}", shp, F32, kind="ExternalOutput").ap()
                 for k, shp in [("stats", [P, NT, 8]), ("xn", [P, NT, D]),
                                ("xpx", [P, NT, D]), ("K", [P, NC, S]),
                                ("csum", [1, D]), ("zbuf", [P, NT, 2]),
                                ("E8", [P, S]), ("q2", [P, NC, 512]),
                                ("xrxk", [P, 2, NC, 512]), ("r", [P, NC, 512]),
                                ("k2", [P, NC, 512]), ("rc", [P, NC, 512]),
                                ("xnfm", [P, 2, NC, 516])]}

    ctx = ExitStack()
    sb = lambda *a: ctx.enter_context(nc.sbuf_tensor(*a))
    ps = lambda *a: ctx.enter_context(nc.psum_tensor(*a))

    x_tm = sb("x_tm", [P, NT, D], F32)
    xn_tm = sb("xn_tm", [P, NT, D], BF16)
    xn_fm = sb("xn_fm", [P, 2, NC, 516], BF16)
    a3 = sb("a3", [P, 32768], mybir.dt.uint8)
    warena = sb("warena", [P, 3, NC, D], BF16)
    ebuf = sb("ebuf", [P, 2, S], BF16)
    ptbuf = sb("ptbuf", [P, 17, P], BF16)
    stats = sb("stats", [P, NT, 8], F32)
    smalls = sb("smalls_s", [P, 8, NC], F32)
    dsm = sb("dsm", [P, 8, NC], F32)
    identb = sb("identb", [P, P], BF16)
    tril01 = sb("tril01", [P, P], BF16)
    identf = sb("identf", [P, P], F32)
    onescol = sb("onescol", [P, 1], BF16)
    epscol = sb("epscol", [P, 1], F32)
    bepscol = sb("bepscol", [P, 1], F32)
    zbuf = sb("zbuf", [P, NT, 2], F32)
    zpart = sb("zpart", [P, 8], F32)
    nzcol = sb("nzcol", [P, 2], BF16)
    zrow = sb("zrow", [1, 2, P], BF16)
    colsum = sb("colsum", [1, D], BF16)
    wst0 = sb("wst0", [P, 1024], F32)
    junkb = sb("junkb", [P, 1024], BF16)

    K_fm = a3.ap().bitcast(BF16).rearrange("p (c s) -> p c s", c=NC)
    junk = ebuf.ap().bitcast(F32)[:, 0, :]

    pb = [ps(f"pb{i}", [P, 512], F32) for i in range(8)]
    pbb = [pb[i].ap().bitcast(BF16) for i in range(8)]

    sems = {}
    for s in ("sPE", "sDV", "sAC", "sGP", "sXD", "sGD", "sST"):
        sems[s] = ctx.enter_context(nc.semaphore(s))

    p_ = Plan()
    bank_war = {}
    buf_war = {}

    def mm_group(bank, mms, waits=(), tag=""):
        w = list(waits)
        if bank in bank_war and bank_war[bank]:
            w.append(bank_war[bank])
        last = None
        n = len(mms)
        for i, m in enumerate(mms):
            lhsT, rhs = m[0], m[1]
            out = m[2] if len(m) > 2 else pb[bank].ap()
            st, sp = (i == 0), (i == n - 1)
            def f(e, out=out, lhsT=lhsT, rhs=rhs, st=st, sp=sp):
                return e.matmul(out, lhsT, rhs, start=st, stop=sp)
            last = p_.add("tensor", f, waits=(w if i == 0 else ()),
                          inc=("sPE" if sp else None), tag=tag)
        return last

    def tr_group(bank, trs, waits=(), dtype=BF16, tag=""):
        w = list(waits)
        if bank in bank_war and bank_war[bank]:
            w.append(bank_war[bank])
        ident = identb[:] if dtype == BF16 else identf[:]
        last = None
        for i, (out, in_) in enumerate(trs):
            def f(e, out=out, in_=in_, ident=ident):
                return e.transpose(out, in_, ident)
            last = p_.add("tensor", f, waits=(w if i == 0 else ()),
                          inc=("sPE" if i == len(trs) - 1 else None), tag=tag)
        return last

    def evac(bank, eng, fn, waits=(), tag=""):
        sem = p_.eng_sem[eng]
        mile = p_.add(eng, fn, waits=waits, inc=sem, tag=tag)
        bank_war[bank] = mile
        return mile

    def dv(fn, waits=(), tag="", inc=True):
        return p_.add("vector", fn, waits=waits, inc=("sDV" if inc else None), tag=tag)

    def ac(fn, waits=(), tag="", inc=True):
        return p_.add("scalar", fn, waits=waits, inc=("sAC" if inc else None), tag=tag)

    def gp(fn, waits=(), tag="", inc=True):
        return p_.add("gpsimd", fn, waits=waits, inc=("sGP" if inc else None), tag=tag)

    def sy_dma(out, in_, waits=(), store=False, tag=""):
        waits = list(waits) + [p_.cur("sXD"), p_.cur("sST")]
        def f(e, out=out, in_=in_):
            return e.dma_start(out=out, in_=in_)
        return p_.add("sync", f, waits=waits, inc=("sST" if store else "sXD"),
                      n=16, tag=tag)

    def hard_barrier(tag=""):
        arr = {}
        for e in ("tensor", "vector", "scalar", "gpsimd"):
            s = p_.eng_sem[e]
            def mk(s=s):
                return lambda en: en.sem_inc(sems[s], 1)
            arr[e] = p_.add(e, mk(), inc=s, tag=f"{tag}barA", attach=False)
        dmaw = [p_.cur("sGD"), p_.cur("sXD"), p_.cur("sST")]
        allw = list(arr.values()) + dmaw
        for e in ("tensor", "vector", "scalar", "gpsimd", "sync"):
            p_.add(e, (lambda en: en.wait_ge(sems["sGP"], 1)), waits=allw,
                   tag=f"{tag}barW")

    def sc_dma(out, in_, waits=(), tag=""):
        waits = list(waits) + [p_.cur("sGD")]
        def f(e, out=out, in_=in_):
            return e.dma_start(out=out, in_=in_)
        return p_.add("scalar", f, waits=waits, inc="sGD", n=16, tag=tag)

    def gp_dma(out, in_, waits=(), tag=""):
        waits = list(waits) + [p_.cur("sGD")]
        def f(e, out=out, in_=in_):
            return e.dma_start(out=out, in_=in_)
        return p_.add("gpsimd", f, waits=waits, inc="sGD", n=16, tag=tag)

    sy_dma(smalls[:], smalls_d, tag="ld_smalls")
    sm_ready = p_.cur("sXD")

    m1 = gp(lambda e: e.memset(identb[:], 0.0), tag="identb0")
    gp(lambda e: e.affine_select(out=identb[:], in_=identb[:],
                                 compare_op=ALU.not_equal, fill=1.0, base=0,
                                 pattern=[[-1, P]], channel_multiplier=1),
       waits=[m1], tag="identb")
    mt = gp(lambda e: e.memset(tril01[:], 1.0), tag="tril0")
    gp(lambda e: e.affine_select(out=tril01[:], in_=tril01[:],
                                 compare_op=ALU.is_ge, fill=0.0, base=0,
                                 pattern=[[-1, P]], channel_multiplier=1),
       waits=[mt], tag="tril")
    m2 = gp(lambda e: e.memset(identf[:], 0.0), tag="identf0")
    gp(lambda e: e.affine_select(out=identf[:], in_=identf[:],
                                 compare_op=ALU.not_equal, fill=1.0, base=0,
                                 pattern=[[-1, P]], channel_multiplier=1),
       waits=[m2], tag="identf")
    gp(lambda e: e.memset(onescol[:], 1.0), tag="ones")
    gp(lambda e: e.memset(epscol[:], LN_EPS), tag="eps")
    gp(lambda e: e.memset(bepscol[:], B_EPS), tag="beps")
    gp(lambda e: e.memset(xn_fm[:], 0.0), tag="fmz")
    init_gp = p_.cur("sGP")

    dv(lambda e: e.tensor_scalar(out=dsm[:, 0, :], in0=smalls[:, 2, :],
                                 scalar1=-1.0, scalar2=1.0, op0=ALU.mult,
                                 op1=ALU.add), waits=[sm_ready], tag="onemur")
    dv(lambda e: e.tensor_scalar(out=dsm[:, 1, :], in0=smalls[:, 3, :],
                                 scalar1=-1.0, scalar2=1.0, op0=ALU.mult,
                                 op1=ALU.add), tag="onemuk")
    ac(lambda e: e.activation(out=dsm[:, 2, :], in_=smalls[:, 7, :],
                              func=AF.Sigmoid), waits=[sm_ready], tag="gate")
    gate_done = p_.cur("sAC")
    dv(lambda e: e.tensor_scalar(out=dsm[:, 3, :], in0=dsm[:, 2, :],
                                 scalar1=-1.0, scalar2=1.0, op0=ALU.mult,
                                 op1=ALU.add), waits=[gate_done], tag="onegate")
    dv(lambda e: e.tensor_scalar(out=dsm[:, 6, :], in0=smalls[:, 4, :],
                                 scalar1=-1.0, scalar2=None, op0=ALU.mult),
       tag="negbq")
    dv(lambda e: e.tensor_scalar(out=dsm[:, 7, :], in0=smalls[:, 5, :],
                                 scalar1=-1.0, scalar2=None, op0=ALU.mult),
       tag="negbk")
    init_dv = p_.cur("sDV")
    init_ac = p_.cur("sAC")

    armed = []

    class LoadW:
        def __init__(self, dram_ap, slot, q=0, tag=""):
            self.dram_ap, self.slot, self.q, self.tag = dram_ap, slot, q, tag
            self.c = NC - 1
            self.gate = []
            self.last = None
            self.mile = None

        def step(self):
            if self.c < 0:
                return
            c = self.c
            self.c -= 1
            dma = sy_dma if self.q == 0 else sc_dma
            wf = warena.ap()[:, self.slot, :, :].rearrange(
                "p c e -> p (c e)").bitcast(F32)
            stage = wf[:, 0:1024] if c >= 2 else wst0.ap()
            w = list(self.gate) + ([self.last] if self.last else [])
            if c < 2:
                w = w + buf_war.get("wst", [])
            ld = dma(stage, self.dram_ap[128 * c:128 * (c + 1), :], waits=w,
                     tag=f"{self.tag}c{c}")
            self.last = ac(lambda e, slot=self.slot, c=c, stage=stage:
                           e.activation(out=warena[:, slot, c, :], in_=stage,
                                        func=AF.Identity),
                           waits=[ld], tag=f"{self.tag}x{c}")
            if c < 2:
                buf_war["wst"] = [self.last]
            if self.c < 0:
                self.mile = self.last

        def fin(self):
            while self.c >= 0:
                self.step()
            if self in armed:
                armed.remove(self)
            return self.mile

    def arm(ldr, gate):
        ldr.gate = list(gate)
        armed.append(ldr)

    def wstep():
        for ldr in list(armed):
            ldr.step()
            if ldr.c < 0:
                armed.remove(ldr)

    slot_free_gate = lambda: p_.barrier_waits("sPE", "sAC", "sDV")

    NSTORE = 0
    carry_qa = carry_ka = None
    stats_done = {}
    stats_guard = []

    for b in range(BL):
        if n_sub >= 4 and b > 0:
            x_ready = []
        else:
            xw = [p_.cur("sST")]
            x_mile = sy_dma(x_tm[:], x_d[b].rearrange("(t p) d -> p t d", p=P),
                            waits=xw, tag=f"ldx{b}")
            x_ready = [x_mile]

        def stats_tile(i, waits, tag):
            w = list(waits) + stats_guard
            s1m = dv(lambda e, i=i: e.tensor_reduce(
                stats[:, i, 0:1], x_tm[:, i, :], axis=AX.X, op=ALU.add),
                waits=w, tag=f"{tag}S1_{i}")
            s2m = ac(lambda e, i=i: e.activation(
                out=junkb.ap(), in_=x_tm[:, i, :], func=AF.Square,
                accum_out=stats[:, i, 1:2]),
                waits=w + buf_war.get("junkb", []), tag=f"{tag}S2_{i}")
            buf_war["junkb"] = [s2m]
            stats_done[i] = (s1m, s2m)

        def ln_stats(xpx, pre_waits, tag):
            for i in range(NT):
                if i not in stats_done:
                    stats_tile(i, pre_waits + [sm_ready], tag)
            gp_m = None
            sw = [m for i in range(NT) for m in stats_done[i]]
            stats_done.clear()
            m1 = dv(lambda e: e.tensor_scalar(out=stats[:, :, 2:3],
                                              in0=stats[:, :, 0:1],
                                              scalar1=1.0 / D, scalar2=None,
                                              op0=ALU.mult),
                    waits=sw, tag=f"{tag}m")
            dv(lambda e: e.tensor_scalar(out=stats[:, :, 4:5],
                                         in0=stats[:, :, 1:2],
                                         scalar1=1.0 / D, scalar2=None,
                                         op0=ALU.mult), tag=f"{tag}s2n")
            s2 = dv(lambda e: e.tensor_tensor(out=stats[:, :, 3:4],
                                              in0=stats[:, :, 2:3],
                                              in1=stats[:, :, 2:3],
                                              op=ALU.mult),
                    waits=[m1], tag=f"{tag}m2")
            var = dv(lambda e: e.tensor_tensor(out=stats[:, :, 3:4],
                                               in0=stats[:, :, 4:5],
                                               in1=stats[:, :, 3:4],
                                               op=ALU.subtract),
                     waits=[s2], tag=f"{tag}var")
            lnv = ac(lambda e: e.activation(out=stats[:, :, 4:5],
                                            in_=stats[:, :, 3:4], func=AF.Ln,
                                            bias=epscol[:, 0:1]),
                     waits=[var, init_gp], tag=f"{tag}ln")
            rstd = ac(lambda e: e.activation(out=stats[:, :, 5:6],
                                             in_=stats[:, :, 4:5], func=AF.Exp,
                                             scale=-0.5),
                      waits=[lnv], tag=f"{tag}rstd")
            mrn = dv(lambda e: e.scalar_tensor_tensor(out=stats[:, :, 6:7],
                                                      in0=stats[:, :, 2:3],
                                                      scalar=-1.0,
                                                      in1=stats[:, :, 5:6],
                                                      op0=ALU.mult,
                                                      op1=ALU.mult),
                     waits=[rstd], tag=f"{tag}mrn")
            opr = None
            if xpx:
                opr = dv(lambda e: e.tensor_scalar(out=stats[:, :, 7:8],
                                                   in0=stats[:, :, 5:6],
                                                   scalar1=1.0, scalar2=None,
                                                   op0=ALU.add),
                         waits=[rstd], tag=f"{tag}opr")
            settle = dv(lambda e: e.tensor_scalar(out=stats[:, :, 4:5],
                                                  in0=stats[:, :, 6:7],
                                                  scalar1=0.0, scalar2=None,
                                                  op0=ALU.add),
                        waits=[mrn] + ([opr] if opr else []),
                        tag=f"{tag}settle")
            stats_guard[:] = [settle]
            return settle, (settle if xpx else None)

        def ln_apply(i, dst, mrn, extra_waits, tag):
            if i % 2 == 0:
                return ac(lambda e, i=i, dst=dst: e.activation(
                    out=xn_tm[:, dst, :], in_=x_tm[:, i, :], func=AF.Identity,
                    scale=stats[:, i, 5:6], bias=stats[:, i, 6:7]),
                    waits=[mrn] + list(extra_waits), tag=f"{tag}xn{i}")
            return dv(lambda e, i=i, dst=dst: e.tensor_scalar(
                out=xn_tm[:, dst, :], in0=x_tm[:, i, :],
                scalar1=stats[:, i, 5:6], scalar2=stats[:, i, 6:7],
                op0=ALU.mult, op1=ALU.add),
                waits=[mrn] + list(extra_waits), tag=f"{tag}xn{i}")

        def ln_xpx(i, opr, apl, tag):
            if i % 2 == 1:
                return ac(lambda e, i=i: e.activation(
                    out=x_tm[:, i, :], in_=x_tm[:, i, :], func=AF.Identity,
                    scale=stats[:, i, 7:8], bias=stats[:, i, 6:7]),
                    waits=[opr, apl], tag=f"{tag}xpx{i}")
            return dv(lambda e, i=i: e.tensor_scalar(
                out=x_tm[:, i, :], in0=x_tm[:, i, :],
                scalar1=stats[:, i, 7:8], scalar2=stats[:, i, 6:7],
                op0=ALU.mult, op1=ALU.add),
                waits=[opr, apl], tag=f"{tag}xpx{i}")

        def fm_transpose_group(g, buf, src_tiles, xn_m, tag, guard_from=None):
            miles = []
            tm = None
            for half in range(4):
                dc0 = 2 * half
                bnk = 6 + (half % 2)
                trs = []
                for dci in range(2):
                    dc = dc0 + dci
                    for t in range(4):
                        out = pbb[bnk][:, 512 * dci + 128 * t:
                                       512 * dci + 128 * t + 128]
                        in_ = xn_tm[:, src_tiles[t], 128 * dc:128 * dc + 128]
                        trs.append((out, in_))
                w = list(xn_m)
                tm = tr_group(bnk, trs, waits=w, tag=f"{tag}tr{half}")
                eng = "vector" if half % 2 == 0 else "scalar"
                if eng == "vector":
                    def f(e, bnk=bnk, buf=buf, dc0=dc0):
                        return e.tensor_copy(xn_fm[:, buf, dc0:dc0 + 2, 4:516],
                                             pbb[bnk][:, 0:1024])
                else:
                    def f(e, bnk=bnk, buf=buf, dc0=dc0):
                        return e.copy(xn_fm[:, buf, dc0:dc0 + 2, 4:516],
                                      pbb[bnk][:, 0:1024])
                miles.append(evac(bnk, eng, f, waits=[tm], tag=f"{tag}ev{half}"))
            return miles, tm

        aw_pre = p_.barrier_waits("sPE", "sAC")
        mrn, opr = ln_stats(xpx=True,
                            pre_waits=x_ready + [p_.cur("sDV"), p_.cur("sAC")],
                            tag=f"b{b}ln1_")
        if carry_qa is None:
            LQA = LoadW(wqaT_d, 0, 0, f"b{b}ldwqa")
            LKA = LoadW(wkaT_d, 1, 1, f"b{b}ldwka")
            arm(LQA, slot_free_gate() if b > 0 else [])
            arm(LKA, slot_free_gate() if b > 0 else [])
        else:
            LQA, LKA = carry_qa, carry_ka
        carry_qa = carry_ka = None
        if n_sub >= 2:
            f1_ldrs = (LoadW(wrT_d, 0, 0, f"b{b}f1_ldwr"),
                       LoadW(wkfT_d, 1, 1, f"b{b}f1_ldwk"),
                       LoadW(wvfT_d, 2, 0, f"b{b}f1_ldwv"))
        else:
            f1_ldrs = None
        w_ka = LKA.fin()
        xn_m = [ln_apply(i, i, mrn, [], tag=f"b{b}ln1_") for i in range(NT)]
        xpx_all = [ln_xpx(i, opr, xn_m[i], tag=f"b{b}ln1_") for i in range(NT)]
        xpx_m = xpx_all[-2:]

        kproj_m = []
        for g in range(NG):
            fm_m, _ = fm_transpose_group(g, g % 2, [4 * g + t for t in range(4)],
                                         xn_m[4 * g + 2:4 * g + 4],
                                         tag=f"b{b}a1g{g}")
            for ec in range(NC):
                bnk = ec % 2
                mms = [(warena[:, 1, dc, 128 * ec:128 * ec + 128],
                        xn_fm[:, g % 2, dc, 4:516]) for dc in range(NC)]
                gm = mm_group(bnk, mms, waits=fm_m + [w_ka], tag=f"b{b}K{g}_{ec}")
                def f(e, bnk=bnk, ec=ec, g=g):
                    return e.activation(out=K_fm[:, ec, 512 * g:512 * g + 512],
                                        in_=pb[bnk].ap(), func=AF.Identity,
                                        bias=smalls[:, 1, ec:ec + 1])
                kproj_m.append(evac(bnk, "scalar", f, waits=[gm],
                                    tag=f"b{b}Kev{g}_{ec}"))

        if f1_ldrs:
            arm(f1_ldrs[1], slot_free_gate())
        cs_m = []
        for dh in range(2):
            mms = [(onescol[:, 0:1], xn_tm[:, kc, 512 * dh:512 * dh + 512],
                    pb[6].ap()[0:1, :]) for kc in range(NT)]
            gm = mm_group(6, mms, waits=xn_m[-2:] + [init_gp],
                          tag=f"b{b}cs{dh}")
            def f(e, dh=dh):
                return e.tensor_copy(colsum[0:1, 512 * dh:512 * dh + 512],
                                     pb[6].ap()[0:1, :])
            cs_m.append(evac(6, "vector", f, waits=[gm], tag=f"b{b}csev{dh}"))

        k_all = kproj_m[-1]

        def emit_scores(c, c4, qg_m):
            eb = c % 2
            zp = 4 * (c % 2)
            exp_m = []
            for kb in range(4):
                bnk = kb % 2
                mms = [(warena[:, 2, ec, 128 * c4:128 * c4 + 128],
                        K_fm[:, ec, 512 * kb:512 * kb + 512])
                       for ec in range(NC)]
                gm = mm_group(bnk, mms,
                              waits=qg_m + [k_all] + buf_war.get(
                                  ("ebuf", eb), []),
                              tag=f"b{b}s{c}_{kb}")
                def f(e, bnk=bnk, eb=eb, kb=kb, zp=zp):
                    return e.activation(
                        out=ebuf[:, eb, 512 * kb:512 * kb + 512],
                        in_=pb[bnk].ap(), func=AF.Exp, scale=1.0 / 32.0,
                        accum_out=zpart[:, zp + kb:zp + kb + 1])
                exp_m.append(evac(bnk, "scalar", f,
                                  waits=[gm] + buf_war.get(("zpart", eb), []),
                                  tag=f"b{b}e{c}_{kb}"))
            return exp_m

        def emit_tail(c, exp_m):
            eb = c % 2
            zp = 4 * (c % 2)
            zr = dv(lambda e, c=c, zp=zp: e.tensor_reduce(
                zbuf[:, c, 0:1], zpart[:, zp:zp + 4], axis=AX.X, op=ALU.add),
                waits=[exp_m[-1]], tag=f"b{b}Z{c}")
            buf_war[("zpart", eb)] = [zr]
            rz = dv(lambda e, c=c: e.reciprocal(zbuf[:, c, 1:2],
                                                zbuf[:, c, 0:1]),
                    waits=[zr], tag=f"b{b}rZ{c}")
            nz = dv(lambda e, c=c, eb=eb: e.tensor_scalar(
                out=nzcol[:, eb:eb + 1], in0=zbuf[:, c, 0:1],
                scalar1=-2.0 / S, scalar2=None, op0=ALU.mult),
                waits=[zr] + buf_war.get(("nzcol", eb), []),
                tag=f"b{b}nz{c}")
            ztr = tr_group(6, [(pbb[6][0:1, 0:P], nzcol[:, eb:eb + 1])],
                           waits=[nz], tag=f"b{b}ztr{c}")
            buf_war[("nzcol", eb)] = [ztr]
            zev = evac(6, "vector",
                       lambda e, eb=eb: e.tensor_copy(zrow[0:1, eb, :],
                                                      pbb[6][0:1, 0:P]),
                       waits=[ztr] + buf_war.get(("zrow", eb), []),
                       tag=f"b{b}zev{c}")
            pt_m = []
            for half in range(2):
                bnk = 2 + half
                trs = []
                for kci in range(8):
                    kc = 8 * half + kci
                    trs.append((pbb[bnk][:, 128 * kci:128 * kci + 128],
                                ebuf[:, eb, 128 * kc:128 * kc + 128]))
                tm = tr_group(bnk, trs, waits=[exp_m[-1]],
                              tag=f"b{b}pt{c}_{half}")
                def f(e, bnk=bnk, half=half):
                    return e.tensor_copy(ptbuf[:, 8 * half:8 * half + 8, :],
                                         pbb[bnk][:, 0:1024])
                pt_m.append(evac(bnk, "vector", f, waits=[tm],
                                 tag=f"b{b}ptev{c}_{half}"))
            buf_war[("ebuf", eb)] = [pt_m[-1]]
            for dh in range(2):
                bnk = 4 + dh
                mms = [(ptbuf[:, kc, :],
                        xn_tm[:, kc, 512 * dh:512 * dh + 512])
                       for kc in range(NT)]
                mms.append((zrow[0:1, eb, :],
                            colsum[0:1, 512 * dh:512 * dh + 512]))
                gm = mm_group(bnk, mms,
                              waits=pt_m + [zev] + cs_m, tag=f"b{b}pv{c}_{dh}")
                buf_war[("zrow", eb)] = [gm]
                def f(e, bnk=bnk, c=c, dh=dh):
                    return e.scalar_tensor_tensor(
                        out=x_tm[:, c, 512 * dh:512 * dh + 512],
                        in0=pb[bnk].ap(), scalar=zbuf[:, c, 1:2],
                        in1=x_tm[:, c, 512 * dh:512 * dh + 512],
                        op0=ALU.mult, op1=ALU.add)
                pv_ev = evac(bnk, "vector", f, waits=[gm, rz] + xpx_m,
                             tag=f"b{b}pvev{c}_{dh}")
            if n_sub >= 2:
                stats_tile(c, [pv_ev], f"b{b}f1_")

        pend = None
        w_qa = LQA.fin()
        for gi, g in enumerate([2, 3, 0, 1]):
            if gi >= 2:
                fm_m, _ = fm_transpose_group(g, g % 2,
                                             [4 * g + t for t in range(4)],
                                             xn_m[4 * g + 2:4 * g + 4],
                                             tag=f"b{b}a2g{g}")
            else:
                fm_m = []
            qg_m = []
            for ec in range(NC):
                bnk = ec % 2
                mms = [(warena[:, 0, dc, 128 * ec:128 * ec + 128],
                        xn_fm[:, g % 2, dc, 4:516]) for dc in range(NC)]
                gm = mm_group(bnk, mms, waits=fm_m + [w_qa], tag=f"b{b}Q{g}_{ec}")
                def f(e, bnk=bnk, ec=ec):
                    return e.activation(out=warena[:, 2, ec, 0:512],
                                        in_=pb[bnk].ap(), func=AF.Identity,
                                        bias=smalls[:, 0, ec:ec + 1])
                qg_m.append(evac(bnk, "scalar", f, waits=[gm],
                                 tag=f"b{b}Qev{g}_{ec}"))
            if gi == 3 and f1_ldrs:
                arm(f1_ldrs[0], slot_free_gate())
            for c4 in range(4):
                c = 4 * g + c4
                wstep()
                em = emit_scores(c, c4, qg_m)
                if pend is not None:
                    emit_tail(*pend)
                pend = (c, em)
        if f1_ldrs:
            arm(f1_ldrs[2], slot_free_gate())
        emit_tail(*pend)
        wstep()

        def emit_ffn(tag, ldrs, nxt, tile_hook=None, dbg_g0=False):
            if ffn_level < 0:
                return
            import os as _os0
            if _os0.environ.get("FFN_PRE2"):
                pre = p_.barrier_waits("sPE", "sAC")
            else:
                pre = p_.barrier_waits("sDV", "sAC", "sPE", "sGP", "sGD")

            xrxk = a3.ap()[:, 0:16384].bitcast(BF16).rearrange(
                "p (u c s) -> p u c s", u=2, c=NC)
            rc_fm = a3.ap()[:, 16384:32768].bitcast(F32).rearrange(
                "p (c s) -> p c s", c=NC)
            r_fm = xn_tm.ap()[:, 8:16, :].bitcast(F32)
            relu_b = ebuf.ap().rearrange("p u s -> p (u s)").rearrange(
                "p (c s) -> p c s", c=NC)

            if ffn_level == 0:
                return
            mrn2, _ = ln_stats(xpx=False, pre_waits=pre, tag=f"{tag}ln")
            w_r = ldrs[0].fin()
            w_k = ldrs[1].fin()
            w_v = ldrs[2].fin()
            gz = dv(lambda e: e.memset(xn_fm[:, 0, :, 3:4], 0.0), waits=pre,
                    tag=f"{tag}gz")
            guard_m = {0: gz}
            st_ = {"tr_prev": None, "mix_prev": None, "ab_prev": None,
                   "c_prev": None, "rc_prev": None}
            gstate = {}

            def s12(g):
                buf = g % 2
                aw = ([st_["tr_prev"]] if st_["tr_prev"] else [])
                xn_mg = [ln_apply(4 * g + t, t, mrn2, aw, tag=f"{tag}g{g}")
                         for t in range(4)]
                fm_m, tr_last = fm_transpose_group(
                    g, buf, [0, 1, 2, 3], xn_mg[-2:], tag=f"{tag}g{g}")
                st_["tr_prev"] = tr_last
                if g + 1 < NG:
                    gc = dv(lambda e, buf=buf: e.tensor_copy(
                        xn_fm[:, 1 - buf, :, 3:4], xn_fm[:, buf, :, 515:516]),
                        waits=fm_m + ([st_["mix_prev"]] if st_["mix_prev"] else []),
                        tag=f"{tag}gc{g}")
                    guard_m[g + 1] = gc
                mw = fm_m + [guard_m[g]] + ([st_["ab_prev"]] if st_["ab_prev"] else [])
                mix_m = None
                for dc in range(NC):
                    wstep()
                    for u, mucol, omcol in ((0, 2, 0), (1, 3, 1)):
                        tsm = dv(lambda e, dc=dc, u=u, omcol=omcol, buf=buf:
                           e.tensor_scalar(
                               out=xrxk[:, u, dc, :],
                               in0=xn_fm[:, buf, dc, 3:515],
                               scalar1=dsm[:, omcol, dc:dc + 1], scalar2=None,
                               op0=ALU.mult),
                           waits=mw, tag=f"{tag}mx{g}_{dc}_{u}")
                        mix_m = dv(lambda e, dc=dc, u=u, mucol=mucol, buf=buf:
                                   e.scalar_tensor_tensor(
                                       out=xrxk[:, u, dc, :],
                                       in0=xn_fm[:, buf, dc, 4:516],
                                       scalar=smalls[:, mucol, dc:dc + 1],
                                       in1=xrxk[:, u, dc, :],
                                       op0=ALU.mult, op1=ALU.add),
                                   waits=[tsm], tag=f"{tag}mxs{g}_{dc}_{u}")
                st_["mix_prev"] = mix_m
                gstate[g] = {"mix_m": mix_m}

            def s3(g):
                gs = gstate[g]
                sig_m = []
                ab_m = None
                for ec in range(NC):
                    wstep()
                    bnk = ec % 2
                    mms = [(warena[:, 0, dc, 128 * ec:128 * ec + 128],
                            xrxk[:, 0, dc, :]) for dc in range(NC)]
                    gm = mm_group(bnk, mms, waits=[gs["mix_m"], w_r],
                                  tag=f"{tag}A{g}_{ec}")
                    sig_m.append(evac(bnk, "scalar",
                                      lambda e, bnk=bnk, ec=ec: e.activation(
                                          out=r_fm[:, ec, :], in_=pb[bnk].ap(),
                                          func=AF.Sigmoid),
                                      waits=[gm] + ([st_["rc_prev"]] if st_["rc_prev"] else []),
                                      tag=f"{tag}Aev{g}_{ec}"))
                for ec in range(NC):
                    bnk = ec % 2
                    mms = [(warena[:, 1, dc, 128 * ec:128 * ec + 128],
                            xrxk[:, 1, dc, :]) for dc in range(NC)]
                    gm = mm_group(bnk, mms, waits=[w_k], tag=f"{tag}B{g}_{ec}")
                    ab_m = gm
                    evac(bnk, "scalar",
                         lambda e, bnk=bnk, ec=ec: e.activation(
                             out=relu_b[:, ec, :], in_=pb[bnk].ap(),
                             func=AF.Relu),
                         waits=[gm] + ([st_["c_prev"]] if st_["c_prev"] else []),
                         tag=f"{tag}Bev{g}_{ec}")
                st_["ab_prev"] = ab_m
                if g == NG - 1 and nxt:
                    arm(nxt[0], slot_free_gate())
                    arm(nxt[1], slot_free_gate())
                relu_m = p_.cur("sAC")
                k2_m = dv(lambda e: e.tensor_tensor(
                    out=relu_b[:, :, :], in0=relu_b[:, :, :],
                    in1=relu_b[:, :, :], op=ALU.mult),
                    waits=[relu_m], tag=f"{tag}k2{g}")
                gs["k2_m"] = k2_m
                gs["sig_m"] = sig_m

            def s4(g):
                gs = gstate[g]
                rc_m = []
                cm = None
                for ec in range(NC):
                    wstep()
                    bnk = ec % 2
                    mms = [(warena[:, 2, dc, 128 * ec:128 * ec + 128],
                            relu_b[:, dc, :]) for dc in range(NC)]
                    gm = mm_group(bnk, mms, waits=[gs["k2_m"], w_v],
                                  tag=f"{tag}C{g}_{ec}")
                    cm = gm
                    rc_m.append(evac(bnk, "vector",
                                     lambda e, bnk=bnk, ec=ec: e.tensor_tensor(
                                         out=rc_fm[:, ec, :], in0=pb[bnk].ap(),
                                         in1=r_fm[:, ec, :], op=ALU.mult),
                                     waits=[gm, gs["sig_m"][ec]],
                                     tag=f"{tag}Cev{g}_{ec}"))
                st_["c_prev"] = cm
                st_["rc_prev"] = rc_m[-1]
                if g == NG - 1 and nxt and nxt[2]:
                    arm(nxt[2], slot_free_gate())
                for t in range(4):
                    ra_ev = None
                    for half in range(2):
                        bnk = 2 + half
                        trs = []
                        for eci in range(4):
                            ec = 4 * half + eci
                            trs.append((pb[bnk].ap()[:, 128 * eci:128 * eci + 128],
                                        rc_fm[:, ec, 128 * t:128 * t + 128]))
                        tm = tr_group(bnk, trs, waits=[rc_m[-1]], dtype=F32,
                                      tag=f"{tag}rt{g}_{t}_{half}")
                        def f(e, bnk=bnk, g=g, t=t, half=half):
                            return e.tensor_tensor(
                                out=x_tm[:, 4 * g + t, 512 * half:512 * half + 512],
                                in0=pb[bnk].ap(),
                                in1=x_tm[:, 4 * g + t, 512 * half:512 * half + 512],
                                op=ALU.add)
                        ra_ev = evac(bnk, "vector", f, waits=[tm],
                                     tag=f"{tag}radd{g}_{t}_{half}")
                    if tile_hook is not None:
                        tile_hook(g, t, ra_ev)

            s12(0); s3(0)
            for g in range(1, NG):
                s12(g)
                s4(g - 1)
                s3(g)
            s4(NG - 1)

        if False and dbg and b == DBG_SEQ:
            dwv = p_.barrier_waits("sDV", "sAC", "sPE")
            def cast_dump(name, src_ap, waits):
                gp_dma(dbg_d[name], src_ap, waits=waits, tag=f"dbg{name}")
            cast_dump("stats", stats[:], dwv)
            cast_dump("xn", xn_tm[:], dwv)
            cast_dump("xpx", x_tm[:], dwv)
            cast_dump("K", K_fm, dwv)
            cast_dump("csum", colsum[:], dwv)
            cast_dump("zbuf", zbuf[:], dwv)
            cast_dump("E8", ebuf[:, 0, :], dwv)
            cast_dump("q2", warena[:, 2, :, 0:512], dwv)

        def emit_attn_b(tag, ldrs, nxt):
            pre = p_.barrier_waits("sDV", "sAC", "sPE", "sGP", "sGD")
            mrn3, _ = ln_stats(xpx=False, pre_waits=pre, tag=f"{tag}ln")
            w_qb = ldrs[0].fin()
            w_kb = ldrs[1].fin()
            w_vb = ldrs[2].fin()
            a3b = a3.ap().bitcast(BF16)
            a3f = a3.ap().bitcast(F32)
            sqkv = a3b[:, 0:12288].rearrange("p (u c s) -> p u c s",
                                             u=3, c=NC)
            Mf = a3f[:, 6144:7168].rearrange("p (c s) -> p c s", c=NC)
            Maug = a3b[:, 14336:15392].rearrange("p (c s) -> p c s", c=NC)
            zf = a3f[:, 7696:7704]
            ztmp = a3f[:, 7704:7712]
            dsc = a3f[:, 7712:7760]
            vd_b = xn_tm.ap()[:, 12, 0:128]
            tmpf = xn_tm.ap()[:, 12, :].bitcast(F32)[:, 128:256]
            adbuf_a = xn_tm.ap()[:, 13, :].bitcast(F32)
            adbuf_b = ebuf.ap()[:, 1, :].bitcast(F32)[:, 0:512]
            t1f = xn_tm.ap()[:, 14, :].bitcast(F32)
            t2f = xn_tm.ap()[:, 15, :].bitcast(F32)
            Ebuf = ebuf.ap()[:, 0, :]

            mi1 = dv(lambda e: e.memset(Mf, 0.0), waits=pre, tag=f"{tag}M0")
            mi2 = dv(lambda e: e.memset(Maug, 0.0), waits=pre, tag=f"{tag}Ma0")
            mi3 = dv(lambda e: e.memset(zf, 0.0), waits=pre, tag=f"{tag}z0")
            maug_ready = [mi1, mi2, mi3]

            tr_prev = None
            proj_war = []
            for G in range(NSEGS):
                buf = G % 2
                aw = ([tr_prev] if tr_prev else [])
                xn_mg = [ln_apply(4 * G + t, t, mrn3, aw, tag=f"{tag}g{G}")
                         for t in range(4)]
                fm_m, tr_last = fm_transpose_group(G, buf, [0, 1, 2, 3],
                                                   xn_mg[-2:], tag=f"{tag}g{G}")
                tr_prev = tr_last
                ev_m = []
                for u, wm, bcol in ((0, w_qb, 4), (1, w_kb, 5), (2, w_vb, 6)):
                    for ec in range(NC):
                        bnk = ec % 2
                        mms = [(warena[:, u, dc, 128 * ec:128 * ec + 128],
                                xn_fm[:, buf, dc, 4:516]) for dc in range(NC)]
                        gm = mm_group(bnk, mms, waits=fm_m + [wm] + proj_war,
                                      tag=f"{tag}p{G}u{u}e{ec}")
                        ev_m.append(evac(bnk, "scalar",
                                    lambda e, bnk=bnk, u=u, ec=ec, bcol=bcol:
                                    e.activation(out=sqkv[:, u, ec, :],
                                                 in_=pb[bnk].ap(),
                                                 func=AF.Identity,
                                                 bias=smalls[:, bcol, ec:ec + 1]),
                                    waits=[gm], tag=f"{tag}pe{G}{u}{ec}"))
                ev_last = [ev_m[-1]]
                if G == NSEGS - 1 and nxt:
                    for L_ in nxt:
                        if L_ is not None:
                            arm(L_, slot_free_gate())
                vt_m = []
                for t in range(4):
                    bnk = 2 + t % 2
                    trs = [(pbb[bnk][:, 128 * ec:128 * ec + 128],
                            sqkv[:, 2, ec, 128 * t:128 * t + 128])
                           for ec in range(NC)]
                    tm_ = tr_group(bnk, trs, waits=ev_last,
                                   tag=f"{tag}vt{G}_{t}")
                    def f(e, bnk=bnk, t=t):
                        return e.tensor_copy(xn_tm[:, 4 + t, :],
                                             pbb[bnk][:, 0:1024])
                    vt_m.append(evac(bnk, "vector", f, waits=[tm_],
                                     tag=f"{tag}vte{G}_{t}"))
                vt_last = vt_m[-1]
                num_read_m = []
                mdelta_first = {6: True, 7: True}
                mdelta_last_m = None
                def passA(h):
                    omgcol = dsm[:, 3, h:h + 1]
                    adbuf = adbuf_a if h % 2 == 0 else adbuf_b
                    z0m = dv(lambda e: e.memset(dsc[:, 36:37], 0.0),
                             waits=buf_war.get("zb1", []),
                             tag=f"{tag}z0_{G}_{h}")
                    e1 = z2m = None
                    adm_l, pte_l = [], []
                    for lc in range(4):
                        par = lc % 4
                        Eslot = Ebuf[:, 512 * par:512 * (par + 1)]
                        sb_ = (h + lc) % 2
                        ncols = 128 * (lc + 1)
                        gm = mm_group(sb_, [(sqkv[:, 0, h, 128 * lc:128 * lc + 128],
                                             sqkv[:, 1, h, 0:ncols],
                                             pb[sb_].ap()[:, 0:ncols])],
                                      waits=ev_last + buf_war.get(("E", par), []),
                                      tag=f"{tag}s{G}_{h}_{lc}")
                        if lc > 0:
                            e1 = ac(lambda e, sb_=sb_, lc=lc, Eslot=Eslot:
                                    e.activation(out=Eslot[:, 0:128 * lc],
                                                 in_=pb[sb_].ap()[:, 0:128 * lc],
                                                 func=AF.Exp,
                                                 scale=1.0 / np.sqrt(DK),
                                                 accum_out=dsc[:, 36 + lc:37 + lc]),
                                    waits=[gm] + buf_war.get("zb1", []),
                                    tag=f"{tag}e1_{G}_{h}_{lc}")
                        e2 = ac(lambda e, sb_=sb_, lc=lc, Eslot=Eslot:
                                e.activation(out=Eslot[:, 128 * lc:128 * lc + 128],
                                             in_=pb[sb_].ap()[:, 128 * lc:128 * lc + 128],
                                             func=AF.Exp, scale=1.0 / np.sqrt(DK)),
                                waits=([e1] if e1 else [gm]),
                                tag=f"{tag}e2_{G}_{h}_{lc}")
                        bank_war[sb_] = e2
                        mk = dv(lambda e, lc=lc, Eslot=Eslot: e.tensor_tensor(
                            out=Eslot[:, 128 * lc:128 * lc + 128],
                            in0=Eslot[:, 128 * lc:128 * lc + 128],
                            in1=tril01[:], op=ALU.mult),
                            waits=[e2, init_gp], tag=f"{tag}mk{G}_{h}_{lc}")
                        z2m = dv(lambda e, lc=lc, Eslot=Eslot:
                                 e.tensor_reduce(dsc[:, 40 + lc:41 + lc],
                                                 Eslot[:, 128 * lc:128 * lc + 128],
                                                 axis=AX.X, op=ALU.add),
                                 waits=[mk], tag=f"{tag}zb2_{G}_{h}_{lc}")
                        trs = [(pbb[2][:, 128 * j:128 * j + 128],
                                Eslot[:, 128 * j:128 * j + 128])
                               for j in range(lc + 1)]
                        ptm = tr_group(2, trs, waits=[mk], tag=f"{tag}pt{G}_{h}_{lc}")
                        pte = evac(2, "scalar",
                                   lambda e, lc=lc: e.copy(
                                       ptbuf[:, 0:lc + 1, :],
                                       pbb[2][:, 0:128 * (lc + 1)]),
                                   waits=[ptm] + buf_war.get("ptbuf", []),
                                   tag=f"{tag}pte{G}_{h}_{lc}")
                        pte_l.append(pte)
                        buf_war[("E", par)] = [ptm]
                        for j in range(lc + 1):
                            st = (lc == 0 and j == 0)
                            sp = (lc == 3 and j == 3)
                            def fad(e, j=j, lc=lc, h=h, st=st, sp=sp):
                                return e.matmul(
                                    pb[4].ap()[:, 128 * lc:128 * lc + 128],
                                    ptbuf[:, j, :],
                                    xn_tm[:, 4 + j, 128 * h:128 * h + 128],
                                    start=st, stop=sp)
                            w = [pte] if j == 0 else []
                            if st:
                                w = w + [vt_last] + buf_war.get("bank4", [])
                            adm = p_.add("tensor", fad, waits=w,
                                         inc=("sPE" if j == lc else None),
                                         tag=f"{tag}ad{G}_{h}_{lc}_{j}")
                        adm_l.append(adm)
                        buf_war["ptbuf"] = [adm]
                    zsum = dv(lambda e: e.tensor_tensor(
                        out=dsc[:, 36:40], in0=dsc[:, 36:40],
                        in1=dsc[:, 40:44], op=ALU.add),
                        waits=[z2m, e1, z0m], tag=f"{tag}zs_{G}_{h}")
                    rzb = dv(lambda e: e.reciprocal(dsc[:, 40:44],
                                                    dsc[:, 36:40]),
                             waits=[zsum], tag=f"{tag}rz_{G}_{h}")
                    buf_war["zb1"] = [rzb]
                    sdb = dv(lambda e, omgcol=omgcol: e.tensor_scalar(
                        out=dsc[:, 44:48], in0=dsc[:, 40:44],
                        scalar1=omgcol, scalar2=None, op0=ALU.mult),
                        waits=[rzb] + buf_war.get("zb_sd", []),
                        tag=f"{tag}sd_{G}_{h}")
                    ads = None
                    for lc in range(4):
                        ads = evac(4, "scalar",
                                   lambda e, lc=lc: e.activation(
                                       out=adbuf[:, 128 * lc:128 * lc + 128],
                                       in_=pb[4].ap()[:, 128 * lc:128 * lc + 128],
                                       func=AF.Identity,
                                       scale=dsc[:, 44 + lc:45 + lc]),
                                   waits=[adm_l[3], sdb] +
                                         buf_war.get(("ad", h % 2, lc), []),
                                   tag=f"{tag}ads{G}_{h}_{lc}")
                    buf_war["zb_sd"] = [ads]
                    buf_war["bank4"] = [ads]
                    sc_pe[h] = p_.cur("sPE")

                def passB(h):
                    nonlocal mdelta_last_m
                    gcol = dsm[:, 2, h:h + 1]
                    adbuf = adbuf_a if h % 2 == 0 else adbuf_b
                    sc_m = sc_pe[h]
                    t12 = xn_tm.ap()[:, 14:16, :].bitcast(F32)
                    squ = sqkv[:, 0:2, h, :]
                    a1 = ac(lambda e, h=h: e.activation(
                        out=t12, in_=squ, func=AF.Relu, scale=-1.0),
                        waits=[("sPE", sc_m[1])] + buf_war.get("t1f", []),
                        tag=f"{tag}tr1_{G}_{h}")
                    a2 = ac(lambda e: e.activation(out=t12, in_=t12,
                                                   func=AF.Exp, scale=-1.0),
                            waits=[a1], tag=f"{tag}tr2_{G}_{h}")
                    el = dv(lambda e, h=h: e.scalar_tensor_tensor(
                        out=squ, in0=squ, scalar=0.0, in1=t12,
                        op0=ALU.max, op1=ALU.add),
                        waits=[a2], tag=f"{tag}tr4_{G}_{h}")
                    buf_war["t1f"] = [el]
                    el_m = [el, el]
                    sktr = tr_group(3, [(pbb[3][:, 128 * t:128 * t + 128],
                                         sqkv[:, 1, h, 128 * t:128 * t + 128])
                                        for t in range(4)],
                                    waits=[el_m[1]], tag=f"{tag}skt{G}_{h}")
                    skte = evac(3, "vector",
                                lambda e, h=h: e.tensor_copy(
                                    xn_tm.ap()[:, 8:12, 128 * h:128 * h + 128],
                                    pbb[3][:, 0:512].rearrange(
                                        "p (t c) -> p t c", t=4)),
                                waits=[sktr], tag=f"{tag}skte{G}_{h}")
                    zt_hold[0] = dv(lambda e, h=h: e.tensor_reduce(
                        ztmp[:, h:h + 1], sqkv[:, 1, h, :], axis=AX.X,
                        op=ALU.add), waits=[el_m[1]], tag=f"{tag}zt{G}_{h}")
                    denm = mm_group(5, [(sqkv[:, u, h, 128 * lc:128 * lc + 128],
                                         Maug[:, h, 128:129],
                                         pb[5].ap()[:, 2 * lc + u:2 * lc + u + 1])
                                        for lc in range(4) for u in range(2)],
                                    waits=maug_ready + [el_m[1]],
                                    tag=f"{tag}den{G}_{h}")
                    num_read_m.append(denm)
                    dev = evac(5, "scalar", lambda e: e.activation(
                        out=dsc[:, 16:24], in_=pb[5].ap()[:, 0:8],
                        func=AF.Identity, bias=bepscol[:, 0:1]),
                        waits=[denm] + buf_war.get("dsc16", []),
                        tag=f"{tag}dev{G}_{h}")
                    rcp = dsc[:, 24:32].rearrange("p (l u) -> p l u", u=2)
                    rbm = dv(lambda e: e.reciprocal(dsc[:, 24:32],
                                                    dsc[:, 16:24]),
                             waits=[dev], tag=f"{tag}rb{G}_{h}")
                    buf_war["dsc16"] = [rbm]
                    gqb = dv(lambda e, gcol=gcol: e.tensor_scalar(
                        out=dsc[:, 32:36], in0=rcp[:, :, 0:1],
                        scalar1=gcol, scalar2=None, op0=ALU.mult),
                        waits=[rbm], tag=f"{tag}gq{G}_{h}")
                    c1_last = None
                    for lc in range(4):
                        nq = [(sqkv[:, 0, h, 128 * lc:128 * lc + 128],
                               Maug[:, h, 0:128], pb[3].ap()[:, 0:128]),
                              (sqkv[:, 1, h, 128 * lc:128 * lc + 128],
                               Maug[:, h, 0:128], pb[3].ap()[:, 128:256])]
                        nm = mm_group(3, nq, waits=maug_ready + [el_m[1], skte],
                                      tag=f"{tag}nm{G}_{h}_{lc}")
                        num_read_m.append(nm)
                        vdp = xn_tm.ap()[:, 12, 128 * (lc % 2):
                                         128 * (lc % 2) + 128]
                        vdm = dv(lambda e, h=h, lc=lc, vdp=vdp:
                                 e.scalar_tensor_tensor(
                                     out=vdp, in0=pb[3].ap()[:, 128:256],
                                     scalar=rcp[:, lc, 1:2],
                                     in1=xn_tm[:, 4 + lc, 128 * h:128 * h + 128],
                                     op0=ALU.mult, op1=ALU.subtract),
                                 waits=[nm, gqb] + buf_war.get(("vd", lc % 2), []),
                                 tag=f"{tag}vd_{G}_{h}_{lc}")
                        mb = 6 + h // 4
                        def fmd(e, h=h, lc=lc, mb=mb, vdp=vdp,
                                st=mdelta_first[mb]):
                            return e.matmul(
                                pb[mb].ap()[:, 128 * (h % 4):128 * (h % 4) + 128],
                                xn_tm[:, 8 + lc, 128 * h:128 * h + 128],
                                vdp, start=st,
                                stop=(h % 4 == 3 and lc == 3))
                        mw = [vdm, skte]
                        if mdelta_first[mb] and mb in bank_war and bank_war[mb]:
                            mw.append(bank_war[mb])
                        mdelta_last_m = p_.add("tensor", fmd, waits=mw,
                                               inc="sPE",
                                               tag=f"{tag}md{G}_{h}_{lc}")
                        mdelta_first[mb] = False
                        buf_war[("vd", lc % 2)] = [mdelta_last_m]
                        c1_last = dv(lambda e, h=h, lc=lc: e.scalar_tensor_tensor(
                            out=adbuf[:, 128 * lc:128 * lc + 128],
                            in0=pb[3].ap()[:, 0:128],
                            scalar=dsc[:, 32 + lc:33 + lc],
                            in1=adbuf[:, 128 * lc:128 * lc + 128],
                            op0=ALU.mult, op1=ALU.add),
                            waits=[vdm], tag=f"{tag}c1_{G}_{h}_{lc}")
                        bank_war[3] = c1_last
                    c2 = dv(lambda e, G=G, h=h, adbuf=adbuf: e.tensor_tensor(
                        out=x_tm[:, 4 * G:4 * G + 4, 128 * h:128 * h + 128],
                        in0=adbuf.rearrange("p (l c) -> p l c", l=4),
                        in1=x_tm[:, 4 * G:4 * G + 4, 128 * h:128 * h + 128],
                        op=ALU.add), waits=[c1_last],
                        tag=f"{tag}c2_{G}_{h}")
                    for lc in range(4):
                        buf_war[("ad", h % 2, lc)] = [c2]
                sc_pe = {}
                zt_hold = [None]
                passA(0)
                for h in range(H):
                    wstep()
                    if h + 1 < H:
                        passA(h + 1)
                    passB(h)
                if n_sub >= 4:
                    for lc in range(4):
                        stats_tile(4 * G + lc, buf_war[("ad", 1, lc)],
                                   f"{tag}pf_")
                zu = dv(lambda e: e.tensor_tensor(out=zf, in0=zf, in1=ztmp,
                                                  op=ALU.add),
                        waits=[zt_hold[0]] + num_read_m[-1:], tag=f"{tag}zu{G}")
                zc = dv(lambda e: e.tensor_copy(
                    Maug[:, :, 128:129],
                    zf.rearrange("p (c o) -> p c o", o=1)),
                        waits=[zu], tag=f"{tag}zc{G}")
                m1_ = dv(lambda e: e.tensor_tensor(out=Mf[:, 0:4, :],
                                                   in0=Mf[:, 0:4, :],
                                                   in1=pb[6].ap(),
                                                   op=ALU.subtract),
                         waits=[mdelta_last_m], tag=f"{tag}mu{G}a")
                m2_ = dv(lambda e: e.tensor_tensor(out=Mf[:, 4:8, :],
                                                   in0=Mf[:, 4:8, :],
                                                   in1=pb[7].ap(),
                                                   op=ALU.subtract),
                         waits=[m1_], tag=f"{tag}mu{G}b")
                bank_war[6] = m2_
                bank_war[7] = m2_
                mc = dv(lambda e: e.tensor_copy(Maug[:, :, 0:128], Mf),
                        waits=[m2_] + num_read_m[-1:], tag=f"{tag}mc{G}")
                maug_ready = [mc, zc]
                proj_war = [p_.cur("sPE"), p_.cur("sDV")]

        ab_ldrs = None
        if n_sub >= 3:
            ab_ldrs = (LoadW(wqb_d, 0, 0, f"b{b}ab_ldwq"),
                       LoadW(wkb_d, 1, 1, f"b{b}ab_ldwk"),
                       LoadW(wvb_d, 2, 0, f"b{b}ab_ldwv"))
        f2_ldrs = None
        if n_sub >= 4:
            f2_ldrs = (LoadW(wrT_d, 0, 0, f"b{b}f2_ldwr"),
                       LoadW(wkfT_d, 1, 1, f"b{b}f2_ldwk"),
                       LoadW(wvfT_d, 2, 0, f"b{b}f2_ldwv"))
        nseq = None
        if n_sub >= 4 and b + 1 < BL:
            carry_qa = LoadW(wqaT_d, 0, 0, f"b{b + 1}ldwqa")
            carry_ka = LoadW(wkaT_d, 1, 1, f"b{b + 1}ldwka")
            nseq = (carry_qa, carry_ka, None)
        def f2_hook(g, t, m, b=b):
            i = 4 * g + t
            otm = out_d[b].rearrange("(t p) d -> p t d", p=P)
            sy_dma(otm[:, i:i + 1, :], x_tm[:, i:i + 1, :], waits=[m],
                   store=True, tag=f"st{b}_{i}")
            if b + 1 < BL:
                itm = x_d[b + 1].rearrange("(t p) d -> p t d", p=P)
                ld = sy_dma(x_tm[:, i:i + 1, :], itm[:, i:i + 1, :],
                            tag=f"ldx{b + 1}_{i}")
                stats_tile(i, [ld], f"b{b + 1}ln1_")

        f1_hook = None
        if n_sub >= 3:
            def f1_hook(g, t, m):
                stats_tile(4 * g + t, [m], f"b{b}ab_")
        if n_sub >= 2:
            emit_ffn(f"b{b}f1_", f1_ldrs, ab_ldrs, tile_hook=f1_hook,
                     dbg_g0=(dbg and b == DBG_SEQ))
        if n_sub >= 3:
            emit_attn_b(f"b{b}ab_", ab_ldrs, f2_ldrs)
        if n_sub >= 4:
            emit_ffn(f"b{b}f2_", f2_ldrs, nseq, tile_hook=f2_hook)
            NSTORE += NT
        assert not armed or (n_sub >= 4 and b + 1 < BL), \
            f"unarmed/armed leftovers at b={b}: {[L.tag for L in armed]}"

        if n_sub < 4:
            st_w = p_.barrier_waits("sDV", "sAC", "sPE")
            sy_dma(out_d[b].rearrange("(t p) d -> p t d", p=P), x_tm[:],
                   waits=st_w, store=True, tag=f"st{b}")
            NSTORE += 1

    p_.add("sync", lambda e: e.nop(),
           waits=[("sST", 16 * NSTORE), p_.cur("sGD"), p_.cur("sXD")],
           tag="final")

    p_.simulate()

    with ctx:
        blk = ctx.enter_context(nc.Block())

        def make_body(eng_name):
            def body(eng):
                for (fn, w, incs, tag) in p_.ops[eng_name]:
                    for (s, v) in w:
                        eng.wait_ge(sems[s], v)
                    inst = fn(eng)
                    for (s, v, a) in incs:
                        if a:
                            inst = inst.then_inc(sems[s], v)
            return body

        blk.sync(make_body("sync"))
        blk.gpsimd(make_body("gpsimd"))
        blk.tensor(make_body("tensor"))
        blk.vector(make_body("vector"))
        blk.scalar(make_body("scalar"))

    return nc


def _host_layout(inputs):
    g = lambda k: np.asarray(inputs[k])
    smalls = np.zeros((P, 8, NC), dtype=np.float32)
    def fm(v):
        return np.ascontiguousarray(v.reshape(NC, P).T)
    smalls[:, 0, :] = fm(g("L0_attn_a_bq"))
    smalls[:, 1, :] = fm(g("L0_attn_a_bk"))
    smalls[:, 2, :] = fm(g("L0_ffn_mu_r"))
    smalls[:, 3, :] = fm(g("L0_ffn_mu_k"))
    smalls[:, 4, :] = fm(g("L0_attn_b_bq"))
    smalls[:, 5, :] = fm(g("L0_attn_b_bk"))
    smalls[:, 6, :] = fm(g("L0_attn_b_bv"))
    smalls[:, 7, :] = np.broadcast_to(g("L0_attn_b_beta")[None, :], (P, 8))
    T = lambda a: np.ascontiguousarray(a.T)
    return {
        "wqaT": T(g("L0_attn_a_Wq")), "wkaT": T(g("L0_attn_a_Wk")),
        "wrT": T(g("L0_ffn_W_r")), "wkfT": T(g("L0_ffn_W_k")),
        "wvfT": T(g("L0_ffn_W_v")),
        "wqb": np.ascontiguousarray(g("L0_attn_b_Wq")),
        "wkb": np.ascontiguousarray(g("L0_attn_b_Wk")),
        "wvb": np.ascontiguousarray(g("L0_attn_b_Wv")),
        "smalls": smalls,
    }


N_SUB = 4

def kernel(**inputs):
    x = np.ascontiguousarray(np.asarray(inputs["L0_attn_a_x"], dtype=np.float32))
    shared = _host_layout(inputs)
    nc = build_nc(n_sub=N_SUB)
    in_maps = []
    for i in range(NCORES):
        m = dict(shared)
        m["x"] = np.ascontiguousarray(x[BL * i:BL * (i + 1)])
        in_maps.append(m)
    res = run_bass_kernel_spmd(nc, in_maps, core_ids=list(range(NCORES)))
    return np.concatenate([r["out"] for r in res.results], axis=0)


if __name__ == "__main__":
    nc = build_nc()
    print("built OK")


# version 11
# speedup vs baseline: 1.1714x; 1.1714x over previous
import numpy as np
from contextlib import ExitStack

import concourse.bass as bass
import concourse.mybir as mybir
from concourse.bass_utils import run_bass_kernel_spmd

F32 = mybir.dt.float32
BF16 = mybir.dt.bfloat16
AF = mybir.ActivationFunctionType
ALU = mybir.AluOpType
AX = mybir.AxisListType

P = 128
B, S, D = 32, 2048, 1024
NCORES = 8
BL = B // NCORES
NT = S // P
NC = D // P
NG = S // 512
H, DK, DV, SEG, NSEGS = 8, 128, 128, 512, 4
LN_EPS = 1e-5
B_EPS = 1e-6

ENGS = ("sync", "gpsimd", "tensor", "vector", "scalar")


class Plan:

    def __init__(self):
        self.ops = {e: [] for e in ENGS}
        self.cnt = {s: 0 for s in ("sPE", "sDV", "sAC", "sGP", "sXD", "sGD", "sST")}
        self.seen = {e: {} for e in ENGS}
        self.eng_sem = {"tensor": "sPE", "vector": "sDV", "scalar": "sAC",
                        "gpsimd": "sGP"}

    def add(self, eng, fn, waits=(), inc=None, n=1, tag="", attach=True):
        w = []
        for wv in waits:
            if wv is None:
                continue
            (s, v) = wv
            if v <= 0:
                continue
            if self.seen[eng].get(s, 0) >= v:
                continue
            self.seen[eng][s] = v
            w.append((s, v))
        incs = []
        if inc is not None:
            self.cnt[inc] += n
            incs.append((inc, n, attach))
        self.ops[eng].append((fn, w, incs, tag))
        return (inc, self.cnt[inc]) if inc is not None else None

    def cur(self, s):
        return (s, self.cnt[s])

    def barrier_waits(self, *sems):
        return [(s, self.cnt[s]) for s in sems]

    def simulate(self):
        pos = {e: 0 for e in ENGS}
        val = {s: 0 for s in self.cnt}
        progress = True
        while progress:
            progress = False
            for e in ENGS:
                while pos[e] < len(self.ops[e]):
                    fn, w, incs, tag = self.ops[e][pos[e]]
                    if any(val[s] < v for (s, v) in w):
                        break
                    for (s, v, _a) in incs:
                        val[s] += v
                    pos[e] += 1
                    progress = True
        stuck = {e: pos[e] for e in ENGS if pos[e] < len(self.ops[e])}
        if stuck:
            msg = []
            for e, p_ in stuck.items():
                fn, w, incs, tag = self.ops[e][p_]
                msg.append(f"{e}@{p_}/{len(self.ops[e])} tag={tag} waits={w} "
                           f"sems={[(s, val[s]) for s, _ in w]}")
            raise RuntimeError("PLAN DEADLOCK:\n" + "\n".join(msg))


FFN_LEVEL = 3
DBG_SEQ = 0

def build_nc(n_sub=4, dbg=False, ffn_level=None):
    if ffn_level is None:
        ffn_level = FFN_LEVEL
    nc = bass.Bass("TRN2", target_bir_lowering=False,
                   dynamic_dma_scratch_size=8192)

    x_d = nc.dram_tensor("x", [BL, S, D], F32, kind="ExternalInput").ap()
    out_d = nc.dram_tensor("out", [BL, S, D], F32, kind="ExternalOutput").ap()
    wqaT_d = nc.dram_tensor("wqaT", [D, D], F32, kind="ExternalInput").ap()
    wkaT_d = nc.dram_tensor("wkaT", [D, D], F32, kind="ExternalInput").ap()
    wrT_d = nc.dram_tensor("wrT", [D, D], F32, kind="ExternalInput").ap()
    wkfT_d = nc.dram_tensor("wkfT", [D, D], F32, kind="ExternalInput").ap()
    wvfT_d = nc.dram_tensor("wvfT", [D, D], F32, kind="ExternalInput").ap()
    wqb_d = nc.dram_tensor("wqb", [D, D], F32, kind="ExternalInput").ap()
    wkb_d = nc.dram_tensor("wkb", [D, D], F32, kind="ExternalInput").ap()
    wvb_d = nc.dram_tensor("wvb", [D, D], F32, kind="ExternalInput").ap()
    smalls_d = nc.dram_tensor("smalls", [P, 8, NC], F32, kind="ExternalInput").ap()
    if dbg:
        dbg_d = {k: nc.dram_tensor(f"dbg_{k}", shp, F32, kind="ExternalOutput").ap()
                 for k, shp in [("stats", [P, NT, 8]), ("xn", [P, NT, D]),
                                ("xpx", [P, NT, D]), ("K", [P, NC, S]),
                                ("csum", [1, D]), ("zbuf", [P, NT, 2]),
                                ("E8", [P, S]), ("q2", [P, NC, 512]),
                                ("xrxk", [P, 2, NC, 512]), ("r", [P, NC, 512]),
                                ("k2", [P, NC, 512]), ("rc", [P, NC, 512]),
                                ("xnfm", [P, 2, NC, 516])]}

    ctx = ExitStack()
    sb = lambda *a: ctx.enter_context(nc.sbuf_tensor(*a))
    ps = lambda *a: ctx.enter_context(nc.psum_tensor(*a))

    x_tm = sb("x_tm", [P, NT, D], F32)
    xn_tm = sb("xn_tm", [P, NT, D], BF16)
    xn_fm = sb("xn_fm", [P, 2, NC, 516], BF16)
    a3 = sb("a3", [P, 32768], mybir.dt.uint8)
    warena = sb("warena", [P, 3, NC, D], BF16)
    ebuf = sb("ebuf", [P, 2, S], BF16)
    ptbuf = sb("ptbuf", [P, 17, P], BF16)
    stats = sb("stats", [P, NT, 8], F32)
    smalls = sb("smalls_s", [P, 8, NC], F32)
    dsm = sb("dsm", [P, 8, NC], F32)
    identb = sb("identb", [P, P], BF16)
    tril01 = sb("tril01", [P, P], BF16)
    identf = sb("identf", [P, P], F32)
    onescol = sb("onescol", [P, 1], BF16)
    epscol = sb("epscol", [P, 1], F32)
    bepscol = sb("bepscol", [P, 1], F32)
    zbuf = sb("zbuf", [P, NT, 2], F32)
    zpart = sb("zpart", [P, 8], F32)
    nzcol = sb("nzcol", [P, 2], BF16)
    zrow = sb("zrow", [1, 2, P], BF16)
    colsum = sb("colsum", [1, D], BF16)
    wst0 = sb("wst0", [P, 1024], F32)
    junkb = sb("junkb", [P, 1024], BF16)

    K_fm = a3.ap().bitcast(BF16).rearrange("p (c s) -> p c s", c=NC)
    junk = ebuf.ap().bitcast(F32)[:, 0, :]

    pb = [ps(f"pb{i}", [P, 512], F32) for i in range(8)]
    pbb = [pb[i].ap().bitcast(BF16) for i in range(8)]

    sems = {}
    for s in ("sPE", "sDV", "sAC", "sGP", "sXD", "sGD", "sST"):
        sems[s] = ctx.enter_context(nc.semaphore(s))

    p_ = Plan()
    bank_war = {}
    buf_war = {}

    def mm_group(bank, mms, waits=(), tag=""):
        w = list(waits)
        if bank in bank_war and bank_war[bank]:
            w.append(bank_war[bank])
        last = None
        n = len(mms)
        for i, m in enumerate(mms):
            lhsT, rhs = m[0], m[1]
            out = m[2] if len(m) > 2 else pb[bank].ap()
            st, sp = (i == 0), (i == n - 1)
            def f(e, out=out, lhsT=lhsT, rhs=rhs, st=st, sp=sp):
                return e.matmul(out, lhsT, rhs, start=st, stop=sp)
            last = p_.add("tensor", f, waits=(w if i == 0 else ()),
                          inc=("sPE" if sp else None), tag=tag)
        return last

    def tr_group(bank, trs, waits=(), dtype=BF16, tag=""):
        w = list(waits)
        if bank in bank_war and bank_war[bank]:
            w.append(bank_war[bank])
        ident = identb[:] if dtype == BF16 else identf[:]
        last = None
        for i, (out, in_) in enumerate(trs):
            def f(e, out=out, in_=in_, ident=ident):
                return e.transpose(out, in_, ident)
            last = p_.add("tensor", f, waits=(w if i == 0 else ()),
                          inc=("sPE" if i == len(trs) - 1 else None), tag=tag)
        return last

    def evac(bank, eng, fn, waits=(), tag=""):
        sem = p_.eng_sem[eng]
        mile = p_.add(eng, fn, waits=waits, inc=sem, tag=tag)
        bank_war[bank] = mile
        return mile

    def dv(fn, waits=(), tag="", inc=True):
        return p_.add("vector", fn, waits=waits, inc=("sDV" if inc else None), tag=tag)

    def ac(fn, waits=(), tag="", inc=True):
        return p_.add("scalar", fn, waits=waits, inc=("sAC" if inc else None), tag=tag)

    def gp(fn, waits=(), tag="", inc=True):
        return p_.add("gpsimd", fn, waits=waits, inc=("sGP" if inc else None), tag=tag)

    def sy_dma(out, in_, waits=(), store=False, tag=""):
        waits = list(waits) + [p_.cur("sXD"), p_.cur("sST")]
        def f(e, out=out, in_=in_):
            return e.dma_start(out=out, in_=in_)
        return p_.add("sync", f, waits=waits, inc=("sST" if store else "sXD"),
                      n=16, tag=tag)

    def hard_barrier(tag=""):
        arr = {}
        for e in ("tensor", "vector", "scalar", "gpsimd"):
            s = p_.eng_sem[e]
            def mk(s=s):
                return lambda en: en.sem_inc(sems[s], 1)
            arr[e] = p_.add(e, mk(), inc=s, tag=f"{tag}barA", attach=False)
        dmaw = [p_.cur("sGD"), p_.cur("sXD"), p_.cur("sST")]
        allw = list(arr.values()) + dmaw
        for e in ("tensor", "vector", "scalar", "gpsimd", "sync"):
            p_.add(e, (lambda en: en.wait_ge(sems["sGP"], 1)), waits=allw,
                   tag=f"{tag}barW")

    def sc_dma(out, in_, waits=(), tag=""):
        waits = list(waits) + [p_.cur("sGD")]
        def f(e, out=out, in_=in_):
            return e.dma_start(out=out, in_=in_)
        return p_.add("scalar", f, waits=waits, inc="sGD", n=16, tag=tag)

    def gp_dma(out, in_, waits=(), tag=""):
        waits = list(waits) + [p_.cur("sGD")]
        def f(e, out=out, in_=in_):
            return e.dma_start(out=out, in_=in_)
        return p_.add("gpsimd", f, waits=waits, inc="sGD", n=16, tag=tag)

    sy_dma(smalls[:], smalls_d, tag="ld_smalls")
    sm_ready = p_.cur("sXD")

    m1 = gp(lambda e: e.memset(identb[:], 0.0), tag="identb0")
    gp(lambda e: e.affine_select(out=identb[:], in_=identb[:],
                                 compare_op=ALU.not_equal, fill=1.0, base=0,
                                 pattern=[[-1, P]], channel_multiplier=1),
       waits=[m1], tag="identb")
    mt = gp(lambda e: e.memset(tril01[:], 1.0), tag="tril0")
    gp(lambda e: e.affine_select(out=tril01[:], in_=tril01[:],
                                 compare_op=ALU.is_ge, fill=0.0, base=0,
                                 pattern=[[-1, P]], channel_multiplier=1),
       waits=[mt], tag="tril")
    m2 = gp(lambda e: e.memset(identf[:], 0.0), tag="identf0")
    gp(lambda e: e.affine_select(out=identf[:], in_=identf[:],
                                 compare_op=ALU.not_equal, fill=1.0, base=0,
                                 pattern=[[-1, P]], channel_multiplier=1),
       waits=[m2], tag="identf")
    gp(lambda e: e.memset(onescol[:], 1.0), tag="ones")
    gp(lambda e: e.memset(epscol[:], LN_EPS), tag="eps")
    gp(lambda e: e.memset(bepscol[:], B_EPS), tag="beps")
    gp(lambda e: e.memset(xn_fm[:], 0.0), tag="fmz")
    init_gp = p_.cur("sGP")

    dv(lambda e: e.tensor_scalar(out=dsm[:, 0, :], in0=smalls[:, 2, :],
                                 scalar1=-1.0, scalar2=1.0, op0=ALU.mult,
                                 op1=ALU.add), waits=[sm_ready], tag="onemur")
    dv(lambda e: e.tensor_scalar(out=dsm[:, 1, :], in0=smalls[:, 3, :],
                                 scalar1=-1.0, scalar2=1.0, op0=ALU.mult,
                                 op1=ALU.add), tag="onemuk")
    ac(lambda e: e.activation(out=dsm[:, 2, :], in_=smalls[:, 7, :],
                              func=AF.Sigmoid), waits=[sm_ready], tag="gate")
    gate_done = p_.cur("sAC")
    dv(lambda e: e.tensor_scalar(out=dsm[:, 3, :], in0=dsm[:, 2, :],
                                 scalar1=-1.0, scalar2=1.0, op0=ALU.mult,
                                 op1=ALU.add), waits=[gate_done], tag="onegate")
    dv(lambda e: e.tensor_scalar(out=dsm[:, 6, :], in0=smalls[:, 4, :],
                                 scalar1=-1.0, scalar2=None, op0=ALU.mult),
       tag="negbq")
    dv(lambda e: e.tensor_scalar(out=dsm[:, 7, :], in0=smalls[:, 5, :],
                                 scalar1=-1.0, scalar2=None, op0=ALU.mult),
       tag="negbk")
    init_dv = p_.cur("sDV")
    init_ac = p_.cur("sAC")

    armed = []

    class LoadW:
        def __init__(self, dram_ap, slot, q=0, tag=""):
            self.dram_ap, self.slot, self.q, self.tag = dram_ap, slot, q, tag
            self.c = NC - 1
            self.gate = []
            self.last = None
            self.mile = None

        def step(self):
            if self.c < 0:
                return
            c = self.c
            self.c -= 1
            dma = sy_dma if self.q == 0 else sc_dma
            wf = warena.ap()[:, self.slot, :, :].rearrange(
                "p c e -> p (c e)").bitcast(F32)
            stage = wf[:, 0:1024] if c >= 2 else wst0.ap()
            w = list(self.gate) + ([self.last] if self.last else [])
            if c < 2:
                w = w + buf_war.get("wst", [])
            ld = dma(stage, self.dram_ap[128 * c:128 * (c + 1), :], waits=w,
                     tag=f"{self.tag}c{c}")
            self.last = ac(lambda e, slot=self.slot, c=c, stage=stage:
                           e.activation(out=warena[:, slot, c, :], in_=stage,
                                        func=AF.Identity),
                           waits=[ld], tag=f"{self.tag}x{c}")
            if c < 2:
                buf_war["wst"] = [self.last]
            if self.c < 0:
                self.mile = self.last

        def fin(self):
            while self.c >= 0:
                self.step()
            if self in armed:
                armed.remove(self)
            return self.mile

    def arm(ldr, gate):
        ldr.gate = list(gate)
        armed.append(ldr)

    def wstep():
        for ldr in list(armed):
            ldr.step()
            if ldr.c < 0:
                armed.remove(ldr)

    slot_free_gate = lambda: p_.barrier_waits("sPE", "sAC", "sDV")

    NSTORE = 0
    carry_qa = carry_ka = None
    stats_done = {}
    stats_guard = {}

    for b in range(BL):
        if n_sub >= 4 and b > 0:
            x_ready = []
        else:
            xw = [p_.cur("sST")]
            x_mile = sy_dma(x_tm[:], x_d[b].rearrange("(t p) d -> p t d", p=P),
                            waits=xw, tag=f"ldx{b}")
            x_ready = [x_mile]

        def stats_tile(i, waits, tag):
            w = list(waits) + stats_guard.get(i // 4, [])
            s1m = dv(lambda e, i=i: e.tensor_reduce(
                stats[:, i, 0:1], x_tm[:, i, :], axis=AX.X, op=ALU.add),
                waits=w, tag=f"{tag}S1_{i}")
            s2m = ac(lambda e, i=i: e.activation(
                out=junkb.ap(), in_=x_tm[:, i, :], func=AF.Square,
                accum_out=stats[:, i, 1:2]),
                waits=w + buf_war.get("junkb", []), tag=f"{tag}S2_{i}")
            buf_war["junkb"] = [s2m]
            stats_done[i] = (s1m, s2m)

        def ln_stats(xpx, pre_waits, tag):
            for i in range(NT):
                if i not in stats_done:
                    stats_tile(i, pre_waits + [sm_ready], tag)
            mrns, oprs = [], []
            for g4 in range(4):
                sl = slice(4 * g4, 4 * g4 + 4)
                sw = [m for i in range(4 * g4, 4 * g4 + 4)
                      for m in stats_done[i]]
                m1 = dv(lambda e, sl=sl: e.tensor_scalar(
                    out=stats[:, sl, 2:3], in0=stats[:, sl, 0:1],
                    scalar1=1.0 / D, scalar2=None, op0=ALU.mult),
                    waits=sw, tag=f"{tag}m{g4}")
                dv(lambda e, sl=sl: e.tensor_scalar(
                    out=stats[:, sl, 4:5], in0=stats[:, sl, 1:2],
                    scalar1=1.0 / D, scalar2=None, op0=ALU.mult),
                    tag=f"{tag}s2n{g4}")
                s2 = dv(lambda e, sl=sl: e.tensor_tensor(
                    out=stats[:, sl, 3:4], in0=stats[:, sl, 2:3],
                    in1=stats[:, sl, 2:3], op=ALU.mult),
                    waits=[m1], tag=f"{tag}m2{g4}")
                var = dv(lambda e, sl=sl: e.tensor_tensor(
                    out=stats[:, sl, 3:4], in0=stats[:, sl, 4:5],
                    in1=stats[:, sl, 3:4], op=ALU.subtract),
                    waits=[s2], tag=f"{tag}var{g4}")
                lnv = ac(lambda e, sl=sl: e.activation(
                    out=stats[:, sl, 4:5], in_=stats[:, sl, 3:4], func=AF.Ln,
                    bias=epscol[:, 0:1]),
                    waits=[var, init_gp], tag=f"{tag}ln{g4}")
                rstd = ac(lambda e, sl=sl: e.activation(
                    out=stats[:, sl, 5:6], in_=stats[:, sl, 4:5], func=AF.Exp,
                    scale=-0.5),
                    waits=[lnv], tag=f"{tag}rstd{g4}")
                mrn = dv(lambda e, sl=sl: e.scalar_tensor_tensor(
                    out=stats[:, sl, 6:7], in0=stats[:, sl, 2:3],
                    scalar=-1.0, in1=stats[:, sl, 5:6],
                    op0=ALU.mult, op1=ALU.mult),
                    waits=[rstd], tag=f"{tag}mrn{g4}")
                opr = None
                if xpx:
                    opr = dv(lambda e, sl=sl: e.tensor_scalar(
                        out=stats[:, sl, 7:8], in0=stats[:, sl, 5:6],
                        scalar1=1.0, scalar2=None, op0=ALU.add),
                        waits=[rstd], tag=f"{tag}opr{g4}")
                settle = dv(lambda e, sl=sl: e.tensor_scalar(
                    out=stats[:, sl, 4:5], in0=stats[:, sl, 6:7],
                    scalar1=0.0, scalar2=None, op0=ALU.add),
                    waits=[mrn] + ([opr] if opr else []),
                    tag=f"{tag}settle{g4}")
                stats_guard[g4] = [settle]
                mrns.append(settle)
                oprs.append(settle if xpx else None)
            stats_done.clear()
            return mrns, oprs

        def ln_apply(i, dst, mrn, extra_waits, tag):
            if i % 2 == 0:
                return ac(lambda e, i=i, dst=dst: e.activation(
                    out=xn_tm[:, dst, :], in_=x_tm[:, i, :], func=AF.Identity,
                    scale=stats[:, i, 5:6], bias=stats[:, i, 6:7]),
                    waits=[mrn] + list(extra_waits), tag=f"{tag}xn{i}")
            return dv(lambda e, i=i, dst=dst: e.tensor_scalar(
                out=xn_tm[:, dst, :], in0=x_tm[:, i, :],
                scalar1=stats[:, i, 5:6], scalar2=stats[:, i, 6:7],
                op0=ALU.mult, op1=ALU.add),
                waits=[mrn] + list(extra_waits), tag=f"{tag}xn{i}")

        def ln_xpx(i, opr, apl, tag):
            if i % 2 == 1:
                return ac(lambda e, i=i: e.activation(
                    out=x_tm[:, i, :], in_=x_tm[:, i, :], func=AF.Identity,
                    scale=stats[:, i, 7:8], bias=stats[:, i, 6:7]),
                    waits=[opr, apl], tag=f"{tag}xpx{i}")
            return dv(lambda e, i=i: e.tensor_scalar(
                out=x_tm[:, i, :], in0=x_tm[:, i, :],
                scalar1=stats[:, i, 7:8], scalar2=stats[:, i, 6:7],
                op0=ALU.mult, op1=ALU.add),
                waits=[opr, apl], tag=f"{tag}xpx{i}")

        def fm_transpose_group(g, buf, src_tiles, xn_m, tag, guard_from=None):
            miles = []
            tm = None
            for half in range(4):
                dc0 = 2 * half
                bnk = 6 + (half % 2)
                trs = []
                for dci in range(2):
                    dc = dc0 + dci
                    for t in range(4):
                        out = pbb[bnk][:, 512 * dci + 128 * t:
                                       512 * dci + 128 * t + 128]
                        in_ = xn_tm[:, src_tiles[t], 128 * dc:128 * dc + 128]
                        trs.append((out, in_))
                w = list(xn_m)
                tm = tr_group(bnk, trs, waits=w, tag=f"{tag}tr{half}")
                eng = "vector" if half % 2 == 0 else "scalar"
                if eng == "vector":
                    def f(e, bnk=bnk, buf=buf, dc0=dc0):
                        return e.tensor_copy(xn_fm[:, buf, dc0:dc0 + 2, 4:516],
                                             pbb[bnk][:, 0:1024])
                else:
                    def f(e, bnk=bnk, buf=buf, dc0=dc0):
                        return e.copy(xn_fm[:, buf, dc0:dc0 + 2, 4:516],
                                      pbb[bnk][:, 0:1024])
                miles.append(evac(bnk, eng, f, waits=[tm], tag=f"{tag}ev{half}"))
            return miles, tm

        aw_pre = p_.barrier_waits("sPE", "sAC")
        mrns, oprs = ln_stats(xpx=True,
                              pre_waits=x_ready + [p_.cur("sDV"),
                                                   p_.cur("sAC")],
                              tag=f"b{b}ln1_")
        if carry_qa is None:
            LQA = LoadW(wqaT_d, 0, 0, f"b{b}ldwqa")
            LKA = LoadW(wkaT_d, 1, 1, f"b{b}ldwka")
            arm(LQA, slot_free_gate() if b > 0 else [])
            arm(LKA, slot_free_gate() if b > 0 else [])
        else:
            LQA, LKA = carry_qa, carry_ka
        carry_qa = carry_ka = None
        if n_sub >= 2:
            f1_ldrs = (LoadW(wrT_d, 0, 0, f"b{b}f1_ldwr"),
                       LoadW(wkfT_d, 1, 1, f"b{b}f1_ldwk"),
                       LoadW(wvfT_d, 2, 0, f"b{b}f1_ldwv"))
        else:
            f1_ldrs = None
        w_ka = LKA.fin()
        xn_m = [ln_apply(i, i, mrns[i // 4], [], tag=f"b{b}ln1_")
                for i in range(NT)]
        xpx_all = [ln_xpx(i, oprs[i // 4], xn_m[i], tag=f"b{b}ln1_")
                   for i in range(NT)]
        xpx_m = xpx_all[-2:]

        kproj_m = []
        for g in range(NG):
            fm_m, _ = fm_transpose_group(g, g % 2, [4 * g + t for t in range(4)],
                                         xn_m[4 * g + 2:4 * g + 4],
                                         tag=f"b{b}a1g{g}")
            for ec in range(NC):
                bnk = ec % 2
                mms = [(warena[:, 1, dc, 128 * ec:128 * ec + 128],
                        xn_fm[:, g % 2, dc, 4:516]) for dc in range(NC)]
                gm = mm_group(bnk, mms, waits=fm_m + [w_ka], tag=f"b{b}K{g}_{ec}")
                def f(e, bnk=bnk, ec=ec, g=g):
                    return e.activation(out=K_fm[:, ec, 512 * g:512 * g + 512],
                                        in_=pb[bnk].ap(), func=AF.Identity,
                                        bias=smalls[:, 1, ec:ec + 1])
                kproj_m.append(evac(bnk, "scalar", f, waits=[gm],
                                    tag=f"b{b}Kev{g}_{ec}"))

        if f1_ldrs:
            arm(f1_ldrs[1], slot_free_gate())
        cs_m = []
        for dh in range(2):
            mms = [(onescol[:, 0:1], xn_tm[:, kc, 512 * dh:512 * dh + 512],
                    pb[6].ap()[0:1, :]) for kc in range(NT)]
            gm = mm_group(6, mms, waits=xn_m[-2:] + [init_gp],
                          tag=f"b{b}cs{dh}")
            def f(e, dh=dh):
                return e.tensor_copy(colsum[0:1, 512 * dh:512 * dh + 512],
                                     pb[6].ap()[0:1, :])
            cs_m.append(evac(6, "vector", f, waits=[gm], tag=f"b{b}csev{dh}"))

        k_all = kproj_m[-1]

        def emit_scores(c, c4, qg_m):
            eb = c % 2
            zp = 4 * (c % 2)
            exp_m = []
            for kb in range(4):
                bnk = kb % 2
                mms = [(warena[:, 2, ec, 128 * c4:128 * c4 + 128],
                        K_fm[:, ec, 512 * kb:512 * kb + 512])
                       for ec in range(NC)]
                gm = mm_group(bnk, mms,
                              waits=qg_m + [k_all] + buf_war.get(
                                  ("ebuf", eb), []),
                              tag=f"b{b}s{c}_{kb}")
                def f(e, bnk=bnk, eb=eb, kb=kb, zp=zp):
                    return e.activation(
                        out=ebuf[:, eb, 512 * kb:512 * kb + 512],
                        in_=pb[bnk].ap(), func=AF.Exp, scale=1.0 / 32.0,
                        accum_out=zpart[:, zp + kb:zp + kb + 1])
                exp_m.append(evac(bnk, "scalar", f,
                                  waits=[gm] + buf_war.get(("zpart", eb), []),
                                  tag=f"b{b}e{c}_{kb}"))
            return exp_m

        def emit_tail(c, exp_m):
            eb = c % 2
            zp = 4 * (c % 2)
            zr = dv(lambda e, c=c, zp=zp: e.tensor_reduce(
                zbuf[:, c, 0:1], zpart[:, zp:zp + 4], axis=AX.X, op=ALU.add),
                waits=[exp_m[-1]], tag=f"b{b}Z{c}")
            buf_war[("zpart", eb)] = [zr]
            rz = dv(lambda e, c=c: e.reciprocal(zbuf[:, c, 1:2],
                                                zbuf[:, c, 0:1]),
                    waits=[zr], tag=f"b{b}rZ{c}")
            nz = dv(lambda e, c=c, eb=eb: e.tensor_scalar(
                out=nzcol[:, eb:eb + 1], in0=zbuf[:, c, 0:1],
                scalar1=-2.0 / S, scalar2=None, op0=ALU.mult),
                waits=[zr] + buf_war.get(("nzcol", eb), []),
                tag=f"b{b}nz{c}")
            ztr = tr_group(6, [(pbb[6][0:1, 0:P], nzcol[:, eb:eb + 1])],
                           waits=[nz], tag=f"b{b}ztr{c}")
            buf_war[("nzcol", eb)] = [ztr]
            zev = evac(6, "vector",
                       lambda e, eb=eb: e.tensor_copy(zrow[0:1, eb, :],
                                                      pbb[6][0:1, 0:P]),
                       waits=[ztr] + buf_war.get(("zrow", eb), []),
                       tag=f"b{b}zev{c}")
            pt_m = []
            for half in range(2):
                bnk = 2 + half
                trs = []
                for kci in range(8):
                    kc = 8 * half + kci
                    trs.append((pbb[bnk][:, 128 * kci:128 * kci + 128],
                                ebuf[:, eb, 128 * kc:128 * kc + 128]))
                tm = tr_group(bnk, trs, waits=[exp_m[-1]],
                              tag=f"b{b}pt{c}_{half}")
                def f(e, bnk=bnk, half=half):
                    return e.tensor_copy(ptbuf[:, 8 * half:8 * half + 8, :],
                                         pbb[bnk][:, 0:1024])
                pt_m.append(evac(bnk, "vector", f, waits=[tm],
                                 tag=f"b{b}ptev{c}_{half}"))
            buf_war[("ebuf", eb)] = [pt_m[-1]]
            for dh in range(2):
                bnk = 4 + dh
                mms = [(ptbuf[:, kc, :],
                        xn_tm[:, kc, 512 * dh:512 * dh + 512])
                       for kc in range(NT)]
                mms.append((zrow[0:1, eb, :],
                            colsum[0:1, 512 * dh:512 * dh + 512]))
                gm = mm_group(bnk, mms,
                              waits=pt_m + [zev] + cs_m, tag=f"b{b}pv{c}_{dh}")
                buf_war[("zrow", eb)] = [gm]
                def f(e, bnk=bnk, c=c, dh=dh):
                    return e.scalar_tensor_tensor(
                        out=x_tm[:, c, 512 * dh:512 * dh + 512],
                        in0=pb[bnk].ap(), scalar=zbuf[:, c, 1:2],
                        in1=x_tm[:, c, 512 * dh:512 * dh + 512],
                        op0=ALU.mult, op1=ALU.add)
                pv_ev = evac(bnk, "vector", f, waits=[gm, rz] + xpx_m,
                             tag=f"b{b}pvev{c}_{dh}")
            if n_sub >= 2:
                stats_tile(c, [pv_ev], f"b{b}f1_")

        pend = None
        w_qa = LQA.fin()
        for gi, g in enumerate([2, 3, 0, 1]):
            if gi >= 2:
                fm_m, _ = fm_transpose_group(g, g % 2,
                                             [4 * g + t for t in range(4)],
                                             xn_m[4 * g + 2:4 * g + 4],
                                             tag=f"b{b}a2g{g}")
            else:
                fm_m = []
            qg_m = []
            for ec in range(NC):
                bnk = ec % 2
                mms = [(warena[:, 0, dc, 128 * ec:128 * ec + 128],
                        xn_fm[:, g % 2, dc, 4:516]) for dc in range(NC)]
                gm = mm_group(bnk, mms, waits=fm_m + [w_qa], tag=f"b{b}Q{g}_{ec}")
                def f(e, bnk=bnk, ec=ec):
                    return e.activation(out=warena[:, 2, ec, 0:512],
                                        in_=pb[bnk].ap(), func=AF.Identity,
                                        bias=smalls[:, 0, ec:ec + 1])
                qg_m.append(evac(bnk, "scalar", f, waits=[gm],
                                 tag=f"b{b}Qev{g}_{ec}"))
            if gi == 3 and f1_ldrs:
                arm(f1_ldrs[0], slot_free_gate())
            for c4 in range(4):
                c = 4 * g + c4
                wstep()
                em = emit_scores(c, c4, qg_m)
                if pend is not None:
                    emit_tail(*pend)
                pend = (c, em)
        if f1_ldrs:
            arm(f1_ldrs[2], slot_free_gate())
        emit_tail(*pend)
        wstep()

        def emit_ffn(tag, ldrs, nxt, tile_hook=None, dbg_g0=False):
            if ffn_level < 0:
                return
            import os as _os0
            if _os0.environ.get("FFN_PRE2"):
                pre = p_.barrier_waits("sPE", "sAC")
            else:
                pre = p_.barrier_waits("sDV", "sAC", "sPE", "sGP", "sGD")

            xrxk = a3.ap()[:, 0:16384].bitcast(BF16).rearrange(
                "p (u c s) -> p u c s", u=2, c=NC)
            rc_fm = a3.ap()[:, 16384:32768].bitcast(F32).rearrange(
                "p (c s) -> p c s", c=NC)
            r_fm = xn_tm.ap()[:, 8:16, :].bitcast(F32)
            relu_b = ebuf.ap().rearrange("p u s -> p (u s)").rearrange(
                "p (c s) -> p c s", c=NC)

            if ffn_level == 0:
                return
            mrns2, _ = ln_stats(xpx=False, pre_waits=pre, tag=f"{tag}ln")
            w_r = ldrs[0].fin()
            w_k = ldrs[1].fin()
            w_v = ldrs[2].fin()
            gz = dv(lambda e: e.memset(xn_fm[:, 0, :, 3:4], 0.0), waits=pre,
                    tag=f"{tag}gz")
            guard_m = {0: gz}
            st_ = {"tr_prev": None, "mix_prev": None, "ab_prev": None,
                   "c_prev": None, "rc_prev": None}
            gstate = {}

            def s12(g):
                buf = g % 2
                aw = ([st_["tr_prev"]] if st_["tr_prev"] else [])
                xn_mg = [ln_apply(4 * g + t, t, mrns2[g], aw,
                                  tag=f"{tag}g{g}") for t in range(4)]
                fm_m, tr_last = fm_transpose_group(
                    g, buf, [0, 1, 2, 3], xn_mg[-2:], tag=f"{tag}g{g}")
                st_["tr_prev"] = tr_last
                if g + 1 < NG:
                    gc = dv(lambda e, buf=buf: e.tensor_copy(
                        xn_fm[:, 1 - buf, :, 3:4], xn_fm[:, buf, :, 515:516]),
                        waits=fm_m + ([st_["mix_prev"]] if st_["mix_prev"] else []),
                        tag=f"{tag}gc{g}")
                    guard_m[g + 1] = gc
                mw = fm_m + [guard_m[g]] + ([st_["ab_prev"]] if st_["ab_prev"] else [])
                mix_m = None
                for dc in range(NC):
                    wstep()
                    for u, mucol, omcol in ((0, 2, 0), (1, 3, 1)):
                        tsm = dv(lambda e, dc=dc, u=u, omcol=omcol, buf=buf:
                           e.tensor_scalar(
                               out=xrxk[:, u, dc, :],
                               in0=xn_fm[:, buf, dc, 3:515],
                               scalar1=dsm[:, omcol, dc:dc + 1], scalar2=None,
                               op0=ALU.mult),
                           waits=mw, tag=f"{tag}mx{g}_{dc}_{u}")
                        mix_m = dv(lambda e, dc=dc, u=u, mucol=mucol, buf=buf:
                                   e.scalar_tensor_tensor(
                                       out=xrxk[:, u, dc, :],
                                       in0=xn_fm[:, buf, dc, 4:516],
                                       scalar=smalls[:, mucol, dc:dc + 1],
                                       in1=xrxk[:, u, dc, :],
                                       op0=ALU.mult, op1=ALU.add),
                                   waits=[tsm], tag=f"{tag}mxs{g}_{dc}_{u}")
                st_["mix_prev"] = mix_m
                gstate[g] = {"mix_m": mix_m}

            def s3(g):
                gs = gstate[g]
                sig_m = []
                ab_m = None
                for ec in range(NC):
                    wstep()
                    bnk = ec % 2
                    mms = [(warena[:, 0, dc, 128 * ec:128 * ec + 128],
                            xrxk[:, 0, dc, :]) for dc in range(NC)]
                    gm = mm_group(bnk, mms, waits=[gs["mix_m"], w_r],
                                  tag=f"{tag}A{g}_{ec}")
                    sig_m.append(evac(bnk, "scalar",
                                      lambda e, bnk=bnk, ec=ec: e.activation(
                                          out=r_fm[:, ec, :], in_=pb[bnk].ap(),
                                          func=AF.Sigmoid),
                                      waits=[gm] + ([st_["rc_prev"]] if st_["rc_prev"] else []),
                                      tag=f"{tag}Aev{g}_{ec}"))
                for ec in range(NC):
                    bnk = ec % 2
                    mms = [(warena[:, 1, dc, 128 * ec:128 * ec + 128],
                            xrxk[:, 1, dc, :]) for dc in range(NC)]
                    gm = mm_group(bnk, mms, waits=[w_k], tag=f"{tag}B{g}_{ec}")
                    ab_m = gm
                    evac(bnk, "scalar",
                         lambda e, bnk=bnk, ec=ec: e.activation(
                             out=relu_b[:, ec, :], in_=pb[bnk].ap(),
                             func=AF.Relu),
                         waits=[gm] + ([st_["c_prev"]] if st_["c_prev"] else []),
                         tag=f"{tag}Bev{g}_{ec}")
                st_["ab_prev"] = ab_m
                if g == NG - 1 and nxt:
                    arm(nxt[0], slot_free_gate())
                    arm(nxt[1], slot_free_gate())
                relu_m = p_.cur("sAC")
                k2_m = dv(lambda e: e.tensor_tensor(
                    out=relu_b[:, :, :], in0=relu_b[:, :, :],
                    in1=relu_b[:, :, :], op=ALU.mult),
                    waits=[relu_m], tag=f"{tag}k2{g}")
                gs["k2_m"] = k2_m
                gs["sig_m"] = sig_m

            def s4(g):
                gs = gstate[g]
                rc_m = []
                cm = None
                for ec in range(NC):
                    wstep()
                    bnk = ec % 2
                    mms = [(warena[:, 2, dc, 128 * ec:128 * ec + 128],
                            relu_b[:, dc, :]) for dc in range(NC)]
                    gm = mm_group(bnk, mms, waits=[gs["k2_m"], w_v],
                                  tag=f"{tag}C{g}_{ec}")
                    cm = gm
                    rc_m.append(evac(bnk, "vector",
                                     lambda e, bnk=bnk, ec=ec: e.tensor_tensor(
                                         out=rc_fm[:, ec, :], in0=pb[bnk].ap(),
                                         in1=r_fm[:, ec, :], op=ALU.mult),
                                     waits=[gm, gs["sig_m"][ec]],
                                     tag=f"{tag}Cev{g}_{ec}"))
                st_["c_prev"] = cm
                st_["rc_prev"] = rc_m[-1]
                if g == NG - 1 and nxt and nxt[2]:
                    arm(nxt[2], slot_free_gate())
                for t in range(4):
                    ra_ev = None
                    for half in range(2):
                        bnk = 2 + half
                        trs = []
                        for eci in range(4):
                            ec = 4 * half + eci
                            trs.append((pb[bnk].ap()[:, 128 * eci:128 * eci + 128],
                                        rc_fm[:, ec, 128 * t:128 * t + 128]))
                        tm = tr_group(bnk, trs, waits=[rc_m[-1]], dtype=F32,
                                      tag=f"{tag}rt{g}_{t}_{half}")
                        def f(e, bnk=bnk, g=g, t=t, half=half):
                            return e.tensor_tensor(
                                out=x_tm[:, 4 * g + t, 512 * half:512 * half + 512],
                                in0=pb[bnk].ap(),
                                in1=x_tm[:, 4 * g + t, 512 * half:512 * half + 512],
                                op=ALU.add)
                        ra_ev = evac(bnk, "vector", f, waits=[tm],
                                     tag=f"{tag}radd{g}_{t}_{half}")
                    if tile_hook is not None:
                        tile_hook(g, t, ra_ev)

            s12(0); s3(0)
            for g in range(1, NG):
                s12(g)
                s4(g - 1)
                s3(g)
            s4(NG - 1)

        if False and dbg and b == DBG_SEQ:
            dwv = p_.barrier_waits("sDV", "sAC", "sPE")
            def cast_dump(name, src_ap, waits):
                gp_dma(dbg_d[name], src_ap, waits=waits, tag=f"dbg{name}")
            cast_dump("stats", stats[:], dwv)
            cast_dump("xn", xn_tm[:], dwv)
            cast_dump("xpx", x_tm[:], dwv)
            cast_dump("K", K_fm, dwv)
            cast_dump("csum", colsum[:], dwv)
            cast_dump("zbuf", zbuf[:], dwv)
            cast_dump("E8", ebuf[:, 0, :], dwv)
            cast_dump("q2", warena[:, 2, :, 0:512], dwv)

        def emit_attn_b(tag, ldrs, nxt):
            pre = p_.barrier_waits("sDV", "sAC", "sPE", "sGP", "sGD")
            mrns3, _ = ln_stats(xpx=False, pre_waits=pre, tag=f"{tag}ln")
            w_qb = ldrs[0].fin()
            w_kb = ldrs[1].fin()
            w_vb = ldrs[2].fin()
            a3b = a3.ap().bitcast(BF16)
            a3f = a3.ap().bitcast(F32)
            sqkv = a3b[:, 0:12288].rearrange("p (u c s) -> p u c s",
                                             u=3, c=NC)
            Mf = a3f[:, 6144:7168].rearrange("p (c s) -> p c s", c=NC)
            Maug = a3b[:, 14336:15392].rearrange("p (c s) -> p c s", c=NC)
            zf = a3f[:, 7696:7704]
            ztmp = a3f[:, 7704:7712]
            dsc = a3f[:, 7712:7760]
            vd_b = xn_tm.ap()[:, 12, 0:128]
            tmpf = xn_tm.ap()[:, 12, :].bitcast(F32)[:, 128:256]
            adbuf_a = xn_tm.ap()[:, 13, :].bitcast(F32)
            adbuf_b = ebuf.ap()[:, 1, :].bitcast(F32)[:, 0:512]
            t1f = xn_tm.ap()[:, 14, :].bitcast(F32)
            t2f = xn_tm.ap()[:, 15, :].bitcast(F32)
            Ebuf = ebuf.ap()[:, 0, :]

            mi1 = dv(lambda e: e.memset(Mf, 0.0), waits=pre, tag=f"{tag}M0")
            mi2 = dv(lambda e: e.memset(Maug, 0.0), waits=pre, tag=f"{tag}Ma0")
            mi3 = dv(lambda e: e.memset(zf, 0.0), waits=pre, tag=f"{tag}z0")
            maug_ready = [mi1, mi2, mi3]

            tr_prev = None
            proj_war = []
            for G in range(NSEGS):
                buf = G % 2
                aw = ([tr_prev] if tr_prev else [])
                xn_mg = [ln_apply(4 * G + t, t, mrns3[G], aw,
                                  tag=f"{tag}g{G}") for t in range(4)]
                fm_m, tr_last = fm_transpose_group(G, buf, [0, 1, 2, 3],
                                                   xn_mg[-2:], tag=f"{tag}g{G}")
                tr_prev = tr_last
                ev_m = []
                for u, wm, bcol in ((0, w_qb, 4), (1, w_kb, 5), (2, w_vb, 6)):
                    for ec in range(NC):
                        bnk = ec % 2
                        mms = [(warena[:, u, dc, 128 * ec:128 * ec + 128],
                                xn_fm[:, buf, dc, 4:516]) for dc in range(NC)]
                        gm = mm_group(bnk, mms, waits=fm_m + [wm] + proj_war,
                                      tag=f"{tag}p{G}u{u}e{ec}")
                        ev_m.append(evac(bnk, "scalar",
                                    lambda e, bnk=bnk, u=u, ec=ec, bcol=bcol:
                                    e.activation(out=sqkv[:, u, ec, :],
                                                 in_=pb[bnk].ap(),
                                                 func=AF.Identity,
                                                 bias=smalls[:, bcol, ec:ec + 1]),
                                    waits=[gm], tag=f"{tag}pe{G}{u}{ec}"))
                ev_last = [ev_m[-1]]
                if G == NSEGS - 1 and nxt:
                    for L_ in nxt:
                        if L_ is not None:
                            arm(L_, slot_free_gate())
                vt_m = []
                for t in range(4):
                    bnk = 2 + t % 2
                    trs = [(pbb[bnk][:, 128 * ec:128 * ec + 128],
                            sqkv[:, 2, ec, 128 * t:128 * t + 128])
                           for ec in range(NC)]
                    tm_ = tr_group(bnk, trs, waits=ev_last,
                                   tag=f"{tag}vt{G}_{t}")
                    def f(e, bnk=bnk, t=t):
                        return e.tensor_copy(xn_tm[:, 4 + t, :],
                                             pbb[bnk][:, 0:1024])
                    vt_m.append(evac(bnk, "vector", f, waits=[tm_],
                                     tag=f"{tag}vte{G}_{t}"))
                vt_last = vt_m[-1]
                num_read_m = []
                mdelta_first = {6: True, 7: True}
                mdelta_last_m = None
                def passA(h):
                    omgcol = dsm[:, 3, h:h + 1]
                    adbuf = adbuf_a if h % 2 == 0 else adbuf_b
                    z0m = dv(lambda e: e.memset(dsc[:, 36:37], 0.0),
                             waits=buf_war.get("zb1", []),
                             tag=f"{tag}z0_{G}_{h}")
                    e1 = z2m = None
                    adm_l, pte_l = [], []
                    for lc in range(4):
                        par = lc % 4
                        Eslot = Ebuf[:, 512 * par:512 * (par + 1)]
                        sb_ = (h + lc) % 2
                        ncols = 128 * (lc + 1)
                        gm = mm_group(sb_, [(sqkv[:, 0, h, 128 * lc:128 * lc + 128],
                                             sqkv[:, 1, h, 0:ncols],
                                             pb[sb_].ap()[:, 0:ncols])],
                                      waits=ev_last + buf_war.get(("E", par), []),
                                      tag=f"{tag}s{G}_{h}_{lc}")
                        if lc > 0:
                            e1 = ac(lambda e, sb_=sb_, lc=lc, Eslot=Eslot:
                                    e.activation(out=Eslot[:, 0:128 * lc],
                                                 in_=pb[sb_].ap()[:, 0:128 * lc],
                                                 func=AF.Exp,
                                                 scale=1.0 / np.sqrt(DK),
                                                 accum_out=dsc[:, 36 + lc:37 + lc]),
                                    waits=[gm] + buf_war.get("zb1", []),
                                    tag=f"{tag}e1_{G}_{h}_{lc}")
                        e2 = ac(lambda e, sb_=sb_, lc=lc, Eslot=Eslot:
                                e.activation(out=Eslot[:, 128 * lc:128 * lc + 128],
                                             in_=pb[sb_].ap()[:, 128 * lc:128 * lc + 128],
                                             func=AF.Exp, scale=1.0 / np.sqrt(DK)),
                                waits=([e1] if e1 else [gm]),
                                tag=f"{tag}e2_{G}_{h}_{lc}")
                        bank_war[sb_] = e2
                        mk = dv(lambda e, lc=lc, Eslot=Eslot: e.tensor_tensor(
                            out=Eslot[:, 128 * lc:128 * lc + 128],
                            in0=Eslot[:, 128 * lc:128 * lc + 128],
                            in1=tril01[:], op=ALU.mult),
                            waits=[e2, init_gp], tag=f"{tag}mk{G}_{h}_{lc}")
                        z2m = dv(lambda e, lc=lc, Eslot=Eslot:
                                 e.tensor_reduce(dsc[:, 40 + lc:41 + lc],
                                                 Eslot[:, 128 * lc:128 * lc + 128],
                                                 axis=AX.X, op=ALU.add),
                                 waits=[mk], tag=f"{tag}zb2_{G}_{h}_{lc}")
                        trs = [(pbb[2][:, 128 * j:128 * j + 128],
                                Eslot[:, 128 * j:128 * j + 128])
                               for j in range(lc + 1)]
                        ptm = tr_group(2, trs, waits=[mk], tag=f"{tag}pt{G}_{h}_{lc}")
                        pte = evac(2, "scalar",
                                   lambda e, lc=lc: e.copy(
                                       ptbuf[:, 0:lc + 1, :],
                                       pbb[2][:, 0:128 * (lc + 1)]),
                                   waits=[ptm] + buf_war.get("ptbuf", []),
                                   tag=f"{tag}pte{G}_{h}_{lc}")
                        pte_l.append(pte)
                        buf_war[("E", par)] = [ptm]
                        for j in range(lc + 1):
                            st = (lc == 0 and j == 0)
                            sp = (lc == 3 and j == 3)
                            def fad(e, j=j, lc=lc, h=h, st=st, sp=sp):
                                return e.matmul(
                                    pb[4].ap()[:, 128 * lc:128 * lc + 128],
                                    ptbuf[:, j, :],
                                    xn_tm[:, 4 + j, 128 * h:128 * h + 128],
                                    start=st, stop=sp)
                            w = [pte] if j == 0 else []
                            if st:
                                w = w + [vt_last] + buf_war.get("bank4", [])
                            adm = p_.add("tensor", fad, waits=w,
                                         inc=("sPE" if j == lc else None),
                                         tag=f"{tag}ad{G}_{h}_{lc}_{j}")
                        adm_l.append(adm)
                        buf_war["ptbuf"] = [adm]
                    zsum = dv(lambda e: e.tensor_tensor(
                        out=dsc[:, 36:40], in0=dsc[:, 36:40],
                        in1=dsc[:, 40:44], op=ALU.add),
                        waits=[z2m, e1, z0m], tag=f"{tag}zs_{G}_{h}")
                    rzb = dv(lambda e: e.reciprocal(dsc[:, 40:44],
                                                    dsc[:, 36:40]),
                             waits=[zsum], tag=f"{tag}rz_{G}_{h}")
                    buf_war["zb1"] = [rzb]
                    sdb = dv(lambda e, omgcol=omgcol: e.tensor_scalar(
                        out=dsc[:, 44:48], in0=dsc[:, 40:44],
                        scalar1=omgcol, scalar2=None, op0=ALU.mult),
                        waits=[rzb] + buf_war.get("zb_sd", []),
                        tag=f"{tag}sd_{G}_{h}")
                    ads = None
                    for lc in range(4):
                        ads = evac(4, "scalar",
                                   lambda e, lc=lc: e.activation(
                                       out=adbuf[:, 128 * lc:128 * lc + 128],
                                       in_=pb[4].ap()[:, 128 * lc:128 * lc + 128],
                                       func=AF.Identity,
                                       scale=dsc[:, 44 + lc:45 + lc]),
                                   waits=[adm_l[3], sdb] +
                                         buf_war.get(("ad", h % 2, lc), []),
                                   tag=f"{tag}ads{G}_{h}_{lc}")
                    buf_war["zb_sd"] = [ads]
                    buf_war["bank4"] = [ads]
                    sc_pe[h] = p_.cur("sPE")

                def passB(h):
                    nonlocal mdelta_last_m
                    gcol = dsm[:, 2, h:h + 1]
                    adbuf = adbuf_a if h % 2 == 0 else adbuf_b
                    sc_m = sc_pe[h]
                    t12 = xn_tm.ap()[:, 14:16, :].bitcast(F32)
                    squ = sqkv[:, 0:2, h, :]
                    a1 = ac(lambda e, h=h: e.activation(
                        out=t12, in_=squ, func=AF.Relu, scale=-1.0),
                        waits=[("sPE", sc_m[1])] + buf_war.get("t1f", []),
                        tag=f"{tag}tr1_{G}_{h}")
                    a2 = ac(lambda e: e.activation(out=t12, in_=t12,
                                                   func=AF.Exp, scale=-1.0),
                            waits=[a1], tag=f"{tag}tr2_{G}_{h}")
                    el = dv(lambda e, h=h: e.scalar_tensor_tensor(
                        out=squ, in0=squ, scalar=0.0, in1=t12,
                        op0=ALU.max, op1=ALU.add),
                        waits=[a2], tag=f"{tag}tr4_{G}_{h}")
                    buf_war["t1f"] = [el]
                    el_m = [el, el]
                    sktr = tr_group(3, [(pbb[3][:, 128 * t:128 * t + 128],
                                         sqkv[:, 1, h, 128 * t:128 * t + 128])
                                        for t in range(4)],
                                    waits=[el_m[1]], tag=f"{tag}skt{G}_{h}")
                    skte = evac(3, "vector",
                                lambda e, h=h: e.tensor_copy(
                                    xn_tm.ap()[:, 8:12, 128 * h:128 * h + 128],
                                    pbb[3][:, 0:512].rearrange(
                                        "p (t c) -> p t c", t=4)),
                                waits=[sktr], tag=f"{tag}skte{G}_{h}")
                    zt_hold[0] = dv(lambda e, h=h: e.tensor_reduce(
                        ztmp[:, h:h + 1], sqkv[:, 1, h, :], axis=AX.X,
                        op=ALU.add), waits=[el_m[1]], tag=f"{tag}zt{G}_{h}")
                    denm = mm_group(5, [(sqkv[:, u, h, 128 * lc:128 * lc + 128],
                                         Maug[:, h, 128:129],
                                         pb[5].ap()[:, 2 * lc + u:2 * lc + u + 1])
                                        for lc in range(4) for u in range(2)],
                                    waits=maug_ready + [el_m[1]],
                                    tag=f"{tag}den{G}_{h}")
                    num_read_m.append(denm)
                    dev = evac(5, "scalar", lambda e: e.activation(
                        out=dsc[:, 16:24], in_=pb[5].ap()[:, 0:8],
                        func=AF.Identity, bias=bepscol[:, 0:1]),
                        waits=[denm] + buf_war.get("dsc16", []),
                        tag=f"{tag}dev{G}_{h}")
                    rcp = dsc[:, 24:32].rearrange("p (l u) -> p l u", u=2)
                    rbm = dv(lambda e: e.reciprocal(dsc[:, 24:32],
                                                    dsc[:, 16:24]),
                             waits=[dev], tag=f"{tag}rb{G}_{h}")
                    buf_war["dsc16"] = [rbm]
                    gqb = dv(lambda e, gcol=gcol: e.tensor_scalar(
                        out=dsc[:, 32:36], in0=rcp[:, :, 0:1],
                        scalar1=gcol, scalar2=None, op0=ALU.mult),
                        waits=[rbm], tag=f"{tag}gq{G}_{h}")
                    c1_last = None
                    for lc in range(4):
                        nb = 3 if lc % 2 == 0 else 5
                        nq = [(sqkv[:, 0, h, 128 * lc:128 * lc + 128],
                               Maug[:, h, 0:128], pb[nb].ap()[:, 0:128]),
                              (sqkv[:, 1, h, 128 * lc:128 * lc + 128],
                               Maug[:, h, 0:128], pb[nb].ap()[:, 128:256])]
                        nm = mm_group(nb, nq, waits=maug_ready + [el_m[1], skte],
                                      tag=f"{tag}nm{G}_{h}_{lc}")
                        num_read_m.append(nm)
                        vdp = xn_tm.ap()[:, 12, 128 * (lc % 2):
                                         128 * (lc % 2) + 128]
                        vdm = dv(lambda e, h=h, lc=lc, vdp=vdp, nb=nb:
                                 e.scalar_tensor_tensor(
                                     out=vdp, in0=pb[nb].ap()[:, 128:256],
                                     scalar=rcp[:, lc, 1:2],
                                     in1=xn_tm[:, 4 + lc, 128 * h:128 * h + 128],
                                     op0=ALU.mult, op1=ALU.subtract),
                                 waits=[nm, gqb] + buf_war.get(("vd", lc % 2), []),
                                 tag=f"{tag}vd_{G}_{h}_{lc}")
                        mb = 6 + h // 4
                        def fmd(e, h=h, lc=lc, mb=mb, vdp=vdp,
                                st=mdelta_first[mb]):
                            return e.matmul(
                                pb[mb].ap()[:, 128 * (h % 4):128 * (h % 4) + 128],
                                xn_tm[:, 8 + lc, 128 * h:128 * h + 128],
                                vdp, start=st,
                                stop=(h % 4 == 3 and lc == 3))
                        mw = [vdm, skte]
                        if mdelta_first[mb] and mb in bank_war and bank_war[mb]:
                            mw.append(bank_war[mb])
                        mdelta_last_m = p_.add("tensor", fmd, waits=mw,
                                               inc="sPE",
                                               tag=f"{tag}md{G}_{h}_{lc}")
                        mdelta_first[mb] = False
                        buf_war[("vd", lc % 2)] = [mdelta_last_m]
                        c1_last = dv(lambda e, h=h, lc=lc, nb=nb:
                                     e.scalar_tensor_tensor(
                                         out=adbuf[:, 128 * lc:128 * lc + 128],
                                         in0=pb[nb].ap()[:, 0:128],
                                         scalar=dsc[:, 32 + lc:33 + lc],
                                         in1=adbuf[:, 128 * lc:128 * lc + 128],
                                         op0=ALU.mult, op1=ALU.add),
                                     waits=[vdm], tag=f"{tag}c1_{G}_{h}_{lc}")
                        bank_war[nb] = c1_last
                    c2 = dv(lambda e, G=G, h=h, adbuf=adbuf: e.tensor_tensor(
                        out=x_tm[:, 4 * G:4 * G + 4, 128 * h:128 * h + 128],
                        in0=adbuf.rearrange("p (l c) -> p l c", l=4),
                        in1=x_tm[:, 4 * G:4 * G + 4, 128 * h:128 * h + 128],
                        op=ALU.add), waits=[c1_last],
                        tag=f"{tag}c2_{G}_{h}")
                    for lc in range(4):
                        buf_war[("ad", h % 2, lc)] = [c2]
                sc_pe = {}
                zt_hold = [None]
                passA(0)
                for h in range(H):
                    wstep()
                    if h + 1 < H:
                        passA(h + 1)
                    passB(h)
                if n_sub >= 4:
                    for lc in range(4):
                        stats_tile(4 * G + lc, buf_war[("ad", 1, lc)],
                                   f"{tag}pf_")
                zu = dv(lambda e: e.tensor_tensor(out=zf, in0=zf, in1=ztmp,
                                                  op=ALU.add),
                        waits=[zt_hold[0]] + num_read_m[-1:], tag=f"{tag}zu{G}")
                zc = dv(lambda e: e.tensor_copy(
                    Maug[:, :, 128:129],
                    zf.rearrange("p (c o) -> p c o", o=1)),
                        waits=[zu], tag=f"{tag}zc{G}")
                m1_ = dv(lambda e: e.tensor_tensor(out=Mf[:, 0:4, :],
                                                   in0=Mf[:, 0:4, :],
                                                   in1=pb[6].ap(),
                                                   op=ALU.subtract),
                         waits=[mdelta_last_m], tag=f"{tag}mu{G}a")
                m2_ = dv(lambda e: e.tensor_tensor(out=Mf[:, 4:8, :],
                                                   in0=Mf[:, 4:8, :],
                                                   in1=pb[7].ap(),
                                                   op=ALU.subtract),
                         waits=[m1_], tag=f"{tag}mu{G}b")
                bank_war[6] = m2_
                bank_war[7] = m2_
                mc = dv(lambda e: e.tensor_copy(Maug[:, :, 0:128], Mf),
                        waits=[m2_] + num_read_m[-1:], tag=f"{tag}mc{G}")
                maug_ready = [mc, zc]
                proj_war = [p_.cur("sPE"), p_.cur("sDV")]

        ab_ldrs = None
        if n_sub >= 3:
            ab_ldrs = (LoadW(wqb_d, 0, 0, f"b{b}ab_ldwq"),
                       LoadW(wkb_d, 1, 1, f"b{b}ab_ldwk"),
                       LoadW(wvb_d, 2, 0, f"b{b}ab_ldwv"))
        f2_ldrs = None
        if n_sub >= 4:
            f2_ldrs = (LoadW(wrT_d, 0, 0, f"b{b}f2_ldwr"),
                       LoadW(wkfT_d, 1, 1, f"b{b}f2_ldwk"),
                       LoadW(wvfT_d, 2, 0, f"b{b}f2_ldwv"))
        nseq = None
        if n_sub >= 4 and b + 1 < BL:
            carry_qa = LoadW(wqaT_d, 0, 0, f"b{b + 1}ldwqa")
            carry_ka = LoadW(wkaT_d, 1, 1, f"b{b + 1}ldwka")
            nseq = (carry_qa, carry_ka, None)
        def f2_hook(g, t, m, b=b):
            i = 4 * g + t
            otm = out_d[b].rearrange("(t p) d -> p t d", p=P)
            sy_dma(otm[:, i:i + 1, :], x_tm[:, i:i + 1, :], waits=[m],
                   store=True, tag=f"st{b}_{i}")
            if b + 1 < BL:
                itm = x_d[b + 1].rearrange("(t p) d -> p t d", p=P)
                ld = sy_dma(x_tm[:, i:i + 1, :], itm[:, i:i + 1, :],
                            tag=f"ldx{b + 1}_{i}")
                stats_tile(i, [ld], f"b{b + 1}ln1_")

        f1_hook = None
        if n_sub >= 3:
            def f1_hook(g, t, m):
                stats_tile(4 * g + t, [m], f"b{b}ab_")
        if n_sub >= 2:
            emit_ffn(f"b{b}f1_", f1_ldrs, ab_ldrs, tile_hook=f1_hook,
                     dbg_g0=(dbg and b == DBG_SEQ))
        if n_sub >= 3:
            emit_attn_b(f"b{b}ab_", ab_ldrs, f2_ldrs)
        if n_sub >= 4:
            emit_ffn(f"b{b}f2_", f2_ldrs, nseq, tile_hook=f2_hook)
            NSTORE += NT
        assert not armed or (n_sub >= 4 and b + 1 < BL), \
            f"unarmed/armed leftovers at b={b}: {[L.tag for L in armed]}"

        if n_sub < 4:
            st_w = p_.barrier_waits("sDV", "sAC", "sPE")
            sy_dma(out_d[b].rearrange("(t p) d -> p t d", p=P), x_tm[:],
                   waits=st_w, store=True, tag=f"st{b}")
            NSTORE += 1

    p_.add("sync", lambda e: e.nop(),
           waits=[("sST", 16 * NSTORE), p_.cur("sGD"), p_.cur("sXD")],
           tag="final")

    p_.simulate()

    with ctx:
        blk = ctx.enter_context(nc.Block())

        def make_body(eng_name):
            def body(eng):
                for (fn, w, incs, tag) in p_.ops[eng_name]:
                    for (s, v) in w:
                        eng.wait_ge(sems[s], v)
                    inst = fn(eng)
                    for (s, v, a) in incs:
                        if a:
                            inst = inst.then_inc(sems[s], v)
            return body

        blk.sync(make_body("sync"))
        blk.gpsimd(make_body("gpsimd"))
        blk.tensor(make_body("tensor"))
        blk.vector(make_body("vector"))
        blk.scalar(make_body("scalar"))

    return nc


def _host_layout(inputs):
    g = lambda k: np.asarray(inputs[k])
    smalls = np.zeros((P, 8, NC), dtype=np.float32)
    def fm(v):
        return np.ascontiguousarray(v.reshape(NC, P).T)
    smalls[:, 0, :] = fm(g("L0_attn_a_bq"))
    smalls[:, 1, :] = fm(g("L0_attn_a_bk"))
    smalls[:, 2, :] = fm(g("L0_ffn_mu_r"))
    smalls[:, 3, :] = fm(g("L0_ffn_mu_k"))
    smalls[:, 4, :] = fm(g("L0_attn_b_bq"))
    smalls[:, 5, :] = fm(g("L0_attn_b_bk"))
    smalls[:, 6, :] = fm(g("L0_attn_b_bv"))
    smalls[:, 7, :] = np.broadcast_to(g("L0_attn_b_beta")[None, :], (P, 8))
    T = lambda a: np.ascontiguousarray(a.T)
    return {
        "wqaT": T(g("L0_attn_a_Wq")), "wkaT": T(g("L0_attn_a_Wk")),
        "wrT": T(g("L0_ffn_W_r")), "wkfT": T(g("L0_ffn_W_k")),
        "wvfT": T(g("L0_ffn_W_v")),
        "wqb": np.ascontiguousarray(g("L0_attn_b_Wq")),
        "wkb": np.ascontiguousarray(g("L0_attn_b_Wk")),
        "wvb": np.ascontiguousarray(g("L0_attn_b_Wv")),
        "smalls": smalls,
    }


N_SUB = 4

def kernel(**inputs):
    x = np.ascontiguousarray(np.asarray(inputs["L0_attn_a_x"], dtype=np.float32))
    shared = _host_layout(inputs)
    nc = build_nc(n_sub=N_SUB)
    in_maps = []
    for i in range(NCORES):
        m = dict(shared)
        m["x"] = np.ascontiguousarray(x[BL * i:BL * (i + 1)])
        in_maps.append(m)
    res = run_bass_kernel_spmd(nc, in_maps, core_ids=list(range(NCORES)))
    return np.concatenate([r["out"] for r in res.results], axis=0)


if __name__ == "__main__":
    nc = build_nc()
    print("built OK")


# version 12
# speedup vs baseline: 1.2147x; 1.0370x over previous
import numpy as np
from contextlib import ExitStack

import concourse.bass as bass
import concourse.mybir as mybir
from concourse.bass_utils import run_bass_kernel_spmd

F32 = mybir.dt.float32
BF16 = mybir.dt.bfloat16
AF = mybir.ActivationFunctionType
ALU = mybir.AluOpType
AX = mybir.AxisListType

P = 128
B, S, D = 32, 2048, 1024
NCORES = 8
BL = B // NCORES
NT = S // P
NC = D // P
NG = S // 512
H, DK, DV, SEG, NSEGS = 8, 128, 128, 512, 4
LN_EPS = 1e-5
B_EPS = 1e-6

ENGS = ("sync", "gpsimd", "tensor", "vector", "scalar")


class Plan:

    def __init__(self):
        self.ops = {e: [] for e in ENGS}
        self.cnt = {s: 0 for s in ("sPE", "sDV", "sAC", "sGP", "sXD", "sGD", "sST")}
        self.seen = {e: {} for e in ENGS}
        self.eng_sem = {"tensor": "sPE", "vector": "sDV", "scalar": "sAC",
                        "gpsimd": "sGP"}

    def add(self, eng, fn, waits=(), inc=None, n=1, tag="", attach=True):
        w = []
        for wv in waits:
            if wv is None:
                continue
            (s, v) = wv
            if v <= 0:
                continue
            if self.seen[eng].get(s, 0) >= v:
                continue
            self.seen[eng][s] = v
            w.append((s, v))
        incs = []
        if inc is not None:
            self.cnt[inc] += n
            incs.append((inc, n, attach))
        self.ops[eng].append((fn, w, incs, tag))
        return (inc, self.cnt[inc]) if inc is not None else None

    def cur(self, s):
        return (s, self.cnt[s])

    def barrier_waits(self, *sems):
        return [(s, self.cnt[s]) for s in sems]

    def simulate(self):
        pos = {e: 0 for e in ENGS}
        val = {s: 0 for s in self.cnt}
        progress = True
        while progress:
            progress = False
            for e in ENGS:
                while pos[e] < len(self.ops[e]):
                    fn, w, incs, tag = self.ops[e][pos[e]]
                    if any(val[s] < v for (s, v) in w):
                        break
                    for (s, v, _a) in incs:
                        val[s] += v
                    pos[e] += 1
                    progress = True
        stuck = {e: pos[e] for e in ENGS if pos[e] < len(self.ops[e])}
        if stuck:
            msg = []
            for e, p_ in stuck.items():
                fn, w, incs, tag = self.ops[e][p_]
                msg.append(f"{e}@{p_}/{len(self.ops[e])} tag={tag} waits={w} "
                           f"sems={[(s, val[s]) for s, _ in w]}")
            raise RuntimeError("PLAN DEADLOCK:\n" + "\n".join(msg))


FFN_LEVEL = 3
DBG_SEQ = 0

def build_nc(n_sub=4, dbg=False, ffn_level=None):
    if ffn_level is None:
        ffn_level = FFN_LEVEL
    nc = bass.Bass("TRN2", target_bir_lowering=False,
                   dynamic_dma_scratch_size=8192)

    x_d = nc.dram_tensor("x", [BL, S, D], F32, kind="ExternalInput").ap()
    out_d = nc.dram_tensor("out", [BL, S, D], F32, kind="ExternalOutput").ap()
    wqaT_d = nc.dram_tensor("wqaT", [D, D], F32, kind="ExternalInput").ap()
    wkaT_d = nc.dram_tensor("wkaT", [D, D], F32, kind="ExternalInput").ap()
    wrT_d = nc.dram_tensor("wrT", [D, D], F32, kind="ExternalInput").ap()
    wkfT_d = nc.dram_tensor("wkfT", [D, D], F32, kind="ExternalInput").ap()
    wvfT_d = nc.dram_tensor("wvfT", [D, D], F32, kind="ExternalInput").ap()
    wqb_d = nc.dram_tensor("wqb", [D, D], F32, kind="ExternalInput").ap()
    wkb_d = nc.dram_tensor("wkb", [D, D], F32, kind="ExternalInput").ap()
    wvb_d = nc.dram_tensor("wvb", [D, D], F32, kind="ExternalInput").ap()
    smalls_d = nc.dram_tensor("smalls", [P, 8, NC], F32, kind="ExternalInput").ap()
    if dbg:
        dbg_d = {k: nc.dram_tensor(f"dbg_{k}", shp, F32, kind="ExternalOutput").ap()
                 for k, shp in [("stats", [P, NT, 8]), ("xn", [P, NT, D]),
                                ("xpx", [P, NT, D]), ("K", [P, NC, S]),
                                ("csum", [1, D]), ("zbuf", [P, NT, 2]),
                                ("E8", [P, S]), ("q2", [P, NC, 512]),
                                ("xrxk", [P, 2, NC, 512]), ("r", [P, NC, 512]),
                                ("k2", [P, NC, 512]), ("rc", [P, NC, 512]),
                                ("xnfm", [P, 2, NC, 516])]}

    ctx = ExitStack()
    sb = lambda *a: ctx.enter_context(nc.sbuf_tensor(*a))
    ps = lambda *a: ctx.enter_context(nc.psum_tensor(*a))

    x_tm = sb("x_tm", [P, NT, D], F32)
    xn_tm = sb("xn_tm", [P, NT, D], BF16)
    xn_fm = sb("xn_fm", [P, 2, NC, 516], BF16)
    a3 = sb("a3", [P, 32768], mybir.dt.uint8)
    warena = sb("warena", [P, 3, NC, D], BF16)
    ebuf = sb("ebuf", [P, 2, S], BF16)
    ptbuf = sb("ptbuf", [P, 17, P], BF16)
    stats = sb("stats", [P, NT, 8], F32)
    smalls = sb("smalls_s", [P, 8, NC], F32)
    dsm = sb("dsm", [P, 8, NC], F32)
    identb = sb("identb", [P, P], BF16)
    tril01 = sb("tril01", [P, P], BF16)
    identf = sb("identf", [P, P], F32)
    onescol = sb("onescol", [P, 1], BF16)
    epscol = sb("epscol", [P, 1], F32)
    bepscol = sb("bepscol", [P, 1], F32)
    zbuf = sb("zbuf", [P, NT, 2], F32)
    zpart = sb("zpart", [P, 8], F32)
    nzcol = sb("nzcol", [P, 2], BF16)
    zrow = sb("zrow", [1, 2, P], BF16)
    colsum = sb("colsum", [1, D], BF16)
    wst0 = sb("wst0", [P, 1024], F32)
    junkb = sb("junkb", [P, 1024], BF16)

    K_fm = a3.ap().bitcast(BF16).rearrange("p (c s) -> p c s", c=NC)
    junk = ebuf.ap().bitcast(F32)[:, 0, :]

    pb = [ps(f"pb{i}", [P, 512], F32) for i in range(8)]
    pbb = [pb[i].ap().bitcast(BF16) for i in range(8)]

    sems = {}
    for s in ("sPE", "sDV", "sAC", "sGP", "sXD", "sGD", "sST"):
        sems[s] = ctx.enter_context(nc.semaphore(s))

    p_ = Plan()
    bank_war = {}
    buf_war = {}

    def mm_group(bank, mms, waits=(), tag=""):
        w = list(waits)
        if bank in bank_war and bank_war[bank]:
            w.append(bank_war[bank])
        last = None
        n = len(mms)
        for i, m in enumerate(mms):
            lhsT, rhs = m[0], m[1]
            out = m[2] if len(m) > 2 else pb[bank].ap()
            st, sp = (i == 0), (i == n - 1)
            def f(e, out=out, lhsT=lhsT, rhs=rhs, st=st, sp=sp):
                return e.matmul(out, lhsT, rhs, start=st, stop=sp)
            last = p_.add("tensor", f, waits=(w if i == 0 else ()),
                          inc=("sPE" if sp else None), tag=tag)
        return last

    def tr_group(bank, trs, waits=(), dtype=BF16, tag=""):
        w = list(waits)
        if bank in bank_war and bank_war[bank]:
            w.append(bank_war[bank])
        ident = identb[:] if dtype == BF16 else identf[:]
        last = None
        for i, (out, in_) in enumerate(trs):
            def f(e, out=out, in_=in_, ident=ident):
                return e.transpose(out, in_, ident)
            last = p_.add("tensor", f, waits=(w if i == 0 else ()),
                          inc=("sPE" if i == len(trs) - 1 else None), tag=tag)
        return last

    def evac(bank, eng, fn, waits=(), tag=""):
        sem = p_.eng_sem[eng]
        mile = p_.add(eng, fn, waits=waits, inc=sem, tag=tag)
        bank_war[bank] = mile
        return mile

    def dv(fn, waits=(), tag="", inc=True):
        return p_.add("vector", fn, waits=waits, inc=("sDV" if inc else None), tag=tag)

    def ac(fn, waits=(), tag="", inc=True):
        return p_.add("scalar", fn, waits=waits, inc=("sAC" if inc else None), tag=tag)

    def gp(fn, waits=(), tag="", inc=True):
        return p_.add("gpsimd", fn, waits=waits, inc=("sGP" if inc else None), tag=tag)

    def sy_dma(out, in_, waits=(), store=False, tag=""):
        waits = list(waits) + [p_.cur("sXD"), p_.cur("sST")]
        def f(e, out=out, in_=in_):
            return e.dma_start(out=out, in_=in_)
        return p_.add("sync", f, waits=waits, inc=("sST" if store else "sXD"),
                      n=16, tag=tag)

    def hard_barrier(tag=""):
        arr = {}
        for e in ("tensor", "vector", "scalar", "gpsimd"):
            s = p_.eng_sem[e]
            def mk(s=s):
                return lambda en: en.sem_inc(sems[s], 1)
            arr[e] = p_.add(e, mk(), inc=s, tag=f"{tag}barA", attach=False)
        dmaw = [p_.cur("sGD"), p_.cur("sXD"), p_.cur("sST")]
        allw = list(arr.values()) + dmaw
        for e in ("tensor", "vector", "scalar", "gpsimd", "sync"):
            p_.add(e, (lambda en: en.wait_ge(sems["sGP"], 1)), waits=allw,
                   tag=f"{tag}barW")

    def sc_dma(out, in_, waits=(), tag=""):
        waits = list(waits) + [p_.cur("sGD")]
        def f(e, out=out, in_=in_):
            return e.dma_start(out=out, in_=in_)
        return p_.add("scalar", f, waits=waits, inc="sGD", n=16, tag=tag)

    def gp_dma(out, in_, waits=(), tag=""):
        waits = list(waits) + [p_.cur("sGD")]
        def f(e, out=out, in_=in_):
            return e.dma_start(out=out, in_=in_)
        return p_.add("gpsimd", f, waits=waits, inc="sGD", n=16, tag=tag)

    sy_dma(smalls[:], smalls_d, tag="ld_smalls")
    sm_ready = p_.cur("sXD")

    m1 = gp(lambda e: e.memset(identb[:], 0.0), tag="identb0")
    gp(lambda e: e.affine_select(out=identb[:], in_=identb[:],
                                 compare_op=ALU.not_equal, fill=1.0, base=0,
                                 pattern=[[-1, P]], channel_multiplier=1),
       waits=[m1], tag="identb")
    mt = gp(lambda e: e.memset(tril01[:], 1.0), tag="tril0")
    gp(lambda e: e.affine_select(out=tril01[:], in_=tril01[:],
                                 compare_op=ALU.is_ge, fill=0.0, base=0,
                                 pattern=[[-1, P]], channel_multiplier=1),
       waits=[mt], tag="tril")
    m2 = gp(lambda e: e.memset(identf[:], 0.0), tag="identf0")
    gp(lambda e: e.affine_select(out=identf[:], in_=identf[:],
                                 compare_op=ALU.not_equal, fill=1.0, base=0,
                                 pattern=[[-1, P]], channel_multiplier=1),
       waits=[m2], tag="identf")
    gp(lambda e: e.memset(onescol[:], 1.0), tag="ones")
    gp(lambda e: e.memset(epscol[:], LN_EPS), tag="eps")
    gp(lambda e: e.memset(bepscol[:], B_EPS), tag="beps")
    gp(lambda e: e.memset(xn_fm[:], 0.0), tag="fmz")
    init_gp = p_.cur("sGP")

    dv(lambda e: e.tensor_scalar(out=dsm[:, 0, :], in0=smalls[:, 2, :],
                                 scalar1=-1.0, scalar2=1.0, op0=ALU.mult,
                                 op1=ALU.add), waits=[sm_ready], tag="onemur")
    dv(lambda e: e.tensor_scalar(out=dsm[:, 1, :], in0=smalls[:, 3, :],
                                 scalar1=-1.0, scalar2=1.0, op0=ALU.mult,
                                 op1=ALU.add), tag="onemuk")
    ac(lambda e: e.activation(out=dsm[:, 2, :], in_=smalls[:, 7, :],
                              func=AF.Sigmoid), waits=[sm_ready], tag="gate")
    gate_done = p_.cur("sAC")
    dv(lambda e: e.tensor_scalar(out=dsm[:, 3, :], in0=dsm[:, 2, :],
                                 scalar1=-1.0, scalar2=1.0, op0=ALU.mult,
                                 op1=ALU.add), waits=[gate_done], tag="onegate")
    dv(lambda e: e.tensor_scalar(out=dsm[:, 6, :], in0=smalls[:, 4, :],
                                 scalar1=-1.0, scalar2=None, op0=ALU.mult),
       tag="negbq")
    dv(lambda e: e.tensor_scalar(out=dsm[:, 7, :], in0=smalls[:, 5, :],
                                 scalar1=-1.0, scalar2=None, op0=ALU.mult),
       tag="negbk")
    init_dv = p_.cur("sDV")
    init_ac = p_.cur("sAC")

    armed = []

    class LoadW:
        def __init__(self, dram_ap, slot, q=0, tag=""):
            self.dram_ap, self.slot, self.q, self.tag = dram_ap, slot, q, tag
            self.c = NC - 1
            self.gate = []
            self.last = None
            self.mile = None

        def step(self):
            if self.c < 0:
                return
            c = self.c
            self.c -= 1
            dma = sy_dma if self.q == 0 else sc_dma
            wf = warena.ap()[:, self.slot, :, :].rearrange(
                "p c e -> p (c e)").bitcast(F32)
            stage = wf[:, 0:1024] if c >= 2 else wst0.ap()
            w = list(self.gate) + ([self.last] if self.last else [])
            if c < 2:
                w = w + buf_war.get("wst", [])
            ld = dma(stage, self.dram_ap[128 * c:128 * (c + 1), :], waits=w,
                     tag=f"{self.tag}c{c}")
            self.last = ac(lambda e, slot=self.slot, c=c, stage=stage:
                           e.activation(out=warena[:, slot, c, :], in_=stage,
                                        func=AF.Identity),
                           waits=[ld], tag=f"{self.tag}x{c}")
            if c < 2:
                buf_war["wst"] = [self.last]
            if self.c < 0:
                self.mile = self.last

        def fin(self):
            while self.c >= 0:
                self.step()
            if self in armed:
                armed.remove(self)
            return self.mile

    def arm(ldr, gate):
        ldr.gate = list(gate)
        armed.append(ldr)

    def wstep():
        for ldr in list(armed):
            ldr.step()
            if ldr.c < 0:
                armed.remove(ldr)

    slot_free_gate = lambda: p_.barrier_waits("sPE", "sAC", "sDV")

    NSTORE = 0
    carry_qa = carry_ka = None
    stats_done = {}
    stats_guard = {}

    for b in range(BL):
        if n_sub >= 4 and b > 0:
            x_ready = []
        else:
            xw = [p_.cur("sST")]
            x_mile = sy_dma(x_tm[:], x_d[b].rearrange("(t p) d -> p t d", p=P),
                            waits=xw, tag=f"ldx{b}")
            x_ready = [x_mile]

        def stats_tile(i, waits, tag):
            w = list(waits) + stats_guard.get(i // 4, [])
            s1m = dv(lambda e, i=i: e.tensor_reduce(
                stats[:, i, 0:1], x_tm[:, i, :], axis=AX.X, op=ALU.add),
                waits=w, tag=f"{tag}S1_{i}")
            s2m = ac(lambda e, i=i: e.activation(
                out=junkb.ap(), in_=x_tm[:, i, :], func=AF.Square,
                accum_out=stats[:, i, 1:2]),
                waits=w + buf_war.get("junkb", []), tag=f"{tag}S2_{i}")
            buf_war["junkb"] = [s2m]
            stats_done[i] = (s1m, s2m)

        def ln_stats(xpx, pre_waits, tag):
            for i in range(NT):
                if i not in stats_done:
                    stats_tile(i, pre_waits + [sm_ready], tag)
            mrns, oprs = [], []
            for g4 in range(4):
                sl = slice(4 * g4, 4 * g4 + 4)
                sw = [m for i in range(4 * g4, 4 * g4 + 4)
                      for m in stats_done[i]]
                m1 = dv(lambda e, sl=sl: e.tensor_scalar(
                    out=stats[:, sl, 2:3], in0=stats[:, sl, 0:1],
                    scalar1=1.0 / D, scalar2=None, op0=ALU.mult),
                    waits=sw, tag=f"{tag}m{g4}")
                dv(lambda e, sl=sl: e.tensor_scalar(
                    out=stats[:, sl, 4:5], in0=stats[:, sl, 1:2],
                    scalar1=1.0 / D, scalar2=None, op0=ALU.mult),
                    tag=f"{tag}s2n{g4}")
                s2 = dv(lambda e, sl=sl: e.tensor_tensor(
                    out=stats[:, sl, 3:4], in0=stats[:, sl, 2:3],
                    in1=stats[:, sl, 2:3], op=ALU.mult),
                    waits=[m1], tag=f"{tag}m2{g4}")
                var = dv(lambda e, sl=sl: e.tensor_tensor(
                    out=stats[:, sl, 3:4], in0=stats[:, sl, 4:5],
                    in1=stats[:, sl, 3:4], op=ALU.subtract),
                    waits=[s2], tag=f"{tag}var{g4}")
                lnv = ac(lambda e, sl=sl: e.activation(
                    out=stats[:, sl, 4:5], in_=stats[:, sl, 3:4], func=AF.Ln,
                    bias=epscol[:, 0:1]),
                    waits=[var, init_gp], tag=f"{tag}ln{g4}")
                rstd = ac(lambda e, sl=sl: e.activation(
                    out=stats[:, sl, 5:6], in_=stats[:, sl, 4:5], func=AF.Exp,
                    scale=-0.5),
                    waits=[lnv], tag=f"{tag}rstd{g4}")
                mrn = dv(lambda e, sl=sl: e.scalar_tensor_tensor(
                    out=stats[:, sl, 6:7], in0=stats[:, sl, 2:3],
                    scalar=-1.0, in1=stats[:, sl, 5:6],
                    op0=ALU.mult, op1=ALU.mult),
                    waits=[rstd], tag=f"{tag}mrn{g4}")
                opr = None
                if xpx:
                    opr = dv(lambda e, sl=sl: e.tensor_scalar(
                        out=stats[:, sl, 7:8], in0=stats[:, sl, 5:6],
                        scalar1=1.0, scalar2=None, op0=ALU.add),
                        waits=[rstd], tag=f"{tag}opr{g4}")
                settle = dv(lambda e, sl=sl: e.tensor_scalar(
                    out=stats[:, sl, 4:5], in0=stats[:, sl, 6:7],
                    scalar1=0.0, scalar2=None, op0=ALU.add),
                    waits=[mrn] + ([opr] if opr else []),
                    tag=f"{tag}settle{g4}")
                stats_guard[g4] = [settle]
                mrns.append(settle)
                oprs.append(settle if xpx else None)
            stats_done.clear()
            return mrns, oprs

        def ln_apply(i, dst, mrn, extra_waits, tag):
            if i % 2 == 0:
                return ac(lambda e, i=i, dst=dst: e.activation(
                    out=xn_tm[:, dst, :], in_=x_tm[:, i, :], func=AF.Identity,
                    scale=stats[:, i, 5:6], bias=stats[:, i, 6:7]),
                    waits=[mrn] + list(extra_waits), tag=f"{tag}xn{i}")
            return dv(lambda e, i=i, dst=dst: e.tensor_scalar(
                out=xn_tm[:, dst, :], in0=x_tm[:, i, :],
                scalar1=stats[:, i, 5:6], scalar2=stats[:, i, 6:7],
                op0=ALU.mult, op1=ALU.add),
                waits=[mrn] + list(extra_waits), tag=f"{tag}xn{i}")

        def ln_xpx(i, opr, apl, tag):
            if i % 2 == 1:
                return ac(lambda e, i=i: e.activation(
                    out=x_tm[:, i, :], in_=x_tm[:, i, :], func=AF.Identity,
                    scale=stats[:, i, 7:8], bias=stats[:, i, 6:7]),
                    waits=[opr, apl], tag=f"{tag}xpx{i}")
            return dv(lambda e, i=i: e.tensor_scalar(
                out=x_tm[:, i, :], in0=x_tm[:, i, :],
                scalar1=stats[:, i, 7:8], scalar2=stats[:, i, 6:7],
                op0=ALU.mult, op1=ALU.add),
                waits=[opr, apl], tag=f"{tag}xpx{i}")

        def fm_transpose_group(g, buf, src_tiles, xn_m, tag, guard_from=None):
            miles = []
            tm = None
            for half in range(4):
                dc0 = 2 * half
                bnk = 6 + (half % 2)
                trs = []
                for dci in range(2):
                    dc = dc0 + dci
                    for t in range(4):
                        out = pbb[bnk][:, 512 * dci + 128 * t:
                                       512 * dci + 128 * t + 128]
                        in_ = xn_tm[:, src_tiles[t], 128 * dc:128 * dc + 128]
                        trs.append((out, in_))
                w = list(xn_m)
                tm = tr_group(bnk, trs, waits=w, tag=f"{tag}tr{half}")
                eng = "vector" if half % 2 == 0 else "scalar"
                if eng == "vector":
                    def f(e, bnk=bnk, buf=buf, dc0=dc0):
                        return e.tensor_copy(xn_fm[:, buf, dc0:dc0 + 2, 4:516],
                                             pbb[bnk][:, 0:1024])
                else:
                    def f(e, bnk=bnk, buf=buf, dc0=dc0):
                        return e.copy(xn_fm[:, buf, dc0:dc0 + 2, 4:516],
                                      pbb[bnk][:, 0:1024])
                miles.append(evac(bnk, eng, f, waits=[tm], tag=f"{tag}ev{half}"))
            return miles, tm

        aw_pre = p_.barrier_waits("sPE", "sAC")
        mrns, oprs = ln_stats(xpx=True,
                              pre_waits=x_ready + [p_.cur("sDV"),
                                                   p_.cur("sAC")],
                              tag=f"b{b}ln1_")
        if carry_qa is None:
            LQA = LoadW(wqaT_d, 0, 0, f"b{b}ldwqa")
            LKA = LoadW(wkaT_d, 1, 1, f"b{b}ldwka")
            arm(LQA, slot_free_gate() if b > 0 else [])
            arm(LKA, slot_free_gate() if b > 0 else [])
        else:
            LQA, LKA = carry_qa, carry_ka
        carry_qa = carry_ka = None
        if n_sub >= 2:
            f1_ldrs = (LoadW(wrT_d, 0, 0, f"b{b}f1_ldwr"),
                       LoadW(wkfT_d, 1, 1, f"b{b}f1_ldwk"),
                       LoadW(wvfT_d, 2, 0, f"b{b}f1_ldwv"))
        else:
            f1_ldrs = None
        w_ka = LKA.fin()
        xn_m = [ln_apply(i, i, mrns[i // 4], [], tag=f"b{b}ln1_")
                for i in range(NT)]
        xpx_all = [ln_xpx(i, oprs[i // 4], xn_m[i], tag=f"b{b}ln1_")
                   for i in range(NT)]
        xpx_m = xpx_all[-2:]

        kproj_m = []
        for g in range(NG):
            fm_m, _ = fm_transpose_group(g, g % 2, [4 * g + t for t in range(4)],
                                         xn_m[4 * g + 2:4 * g + 4],
                                         tag=f"b{b}a1g{g}")
            for ec in range(NC):
                bnk = ec % 2
                mms = [(warena[:, 1, dc, 128 * ec:128 * ec + 128],
                        xn_fm[:, g % 2, dc, 4:516]) for dc in range(NC)]
                gm = mm_group(bnk, mms, waits=fm_m + [w_ka], tag=f"b{b}K{g}_{ec}")
                def f(e, bnk=bnk, ec=ec, g=g):
                    return e.activation(out=K_fm[:, ec, 512 * g:512 * g + 512],
                                        in_=pb[bnk].ap(), func=AF.Identity,
                                        bias=smalls[:, 1, ec:ec + 1])
                kproj_m.append(evac(bnk, "scalar", f, waits=[gm],
                                    tag=f"b{b}Kev{g}_{ec}"))

        if f1_ldrs:
            arm(f1_ldrs[1], slot_free_gate())
        cs_m = []
        for dh in range(2):
            mms = [(onescol[:, 0:1], xn_tm[:, kc, 512 * dh:512 * dh + 512],
                    pb[6].ap()[0:1, :]) for kc in range(NT)]
            gm = mm_group(6, mms, waits=xn_m[-2:] + [init_gp],
                          tag=f"b{b}cs{dh}")
            def f(e, dh=dh):
                return e.tensor_copy(colsum[0:1, 512 * dh:512 * dh + 512],
                                     pb[6].ap()[0:1, :])
            cs_m.append(evac(6, "vector", f, waits=[gm], tag=f"b{b}csev{dh}"))

        k_all = kproj_m[-1]

        def emit_scores(c, c4, qg_m):
            eb = c % 2
            zp = 4 * (c % 2)
            exp_m = []
            for kb in range(4):
                bnk = kb % 2
                mms = [(warena[:, 2, ec, 128 * c4:128 * c4 + 128],
                        K_fm[:, ec, 512 * kb:512 * kb + 512])
                       for ec in range(NC)]
                gm = mm_group(bnk, mms,
                              waits=qg_m + [k_all] + buf_war.get(
                                  ("ebuf", eb), []),
                              tag=f"b{b}s{c}_{kb}")
                def f(e, bnk=bnk, eb=eb, kb=kb, zp=zp):
                    return e.activation(
                        out=ebuf[:, eb, 512 * kb:512 * kb + 512],
                        in_=pb[bnk].ap(), func=AF.Exp, scale=1.0 / 32.0,
                        accum_out=zpart[:, zp + kb:zp + kb + 1])
                exp_m.append(evac(bnk, "scalar", f,
                                  waits=[gm] + buf_war.get(("zpart", eb), []),
                                  tag=f"b{b}e{c}_{kb}"))
            return exp_m

        def emit_tail(c, exp_m):
            eb = c % 2
            zp = 4 * (c % 2)
            zr = dv(lambda e, c=c, zp=zp: e.tensor_reduce(
                zbuf[:, c, 0:1], zpart[:, zp:zp + 4], axis=AX.X, op=ALU.add),
                waits=[exp_m[-1]], tag=f"b{b}Z{c}")
            buf_war[("zpart", eb)] = [zr]
            rz = dv(lambda e, c=c: e.reciprocal(zbuf[:, c, 1:2],
                                                zbuf[:, c, 0:1]),
                    waits=[zr], tag=f"b{b}rZ{c}")
            nz = dv(lambda e, c=c, eb=eb: e.tensor_scalar(
                out=nzcol[:, eb:eb + 1], in0=zbuf[:, c, 0:1],
                scalar1=-2.0 / S, scalar2=None, op0=ALU.mult),
                waits=[zr] + buf_war.get(("nzcol", eb), []),
                tag=f"b{b}nz{c}")
            ztr = tr_group(6, [(pbb[6][0:1, 0:P], nzcol[:, eb:eb + 1])],
                           waits=[nz], tag=f"b{b}ztr{c}")
            buf_war[("nzcol", eb)] = [ztr]
            zev = evac(6, "vector",
                       lambda e, eb=eb: e.tensor_copy(zrow[0:1, eb, :],
                                                      pbb[6][0:1, 0:P]),
                       waits=[ztr] + buf_war.get(("zrow", eb), []),
                       tag=f"b{b}zev{c}")
            pt_m = []
            for half in range(2):
                bnk = 2 + half
                trs = []
                for kci in range(8):
                    kc = 8 * half + kci
                    trs.append((pbb[bnk][:, 128 * kci:128 * kci + 128],
                                ebuf[:, eb, 128 * kc:128 * kc + 128]))
                tm = tr_group(bnk, trs, waits=[exp_m[-1]],
                              tag=f"b{b}pt{c}_{half}")
                def f(e, bnk=bnk, half=half):
                    return e.tensor_copy(ptbuf[:, 8 * half:8 * half + 8, :],
                                         pbb[bnk][:, 0:1024])
                pt_m.append(evac(bnk, "vector", f, waits=[tm],
                                 tag=f"b{b}ptev{c}_{half}"))
            buf_war[("ebuf", eb)] = [pt_m[-1]]
            for dh in range(2):
                bnk = 4 + dh
                mms = [(ptbuf[:, kc, :],
                        xn_tm[:, kc, 512 * dh:512 * dh + 512])
                       for kc in range(NT)]
                mms.append((zrow[0:1, eb, :],
                            colsum[0:1, 512 * dh:512 * dh + 512]))
                gm = mm_group(bnk, mms,
                              waits=pt_m + [zev] + cs_m, tag=f"b{b}pv{c}_{dh}")
                buf_war[("zrow", eb)] = [gm]
                def f(e, bnk=bnk, c=c, dh=dh):
                    return e.scalar_tensor_tensor(
                        out=x_tm[:, c, 512 * dh:512 * dh + 512],
                        in0=pb[bnk].ap(), scalar=zbuf[:, c, 1:2],
                        in1=x_tm[:, c, 512 * dh:512 * dh + 512],
                        op0=ALU.mult, op1=ALU.add)
                pv_ev = evac(bnk, "vector", f, waits=[gm, rz] + xpx_m,
                             tag=f"b{b}pvev{c}_{dh}")
            if n_sub >= 2:
                stats_tile(c, [pv_ev], f"b{b}f1_")

        pend = None
        w_qa = LQA.fin()
        for gi, g in enumerate([2, 3, 0, 1]):
            if gi >= 2:
                fm_m, _ = fm_transpose_group(g, g % 2,
                                             [4 * g + t for t in range(4)],
                                             xn_m[4 * g + 2:4 * g + 4],
                                             tag=f"b{b}a2g{g}")
            else:
                fm_m = []
            qg_m = []
            for ec in range(NC):
                bnk = ec % 2
                mms = [(warena[:, 0, dc, 128 * ec:128 * ec + 128],
                        xn_fm[:, g % 2, dc, 4:516]) for dc in range(NC)]
                gm = mm_group(bnk, mms, waits=fm_m + [w_qa], tag=f"b{b}Q{g}_{ec}")
                def f(e, bnk=bnk, ec=ec):
                    return e.activation(out=warena[:, 2, ec, 0:512],
                                        in_=pb[bnk].ap(), func=AF.Identity,
                                        bias=smalls[:, 0, ec:ec + 1])
                qg_m.append(evac(bnk, "scalar", f, waits=[gm],
                                 tag=f"b{b}Qev{g}_{ec}"))
            if gi == 3 and f1_ldrs:
                arm(f1_ldrs[0], slot_free_gate())
            for c4 in range(4):
                c = 4 * g + c4
                wstep()
                em = emit_scores(c, c4, qg_m)
                if pend is not None:
                    emit_tail(*pend)
                pend = (c, em)
        if f1_ldrs:
            arm(f1_ldrs[2], slot_free_gate())
        emit_tail(*pend)
        wstep()

        def emit_ffn(tag, ldrs, nxt, tile_hook=None, dbg_g0=False):
            if ffn_level < 0:
                return
            import os as _os0
            if _os0.environ.get("FFN_PRE2"):
                pre = p_.barrier_waits("sPE", "sAC")
            else:
                pre = p_.barrier_waits("sDV", "sAC", "sPE", "sGP", "sGD")

            xrxk = a3.ap()[:, 0:16384].bitcast(BF16).rearrange(
                "p (u c s) -> p u c s", u=2, c=NC)
            rc_fm = a3.ap()[:, 16384:32768].bitcast(F32).rearrange(
                "p (c s) -> p c s", c=NC)
            r_fm = xn_tm.ap()[:, 8:16, :].bitcast(F32)
            relu_b = ebuf.ap().rearrange("p u s -> p (u s)").rearrange(
                "p (c s) -> p c s", c=NC)

            if ffn_level == 0:
                return
            mrns2, _ = ln_stats(xpx=False, pre_waits=pre, tag=f"{tag}ln")
            w_r = ldrs[0].fin()
            w_k = ldrs[1].fin()
            w_v = ldrs[2].fin()
            gz = dv(lambda e: e.memset(xn_fm[:, 0, :, 3:4], 0.0), waits=pre,
                    tag=f"{tag}gz")
            guard_m = {0: gz}
            st_ = {"tr_prev": None, "mix_prev": None, "ab_prev": None,
                   "c_prev": None, "rc_prev": None}
            gstate = {}

            def s12(g):
                buf = g % 2
                aw = ([st_["tr_prev"]] if st_["tr_prev"] else [])
                xn_mg = [ln_apply(4 * g + t, t, mrns2[g], aw,
                                  tag=f"{tag}g{g}") for t in range(4)]
                fm_m, tr_last = fm_transpose_group(
                    g, buf, [0, 1, 2, 3], xn_mg[-2:], tag=f"{tag}g{g}")
                st_["tr_prev"] = tr_last
                if g + 1 < NG:
                    gc = dv(lambda e, buf=buf: e.tensor_copy(
                        xn_fm[:, 1 - buf, :, 3:4], xn_fm[:, buf, :, 515:516]),
                        waits=fm_m + ([st_["mix_prev"]] if st_["mix_prev"] else []),
                        tag=f"{tag}gc{g}")
                    guard_m[g + 1] = gc
                mw = fm_m + [guard_m[g]] + ([st_["ab_prev"]] if st_["ab_prev"] else [])
                mix_m = None
                for dc in range(NC):
                    wstep()
                    for u, mucol, omcol in ((0, 2, 0), (1, 3, 1)):
                        tsm = dv(lambda e, dc=dc, u=u, omcol=omcol, buf=buf:
                           e.tensor_scalar(
                               out=xrxk[:, u, dc, :],
                               in0=xn_fm[:, buf, dc, 3:515],
                               scalar1=dsm[:, omcol, dc:dc + 1], scalar2=None,
                               op0=ALU.mult),
                           waits=mw, tag=f"{tag}mx{g}_{dc}_{u}")
                        mix_m = dv(lambda e, dc=dc, u=u, mucol=mucol, buf=buf:
                                   e.scalar_tensor_tensor(
                                       out=xrxk[:, u, dc, :],
                                       in0=xn_fm[:, buf, dc, 4:516],
                                       scalar=smalls[:, mucol, dc:dc + 1],
                                       in1=xrxk[:, u, dc, :],
                                       op0=ALU.mult, op1=ALU.add),
                                   waits=[tsm], tag=f"{tag}mxs{g}_{dc}_{u}")
                st_["mix_prev"] = mix_m
                gstate[g] = {"mix_m": mix_m}

            def s3(g):
                gs = gstate[g]
                sig_m = []
                ab_m = None
                for ec in range(NC):
                    wstep()
                    bnk = ec % 2
                    mms = [(warena[:, 0, dc, 128 * ec:128 * ec + 128],
                            xrxk[:, 0, dc, :]) for dc in range(NC)]
                    gm = mm_group(bnk, mms, waits=[gs["mix_m"], w_r],
                                  tag=f"{tag}A{g}_{ec}")
                    sig_m.append(evac(bnk, "scalar",
                                      lambda e, bnk=bnk, ec=ec: e.activation(
                                          out=r_fm[:, ec, :], in_=pb[bnk].ap(),
                                          func=AF.Sigmoid),
                                      waits=[gm] + ([st_["rc_prev"]] if st_["rc_prev"] else []),
                                      tag=f"{tag}Aev{g}_{ec}"))
                for ec in range(NC):
                    bnk = ec % 2
                    mms = [(warena[:, 1, dc, 128 * ec:128 * ec + 128],
                            xrxk[:, 1, dc, :]) for dc in range(NC)]
                    gm = mm_group(bnk, mms, waits=[w_k], tag=f"{tag}B{g}_{ec}")
                    ab_m = gm
                    evac(bnk, "scalar",
                         lambda e, bnk=bnk, ec=ec: e.activation(
                             out=relu_b[:, ec, :], in_=pb[bnk].ap(),
                             func=AF.Relu),
                         waits=[gm] + ([st_["c_prev"]] if st_["c_prev"] else []),
                         tag=f"{tag}Bev{g}_{ec}")
                st_["ab_prev"] = ab_m
                if g == NG - 1 and nxt:
                    arm(nxt[0], slot_free_gate())
                    arm(nxt[1], slot_free_gate())
                relu_m = p_.cur("sAC")
                k2_m = dv(lambda e: e.tensor_tensor(
                    out=relu_b[:, :, :], in0=relu_b[:, :, :],
                    in1=relu_b[:, :, :], op=ALU.mult),
                    waits=[relu_m], tag=f"{tag}k2{g}")
                gs["k2_m"] = k2_m
                gs["sig_m"] = sig_m

            def s4(g):
                gs = gstate[g]
                rc_m = []
                cm = None
                for ec in range(NC):
                    wstep()
                    bnk = ec % 2
                    mms = [(warena[:, 2, dc, 128 * ec:128 * ec + 128],
                            relu_b[:, dc, :]) for dc in range(NC)]
                    gm = mm_group(bnk, mms, waits=[gs["k2_m"], w_v],
                                  tag=f"{tag}C{g}_{ec}")
                    cm = gm
                    rc_m.append(evac(bnk, "vector",
                                     lambda e, bnk=bnk, ec=ec: e.tensor_tensor(
                                         out=rc_fm[:, ec, :], in0=pb[bnk].ap(),
                                         in1=r_fm[:, ec, :], op=ALU.mult),
                                     waits=[gm, gs["sig_m"][ec]],
                                     tag=f"{tag}Cev{g}_{ec}"))
                st_["c_prev"] = cm
                st_["rc_prev"] = rc_m[-1]
                if g == NG - 1 and nxt and nxt[2]:
                    arm(nxt[2], slot_free_gate())
                for t in range(4):
                    ra_ev = None
                    for half in range(2):
                        bnk = 2 + half
                        trs = []
                        for eci in range(4):
                            ec = 4 * half + eci
                            trs.append((pb[bnk].ap()[:, 128 * eci:128 * eci + 128],
                                        rc_fm[:, ec, 128 * t:128 * t + 128]))
                        tm = tr_group(bnk, trs, waits=[rc_m[-1]], dtype=F32,
                                      tag=f"{tag}rt{g}_{t}_{half}")
                        def f(e, bnk=bnk, g=g, t=t, half=half):
                            return e.tensor_tensor(
                                out=x_tm[:, 4 * g + t, 512 * half:512 * half + 512],
                                in0=pb[bnk].ap(),
                                in1=x_tm[:, 4 * g + t, 512 * half:512 * half + 512],
                                op=ALU.add)
                        ra_ev = evac(bnk, "vector", f, waits=[tm],
                                     tag=f"{tag}radd{g}_{t}_{half}")
                    if tile_hook is not None:
                        tile_hook(g, t, ra_ev)

            s12(0); s3(0)
            for g in range(1, NG):
                s12(g)
                s4(g - 1)
                s3(g)
            s4(NG - 1)

        if False and dbg and b == DBG_SEQ:
            dwv = p_.barrier_waits("sDV", "sAC", "sPE")
            def cast_dump(name, src_ap, waits):
                gp_dma(dbg_d[name], src_ap, waits=waits, tag=f"dbg{name}")
            cast_dump("stats", stats[:], dwv)
            cast_dump("xn", xn_tm[:], dwv)
            cast_dump("xpx", x_tm[:], dwv)
            cast_dump("K", K_fm, dwv)
            cast_dump("csum", colsum[:], dwv)
            cast_dump("zbuf", zbuf[:], dwv)
            cast_dump("E8", ebuf[:, 0, :], dwv)
            cast_dump("q2", warena[:, 2, :, 0:512], dwv)

        def emit_attn_b(tag, ldrs, nxt):
            pre = p_.barrier_waits("sDV", "sAC", "sPE", "sGP", "sGD")
            mrns3, _ = ln_stats(xpx=False, pre_waits=pre, tag=f"{tag}ln")
            w_qb = ldrs[0].fin()
            w_kb = ldrs[1].fin()
            w_vb = ldrs[2].fin()
            a3b = a3.ap().bitcast(BF16)
            a3f = a3.ap().bitcast(F32)
            sqkv = a3b[:, 0:12288].rearrange("p (u c s) -> p u c s",
                                             u=3, c=NC)
            Mf = a3f[:, 6144:7168].rearrange("p (c s) -> p c s", c=NC)
            Maug = a3b[:, 14336:15392].rearrange("p (c s) -> p c s", c=NC)
            zf = a3f[:, 7696:7704]
            ztmp = a3f[:, 7704:7712]
            dsc = a3f[:, 7712:7760]
            vd_b = xn_tm.ap()[:, 12, 0:128]
            tmpf = xn_tm.ap()[:, 12, :].bitcast(F32)[:, 128:256]
            adbuf_a = xn_tm.ap()[:, 13, :].bitcast(F32)
            adbuf_b = ebuf.ap()[:, 1, :].bitcast(F32)[:, 0:512]
            t1f = xn_tm.ap()[:, 14, :].bitcast(F32)
            t2f = xn_tm.ap()[:, 15, :].bitcast(F32)
            Ebuf = ebuf.ap()[:, 0, :]

            mi1 = dv(lambda e: e.memset(Mf, 0.0), waits=pre, tag=f"{tag}M0")
            mi2 = dv(lambda e: e.memset(Maug, 0.0), waits=pre, tag=f"{tag}Ma0")
            mi3 = dv(lambda e: e.memset(zf, 0.0), waits=pre, tag=f"{tag}z0")
            maug_ready = [mi1, mi2, mi3]

            tr_prev = None
            proj_war = []
            for G in range(NSEGS):
                buf = G % 2
                aw = ([tr_prev] if tr_prev else [])
                xn_mg = [ln_apply(4 * G + t, t, mrns3[G], aw,
                                  tag=f"{tag}g{G}") for t in range(4)]
                fm_m, tr_last = fm_transpose_group(G, buf, [0, 1, 2, 3],
                                                   xn_mg[-2:], tag=f"{tag}g{G}")
                tr_prev = tr_last
                ev_m = []
                for u, wm, bcol in ((0, w_qb, 4), (1, w_kb, 5), (2, w_vb, 6)):
                    for ec in range(NC):
                        bnk = ec % 2
                        mms = [(warena[:, u, dc, 128 * ec:128 * ec + 128],
                                xn_fm[:, buf, dc, 4:516]) for dc in range(NC)]
                        gm = mm_group(bnk, mms, waits=fm_m + [wm] + proj_war,
                                      tag=f"{tag}p{G}u{u}e{ec}")
                        ev_m.append(evac(bnk, "scalar",
                                    lambda e, bnk=bnk, u=u, ec=ec, bcol=bcol:
                                    e.activation(out=sqkv[:, u, ec, :],
                                                 in_=pb[bnk].ap(),
                                                 func=AF.Identity,
                                                 bias=smalls[:, bcol, ec:ec + 1]),
                                    waits=[gm], tag=f"{tag}pe{G}{u}{ec}"))
                ev_last = [ev_m[-1]]
                if G == NSEGS - 1 and nxt:
                    for L_ in nxt:
                        if L_ is not None:
                            arm(L_, slot_free_gate())
                vt_m = []
                for t in range(4):
                    bnk = 2 + t % 2
                    trs = [(pbb[bnk][:, 128 * ec:128 * ec + 128],
                            sqkv[:, 2, ec, 128 * t:128 * t + 128])
                           for ec in range(NC)]
                    tm_ = tr_group(bnk, trs, waits=ev_last,
                                   tag=f"{tag}vt{G}_{t}")
                    def f(e, bnk=bnk, t=t):
                        return e.tensor_copy(xn_tm[:, 4 + t, :],
                                             pbb[bnk][:, 0:1024])
                    vt_m.append(evac(bnk, "vector", f, waits=[tm_],
                                     tag=f"{tag}vte{G}_{t}"))
                vt_last = vt_m[-1]
                num_read_m = []
                mdelta_first = {6: True, 7: True}
                mdelta_last_m = None
                def passA_g(h):
                    omgcol = dsm[:, 3, h:h + 1]
                    adbuf = adbuf_a if h % 2 == 0 else adbuf_b
                    z0m = dv(lambda e: e.memset(dsc[:, 36:37], 0.0),
                             waits=buf_war.get("zb1", []),
                             tag=f"{tag}z0_{G}_{h}")
                    e1 = z2m = None
                    adm_l, pte_l = [], []
                    for lc in range(4):
                        par = lc % 4
                        Eslot = Ebuf[:, 512 * par:512 * (par + 1)]
                        sb_ = (h + lc) % 2
                        ncols = 128 * (lc + 1)
                        gm = mm_group(sb_, [(sqkv[:, 0, h, 128 * lc:128 * lc + 128],
                                             sqkv[:, 1, h, 0:ncols],
                                             pb[sb_].ap()[:, 0:ncols])],
                                      waits=ev_last + buf_war.get(("E", par), []),
                                      tag=f"{tag}s{G}_{h}_{lc}")
                        if lc > 0:
                            e1 = ac(lambda e, sb_=sb_, lc=lc, Eslot=Eslot:
                                    e.activation(out=Eslot[:, 0:128 * lc],
                                                 in_=pb[sb_].ap()[:, 0:128 * lc],
                                                 func=AF.Exp,
                                                 scale=1.0 / np.sqrt(DK),
                                                 accum_out=dsc[:, 36 + lc:37 + lc]),
                                    waits=[gm] + buf_war.get("zb1", []),
                                    tag=f"{tag}e1_{G}_{h}_{lc}")
                        e2 = ac(lambda e, sb_=sb_, lc=lc, Eslot=Eslot:
                                e.activation(out=Eslot[:, 128 * lc:128 * lc + 128],
                                             in_=pb[sb_].ap()[:, 128 * lc:128 * lc + 128],
                                             func=AF.Exp, scale=1.0 / np.sqrt(DK)),
                                waits=([e1] if e1 else [gm]),
                                tag=f"{tag}e2_{G}_{h}_{lc}")
                        bank_war[sb_] = e2
                        mk = dv(lambda e, lc=lc, Eslot=Eslot: e.tensor_tensor(
                            out=Eslot[:, 128 * lc:128 * lc + 128],
                            in0=Eslot[:, 128 * lc:128 * lc + 128],
                            in1=tril01[:], op=ALU.mult),
                            waits=[e2, init_gp], tag=f"{tag}mk{G}_{h}_{lc}")
                        z2m = dv(lambda e, lc=lc, Eslot=Eslot:
                                 e.tensor_reduce(dsc[:, 40 + lc:41 + lc],
                                                 Eslot[:, 128 * lc:128 * lc + 128],
                                                 axis=AX.X, op=ALU.add),
                                 waits=[mk], tag=f"{tag}zb2_{G}_{h}_{lc}")
                        trs = [(pbb[2][:, 128 * j:128 * j + 128],
                                Eslot[:, 128 * j:128 * j + 128])
                               for j in range(lc + 1)]
                        ptm = tr_group(2, trs, waits=[mk], tag=f"{tag}pt{G}_{h}_{lc}")
                        pte = evac(2, "scalar",
                                   lambda e, lc=lc: e.copy(
                                       ptbuf[:, 0:lc + 1, :],
                                       pbb[2][:, 0:128 * (lc + 1)]),
                                   waits=[ptm] + buf_war.get("ptbuf", []),
                                   tag=f"{tag}pte{G}_{h}_{lc}")
                        pte_l.append(pte)
                        buf_war[("E", par)] = [ptm]
                        for j in range(lc + 1):
                            st = (lc == 0 and j == 0)
                            sp = (lc == 3 and j == 3)
                            def fad(e, j=j, lc=lc, h=h, st=st, sp=sp):
                                return e.matmul(
                                    pb[4].ap()[:, 128 * lc:128 * lc + 128],
                                    ptbuf[:, j, :],
                                    xn_tm[:, 4 + j, 128 * h:128 * h + 128],
                                    start=st, stop=sp)
                            w = [pte] if j == 0 else []
                            if st:
                                w = w + [vt_last] + buf_war.get("bank4", [])
                            adm = p_.add("tensor", fad, waits=w,
                                         inc=("sPE" if j == lc else None),
                                         tag=f"{tag}ad{G}_{h}_{lc}_{j}")
                        adm_l.append(adm)
                        buf_war["ptbuf"] = [adm]
                        yield
                    zsum = dv(lambda e: e.tensor_tensor(
                        out=dsc[:, 36:40], in0=dsc[:, 36:40],
                        in1=dsc[:, 40:44], op=ALU.add),
                        waits=[z2m, e1, z0m], tag=f"{tag}zs_{G}_{h}")
                    rzb = dv(lambda e: e.reciprocal(dsc[:, 40:44],
                                                    dsc[:, 36:40]),
                             waits=[zsum], tag=f"{tag}rz_{G}_{h}")
                    buf_war["zb1"] = [rzb]
                    sdb = dv(lambda e, omgcol=omgcol: e.tensor_scalar(
                        out=dsc[:, 44:48], in0=dsc[:, 40:44],
                        scalar1=omgcol, scalar2=None, op0=ALU.mult),
                        waits=[rzb] + buf_war.get("zb_sd", []),
                        tag=f"{tag}sd_{G}_{h}")
                    ads = None
                    for lc in range(4):
                        ads = evac(4, "scalar",
                                   lambda e, lc=lc: e.activation(
                                       out=adbuf[:, 128 * lc:128 * lc + 128],
                                       in_=pb[4].ap()[:, 128 * lc:128 * lc + 128],
                                       func=AF.Identity,
                                       scale=dsc[:, 44 + lc:45 + lc]),
                                   waits=[adm_l[3], sdb] +
                                         buf_war.get(("ad", h % 2, lc), []),
                                   tag=f"{tag}ads{G}_{h}_{lc}")
                    buf_war["zb_sd"] = [ads]
                    buf_war["bank4"] = [ads]
                    sc_pe[h] = p_.cur("sPE")

                def passB_g(h):
                    nonlocal mdelta_last_m
                    gcol = dsm[:, 2, h:h + 1]
                    adbuf = adbuf_a if h % 2 == 0 else adbuf_b
                    sc_m = sc_pe[h]
                    t12 = xn_tm.ap()[:, 14:16, :].bitcast(F32)
                    squ = sqkv[:, 0:2, h, :]
                    a1 = ac(lambda e, h=h: e.activation(
                        out=t12, in_=squ, func=AF.Relu, scale=-1.0),
                        waits=[("sPE", sc_m[1])] + buf_war.get("t1f", []),
                        tag=f"{tag}tr1_{G}_{h}")
                    a2 = ac(lambda e: e.activation(out=t12, in_=t12,
                                                   func=AF.Exp, scale=-1.0),
                            waits=[a1], tag=f"{tag}tr2_{G}_{h}")
                    el = dv(lambda e, h=h: e.scalar_tensor_tensor(
                        out=squ, in0=squ, scalar=0.0, in1=t12,
                        op0=ALU.max, op1=ALU.add),
                        waits=[a2], tag=f"{tag}tr4_{G}_{h}")
                    buf_war["t1f"] = [el]
                    el_m = [el, el]
                    yield
                    sktr = tr_group(3, [(pbb[3][:, 128 * t:128 * t + 128],
                                         sqkv[:, 1, h, 128 * t:128 * t + 128])
                                        for t in range(4)],
                                    waits=[el_m[1]], tag=f"{tag}skt{G}_{h}")
                    skte = evac(3, "vector",
                                lambda e, h=h: e.tensor_copy(
                                    xn_tm.ap()[:, 8:12, 128 * h:128 * h + 128],
                                    pbb[3][:, 0:512].rearrange(
                                        "p (t c) -> p t c", t=4)),
                                waits=[sktr], tag=f"{tag}skte{G}_{h}")
                    zt_hold[0] = dv(lambda e, h=h: e.tensor_reduce(
                        ztmp[:, h:h + 1], sqkv[:, 1, h, :], axis=AX.X,
                        op=ALU.add), waits=[el_m[1]], tag=f"{tag}zt{G}_{h}")
                    yield
                    denm = mm_group(5, [(sqkv[:, u, h, 128 * lc:128 * lc + 128],
                                         Maug[:, h, 128:129],
                                         pb[5].ap()[:, 2 * lc + u:2 * lc + u + 1])
                                        for lc in range(4) for u in range(2)],
                                    waits=maug_ready + [el_m[1]],
                                    tag=f"{tag}den{G}_{h}")
                    num_read_m.append(denm)
                    dev = evac(5, "scalar", lambda e: e.activation(
                        out=dsc[:, 16:24], in_=pb[5].ap()[:, 0:8],
                        func=AF.Identity, bias=bepscol[:, 0:1]),
                        waits=[denm] + buf_war.get("dsc16", []),
                        tag=f"{tag}dev{G}_{h}")
                    rcp = dsc[:, 24:32].rearrange("p (l u) -> p l u", u=2)
                    rbm = dv(lambda e: e.reciprocal(dsc[:, 24:32],
                                                    dsc[:, 16:24]),
                             waits=[dev], tag=f"{tag}rb{G}_{h}")
                    buf_war["dsc16"] = [rbm]
                    gqb = dv(lambda e, gcol=gcol: e.tensor_scalar(
                        out=dsc[:, 32:36], in0=rcp[:, :, 0:1],
                        scalar1=gcol, scalar2=None, op0=ALU.mult),
                        waits=[rbm], tag=f"{tag}gq{G}_{h}")
                    yield
                    c1_last = None
                    for lc in range(4):
                        nb = 3 if lc % 2 == 0 else 5
                        nq = [(sqkv[:, 0, h, 128 * lc:128 * lc + 128],
                               Maug[:, h, 0:128], pb[nb].ap()[:, 0:128]),
                              (sqkv[:, 1, h, 128 * lc:128 * lc + 128],
                               Maug[:, h, 0:128], pb[nb].ap()[:, 128:256])]
                        nm = mm_group(nb, nq, waits=maug_ready + [el_m[1], skte],
                                      tag=f"{tag}nm{G}_{h}_{lc}")
                        num_read_m.append(nm)
                        vdp = xn_tm.ap()[:, 12, 128 * (lc % 2):
                                         128 * (lc % 2) + 128]
                        vdm = dv(lambda e, h=h, lc=lc, vdp=vdp, nb=nb:
                                 e.scalar_tensor_tensor(
                                     out=vdp, in0=pb[nb].ap()[:, 128:256],
                                     scalar=rcp[:, lc, 1:2],
                                     in1=xn_tm[:, 4 + lc, 128 * h:128 * h + 128],
                                     op0=ALU.mult, op1=ALU.subtract),
                                 waits=[nm, gqb] + buf_war.get(("vd", lc % 2), []),
                                 tag=f"{tag}vd_{G}_{h}_{lc}")
                        mb = 6 + h // 4
                        def fmd(e, h=h, lc=lc, mb=mb, vdp=vdp,
                                st=mdelta_first[mb]):
                            return e.matmul(
                                pb[mb].ap()[:, 128 * (h % 4):128 * (h % 4) + 128],
                                xn_tm[:, 8 + lc, 128 * h:128 * h + 128],
                                vdp, start=st,
                                stop=(h % 4 == 3 and lc == 3))
                        mw = [vdm, skte]
                        if mdelta_first[mb] and mb in bank_war and bank_war[mb]:
                            mw.append(bank_war[mb])
                        mdelta_last_m = p_.add("tensor", fmd, waits=mw,
                                               inc="sPE",
                                               tag=f"{tag}md{G}_{h}_{lc}")
                        mdelta_first[mb] = False
                        buf_war[("vd", lc % 2)] = [mdelta_last_m]
                        c1_last = dv(lambda e, h=h, lc=lc, nb=nb:
                                     e.scalar_tensor_tensor(
                                         out=adbuf[:, 128 * lc:128 * lc + 128],
                                         in0=pb[nb].ap()[:, 0:128],
                                         scalar=dsc[:, 32 + lc:33 + lc],
                                         in1=adbuf[:, 128 * lc:128 * lc + 128],
                                         op0=ALU.mult, op1=ALU.add),
                                     waits=[vdm], tag=f"{tag}c1_{G}_{h}_{lc}")
                        bank_war[nb] = c1_last
                        yield
                    c2 = dv(lambda e, G=G, h=h, adbuf=adbuf: e.tensor_tensor(
                        out=x_tm[:, 4 * G:4 * G + 4, 128 * h:128 * h + 128],
                        in0=adbuf.rearrange("p (l c) -> p l c", l=4),
                        in1=x_tm[:, 4 * G:4 * G + 4, 128 * h:128 * h + 128],
                        op=ALU.add), waits=[c1_last],
                        tag=f"{tag}c2_{G}_{h}")
                    for lc in range(4):
                        buf_war[("ad", h % 2, lc)] = [c2]
                sc_pe = {}
                zt_hold = [None]
                for _ in passA_g(0):
                    pass
                for h in range(H):
                    wstep()
                    gens = [passB_g(h)]
                    if h + 1 < H:
                        gens.append(passA_g(h + 1))
                    while gens:
                        for g_ in list(gens):
                            try:
                                next(g_)
                            except StopIteration:
                                gens.remove(g_)
                if n_sub >= 4:
                    for lc in range(4):
                        stats_tile(4 * G + lc, buf_war[("ad", 1, lc)],
                                   f"{tag}pf_")
                zu = dv(lambda e: e.tensor_tensor(out=zf, in0=zf, in1=ztmp,
                                                  op=ALU.add),
                        waits=[zt_hold[0]] + num_read_m[-1:], tag=f"{tag}zu{G}")
                zc = dv(lambda e: e.tensor_copy(
                    Maug[:, :, 128:129],
                    zf.rearrange("p (c o) -> p c o", o=1)),
                        waits=[zu], tag=f"{tag}zc{G}")
                m1_ = dv(lambda e: e.tensor_tensor(out=Mf[:, 0:4, :],
                                                   in0=Mf[:, 0:4, :],
                                                   in1=pb[6].ap(),
                                                   op=ALU.subtract),
                         waits=[mdelta_last_m], tag=f"{tag}mu{G}a")
                m2_ = dv(lambda e: e.tensor_tensor(out=Mf[:, 4:8, :],
                                                   in0=Mf[:, 4:8, :],
                                                   in1=pb[7].ap(),
                                                   op=ALU.subtract),
                         waits=[m1_], tag=f"{tag}mu{G}b")
                bank_war[6] = m2_
                bank_war[7] = m2_
                mc = dv(lambda e: e.tensor_copy(Maug[:, :, 0:128], Mf),
                        waits=[m2_] + num_read_m[-1:], tag=f"{tag}mc{G}")
                maug_ready = [mc, zc]
                proj_war = [p_.cur("sPE"), p_.cur("sDV")]

        ab_ldrs = None
        if n_sub >= 3:
            ab_ldrs = (LoadW(wqb_d, 0, 0, f"b{b}ab_ldwq"),
                       LoadW(wkb_d, 1, 1, f"b{b}ab_ldwk"),
                       LoadW(wvb_d, 2, 0, f"b{b}ab_ldwv"))
        f2_ldrs = None
        if n_sub >= 4:
            f2_ldrs = (LoadW(wrT_d, 0, 0, f"b{b}f2_ldwr"),
                       LoadW(wkfT_d, 1, 1, f"b{b}f2_ldwk"),
                       LoadW(wvfT_d, 2, 0, f"b{b}f2_ldwv"))
        nseq = None
        if n_sub >= 4 and b + 1 < BL:
            carry_qa = LoadW(wqaT_d, 0, 0, f"b{b + 1}ldwqa")
            carry_ka = LoadW(wkaT_d, 1, 1, f"b{b + 1}ldwka")
            nseq = (carry_qa, carry_ka, None)
        def f2_hook(g, t, m, b=b):
            i = 4 * g + t
            otm = out_d[b].rearrange("(t p) d -> p t d", p=P)
            sy_dma(otm[:, i:i + 1, :], x_tm[:, i:i + 1, :], waits=[m],
                   store=True, tag=f"st{b}_{i}")
            if b + 1 < BL:
                itm = x_d[b + 1].rearrange("(t p) d -> p t d", p=P)
                ld = sy_dma(x_tm[:, i:i + 1, :], itm[:, i:i + 1, :],
                            tag=f"ldx{b + 1}_{i}")
                stats_tile(i, [ld], f"b{b + 1}ln1_")

        f1_hook = None
        if n_sub >= 3:
            def f1_hook(g, t, m):
                stats_tile(4 * g + t, [m], f"b{b}ab_")
        if n_sub >= 2:
            emit_ffn(f"b{b}f1_", f1_ldrs, ab_ldrs, tile_hook=f1_hook,
                     dbg_g0=(dbg and b == DBG_SEQ))
        if n_sub >= 3:
            emit_attn_b(f"b{b}ab_", ab_ldrs, f2_ldrs)
        if n_sub >= 4:
            emit_ffn(f"b{b}f2_", f2_ldrs, nseq, tile_hook=f2_hook)
            NSTORE += NT
        assert not armed or (n_sub >= 4 and b + 1 < BL), \
            f"unarmed/armed leftovers at b={b}: {[L.tag for L in armed]}"

        if n_sub < 4:
            st_w = p_.barrier_waits("sDV", "sAC", "sPE")
            sy_dma(out_d[b].rearrange("(t p) d -> p t d", p=P), x_tm[:],
                   waits=st_w, store=True, tag=f"st{b}")
            NSTORE += 1

    p_.add("sync", lambda e: e.nop(),
           waits=[("sST", 16 * NSTORE), p_.cur("sGD"), p_.cur("sXD")],
           tag="final")

    p_.simulate()

    with ctx:
        blk = ctx.enter_context(nc.Block())

        def make_body(eng_name):
            def body(eng):
                for (fn, w, incs, tag) in p_.ops[eng_name]:
                    for (s, v) in w:
                        eng.wait_ge(sems[s], v)
                    inst = fn(eng)
                    for (s, v, a) in incs:
                        if a:
                            inst = inst.then_inc(sems[s], v)
            return body

        blk.sync(make_body("sync"))
        blk.gpsimd(make_body("gpsimd"))
        blk.tensor(make_body("tensor"))
        blk.vector(make_body("vector"))
        blk.scalar(make_body("scalar"))

    return nc


def _host_layout(inputs):
    g = lambda k: np.asarray(inputs[k])
    smalls = np.zeros((P, 8, NC), dtype=np.float32)
    def fm(v):
        return np.ascontiguousarray(v.reshape(NC, P).T)
    smalls[:, 0, :] = fm(g("L0_attn_a_bq"))
    smalls[:, 1, :] = fm(g("L0_attn_a_bk"))
    smalls[:, 2, :] = fm(g("L0_ffn_mu_r"))
    smalls[:, 3, :] = fm(g("L0_ffn_mu_k"))
    smalls[:, 4, :] = fm(g("L0_attn_b_bq"))
    smalls[:, 5, :] = fm(g("L0_attn_b_bk"))
    smalls[:, 6, :] = fm(g("L0_attn_b_bv"))
    smalls[:, 7, :] = np.broadcast_to(g("L0_attn_b_beta")[None, :], (P, 8))
    T = lambda a: np.ascontiguousarray(a.T)
    return {
        "wqaT": T(g("L0_attn_a_Wq")), "wkaT": T(g("L0_attn_a_Wk")),
        "wrT": T(g("L0_ffn_W_r")), "wkfT": T(g("L0_ffn_W_k")),
        "wvfT": T(g("L0_ffn_W_v")),
        "wqb": np.ascontiguousarray(g("L0_attn_b_Wq")),
        "wkb": np.ascontiguousarray(g("L0_attn_b_Wk")),
        "wvb": np.ascontiguousarray(g("L0_attn_b_Wv")),
        "smalls": smalls,
    }


N_SUB = 4

def kernel(**inputs):
    x = np.ascontiguousarray(np.asarray(inputs["L0_attn_a_x"], dtype=np.float32))
    shared = _host_layout(inputs)
    nc = build_nc(n_sub=N_SUB)
    in_maps = []
    for i in range(NCORES):
        m = dict(shared)
        m["x"] = np.ascontiguousarray(x[BL * i:BL * (i + 1)])
        in_maps.append(m)
    res = run_bass_kernel_spmd(nc, in_maps, core_ids=list(range(NCORES)))
    return np.concatenate([r["out"] for r in res.results], axis=0)


if __name__ == "__main__":
    nc = build_nc()
    print("built OK")


# version 13
# speedup vs baseline: 1.2530x; 1.0315x over previous
import numpy as np
from contextlib import ExitStack

import concourse.bass as bass
import concourse.mybir as mybir
from concourse.bass_utils import run_bass_kernel_spmd

F32 = mybir.dt.float32
BF16 = mybir.dt.bfloat16
AF = mybir.ActivationFunctionType
ALU = mybir.AluOpType
AX = mybir.AxisListType

P = 128
B, S, D = 32, 2048, 1024
NCORES = 8
BL = B // NCORES
NT = S // P
NC = D // P
NG = S // 512
H, DK, DV, SEG, NSEGS = 8, 128, 128, 512, 4
LN_EPS = 1e-5
B_EPS = 1e-6

ENGS = ("sync", "gpsimd", "tensor", "vector", "scalar")


class Plan:

    def __init__(self):
        self.ops = {e: [] for e in ENGS}
        self.cnt = {s: 0 for s in ("sPE", "sDV", "sAC", "sGP", "sXD", "sGD", "sST")}
        self.seen = {e: {} for e in ENGS}
        self.eng_sem = {"tensor": "sPE", "vector": "sDV", "scalar": "sAC",
                        "gpsimd": "sGP"}

    def add(self, eng, fn, waits=(), inc=None, n=1, tag="", attach=True):
        w = []
        for wv in waits:
            if wv is None:
                continue
            (s, v) = wv
            if v <= 0:
                continue
            if self.seen[eng].get(s, 0) >= v:
                continue
            self.seen[eng][s] = v
            w.append((s, v))
        incs = []
        if inc is not None:
            self.cnt[inc] += n
            incs.append((inc, n, attach))
        self.ops[eng].append((fn, w, incs, tag))
        return (inc, self.cnt[inc]) if inc is not None else None

    def cur(self, s):
        return (s, self.cnt[s])

    def barrier_waits(self, *sems):
        return [(s, self.cnt[s]) for s in sems]

    def simulate(self):
        pos = {e: 0 for e in ENGS}
        val = {s: 0 for s in self.cnt}
        progress = True
        while progress:
            progress = False
            for e in ENGS:
                while pos[e] < len(self.ops[e]):
                    fn, w, incs, tag = self.ops[e][pos[e]]
                    if any(val[s] < v for (s, v) in w):
                        break
                    for (s, v, _a) in incs:
                        val[s] += v
                    pos[e] += 1
                    progress = True
        stuck = {e: pos[e] for e in ENGS if pos[e] < len(self.ops[e])}
        if stuck:
            msg = []
            for e, p_ in stuck.items():
                fn, w, incs, tag = self.ops[e][p_]
                msg.append(f"{e}@{p_}/{len(self.ops[e])} tag={tag} waits={w} "
                           f"sems={[(s, val[s]) for s, _ in w]}")
            raise RuntimeError("PLAN DEADLOCK:\n" + "\n".join(msg))


FFN_LEVEL = 3
DBG_SEQ = 0

def build_nc(n_sub=4, dbg=False, ffn_level=None):
    if ffn_level is None:
        ffn_level = FFN_LEVEL
    nc = bass.Bass("TRN2", target_bir_lowering=False,
                   dynamic_dma_scratch_size=8192)

    x_d = nc.dram_tensor("x", [BL, S, D], F32, kind="ExternalInput").ap()
    out_d = nc.dram_tensor("out", [BL, S, D], F32, kind="ExternalOutput").ap()
    wqaT_d = nc.dram_tensor("wqaT", [D, D], F32, kind="ExternalInput").ap()
    wkaT_d = nc.dram_tensor("wkaT", [D, D], F32, kind="ExternalInput").ap()
    wrT_d = nc.dram_tensor("wrT", [D, D], F32, kind="ExternalInput").ap()
    wkfT_d = nc.dram_tensor("wkfT", [D, D], F32, kind="ExternalInput").ap()
    wvfT_d = nc.dram_tensor("wvfT", [D, D], F32, kind="ExternalInput").ap()
    wqb_d = nc.dram_tensor("wqb", [D, D], F32, kind="ExternalInput").ap()
    wkb_d = nc.dram_tensor("wkb", [D, D], F32, kind="ExternalInput").ap()
    wvb_d = nc.dram_tensor("wvb", [D, D], F32, kind="ExternalInput").ap()
    smalls_d = nc.dram_tensor("smalls", [P, 8, NC], F32, kind="ExternalInput").ap()
    if dbg:
        dbg_d = {k: nc.dram_tensor(f"dbg_{k}", shp, F32, kind="ExternalOutput").ap()
                 for k, shp in [("stats", [P, NT, 8]), ("xn", [P, NT, D]),
                                ("xpx", [P, NT, D]), ("K", [P, NC, S]),
                                ("csum", [1, D]), ("zbuf", [P, NT, 2]),
                                ("E8", [P, S]), ("q2", [P, NC, 512]),
                                ("xrxk", [P, 2, NC, 512]), ("r", [P, NC, 512]),
                                ("k2", [P, NC, 512]), ("rc", [P, NC, 512]),
                                ("xnfm", [P, 2, NC, 516])]}

    ctx = ExitStack()
    sb = lambda *a: ctx.enter_context(nc.sbuf_tensor(*a))
    ps = lambda *a: ctx.enter_context(nc.psum_tensor(*a))

    x_tm = sb("x_tm", [P, NT, D], F32)
    xn_tm = sb("xn_tm", [P, NT, D], BF16)
    xn_fm = sb("xn_fm", [P, 2, NC, 516], BF16)
    a3 = sb("a3", [P, 32768], mybir.dt.uint8)
    warena = sb("warena", [P, 3, NC, D], BF16)
    ebuf = sb("ebuf", [P, 2, S], BF16)
    ptbuf = sb("ptbuf", [P, 17, P], BF16)
    stats = sb("stats", [P, NT, 8], F32)
    smalls = sb("smalls_s", [P, 8, NC], F32)
    dsm = sb("dsm", [P, 8, NC], F32)
    identb = sb("identb", [P, P], BF16)
    tril01 = sb("tril01", [P, P], BF16)
    identf = sb("identf", [P, P], F32)
    onescol = sb("onescol", [P, 1], BF16)
    epscol = sb("epscol", [P, 1], F32)
    bepscol = sb("bepscol", [P, 1], F32)
    zbuf = sb("zbuf", [P, NT, 2], F32)
    zpart = sb("zpart", [P, 8], F32)
    nzcol = sb("nzcol", [P, 2], BF16)
    zrow = sb("zrow", [1, 2, P], BF16)
    colsum = sb("colsum", [1, D], BF16)
    wst0 = sb("wst0", [P, 1024], F32)
    junkb = sb("junkb", [P, 1024], BF16)

    K_fm = a3.ap().bitcast(BF16).rearrange("p (c s) -> p c s", c=NC)
    junk = ebuf.ap().bitcast(F32)[:, 0, :]

    pb = [ps(f"pb{i}", [P, 512], F32) for i in range(8)]
    pbb = [pb[i].ap().bitcast(BF16) for i in range(8)]

    sems = {}
    for s in ("sPE", "sDV", "sAC", "sGP", "sXD", "sGD", "sST"):
        sems[s] = ctx.enter_context(nc.semaphore(s))

    p_ = Plan()
    bank_war = {}
    buf_war = {}

    def mm_group(bank, mms, waits=(), tag=""):
        w = list(waits)
        if bank in bank_war and bank_war[bank]:
            w.append(bank_war[bank])
        last = None
        n = len(mms)
        for i, m in enumerate(mms):
            lhsT, rhs = m[0], m[1]
            out = m[2] if len(m) > 2 else pb[bank].ap()
            st, sp = (i == 0), (i == n - 1)
            def f(e, out=out, lhsT=lhsT, rhs=rhs, st=st, sp=sp):
                return e.matmul(out, lhsT, rhs, start=st, stop=sp)
            last = p_.add("tensor", f, waits=(w if i == 0 else ()),
                          inc=("sPE" if sp else None), tag=tag)
        return last

    def tr_group(bank, trs, waits=(), dtype=BF16, tag=""):
        w = list(waits)
        if bank in bank_war and bank_war[bank]:
            w.append(bank_war[bank])
        ident = identb[:] if dtype == BF16 else identf[:]
        last = None
        for i, (out, in_) in enumerate(trs):
            def f(e, out=out, in_=in_, ident=ident):
                return e.transpose(out, in_, ident)
            last = p_.add("tensor", f, waits=(w if i == 0 else ()),
                          inc=("sPE" if i == len(trs) - 1 else None), tag=tag)
        return last

    def evac(bank, eng, fn, waits=(), tag=""):
        sem = p_.eng_sem[eng]
        mile = p_.add(eng, fn, waits=waits, inc=sem, tag=tag)
        bank_war[bank] = mile
        return mile

    def dv(fn, waits=(), tag="", inc=True):
        return p_.add("vector", fn, waits=waits, inc=("sDV" if inc else None), tag=tag)

    def ac(fn, waits=(), tag="", inc=True):
        return p_.add("scalar", fn, waits=waits, inc=("sAC" if inc else None), tag=tag)

    def gp(fn, waits=(), tag="", inc=True):
        return p_.add("gpsimd", fn, waits=waits, inc=("sGP" if inc else None), tag=tag)

    def sy_dma(out, in_, waits=(), store=False, tag=""):
        waits = list(waits) + [p_.cur("sXD"), p_.cur("sST")]
        def f(e, out=out, in_=in_):
            return e.dma_start(out=out, in_=in_)
        return p_.add("sync", f, waits=waits, inc=("sST" if store else "sXD"),
                      n=16, tag=tag)

    def hard_barrier(tag=""):
        arr = {}
        for e in ("tensor", "vector", "scalar", "gpsimd"):
            s = p_.eng_sem[e]
            def mk(s=s):
                return lambda en: en.sem_inc(sems[s], 1)
            arr[e] = p_.add(e, mk(), inc=s, tag=f"{tag}barA", attach=False)
        dmaw = [p_.cur("sGD"), p_.cur("sXD"), p_.cur("sST")]
        allw = list(arr.values()) + dmaw
        for e in ("tensor", "vector", "scalar", "gpsimd", "sync"):
            p_.add(e, (lambda en: en.wait_ge(sems["sGP"], 1)), waits=allw,
                   tag=f"{tag}barW")

    def sc_dma(out, in_, waits=(), tag=""):
        waits = list(waits) + [p_.cur("sGD")]
        def f(e, out=out, in_=in_):
            return e.dma_start(out=out, in_=in_)
        return p_.add("scalar", f, waits=waits, inc="sGD", n=16, tag=tag)

    def gp_dma(out, in_, waits=(), tag=""):
        waits = list(waits) + [p_.cur("sGD")]
        def f(e, out=out, in_=in_):
            return e.dma_start(out=out, in_=in_)
        return p_.add("gpsimd", f, waits=waits, inc="sGD", n=16, tag=tag)

    sy_dma(smalls[:], smalls_d, tag="ld_smalls")
    sm_ready = p_.cur("sXD")

    m1 = gp(lambda e: e.memset(identb[:], 0.0), tag="identb0")
    gp(lambda e: e.affine_select(out=identb[:], in_=identb[:],
                                 compare_op=ALU.not_equal, fill=1.0, base=0,
                                 pattern=[[-1, P]], channel_multiplier=1),
       waits=[m1], tag="identb")
    mt = gp(lambda e: e.memset(tril01[:], 1.0), tag="tril0")
    gp(lambda e: e.affine_select(out=tril01[:], in_=tril01[:],
                                 compare_op=ALU.is_ge, fill=0.0, base=0,
                                 pattern=[[-1, P]], channel_multiplier=1),
       waits=[mt], tag="tril")
    m2 = gp(lambda e: e.memset(identf[:], 0.0), tag="identf0")
    gp(lambda e: e.affine_select(out=identf[:], in_=identf[:],
                                 compare_op=ALU.not_equal, fill=1.0, base=0,
                                 pattern=[[-1, P]], channel_multiplier=1),
       waits=[m2], tag="identf")
    gp(lambda e: e.memset(onescol[:], 1.0), tag="ones")
    gp(lambda e: e.memset(epscol[:], LN_EPS), tag="eps")
    gp(lambda e: e.memset(bepscol[:], B_EPS), tag="beps")
    gp(lambda e: e.memset(xn_fm[:], 0.0), tag="fmz")
    init_gp = p_.cur("sGP")

    dv(lambda e: e.tensor_scalar(out=dsm[:, 0, :], in0=smalls[:, 2, :],
                                 scalar1=-1.0, scalar2=1.0, op0=ALU.mult,
                                 op1=ALU.add), waits=[sm_ready], tag="onemur")
    dv(lambda e: e.tensor_scalar(out=dsm[:, 1, :], in0=smalls[:, 3, :],
                                 scalar1=-1.0, scalar2=1.0, op0=ALU.mult,
                                 op1=ALU.add), tag="onemuk")
    ac(lambda e: e.activation(out=dsm[:, 2, :], in_=smalls[:, 7, :],
                              func=AF.Sigmoid), waits=[sm_ready], tag="gate")
    gate_done = p_.cur("sAC")
    dv(lambda e: e.tensor_scalar(out=dsm[:, 3, :], in0=dsm[:, 2, :],
                                 scalar1=-1.0, scalar2=1.0, op0=ALU.mult,
                                 op1=ALU.add), waits=[gate_done], tag="onegate")
    dv(lambda e: e.tensor_scalar(out=dsm[:, 6, :], in0=smalls[:, 4, :],
                                 scalar1=-1.0, scalar2=None, op0=ALU.mult),
       tag="negbq")
    dv(lambda e: e.tensor_scalar(out=dsm[:, 7, :], in0=smalls[:, 5, :],
                                 scalar1=-1.0, scalar2=None, op0=ALU.mult),
       tag="negbk")
    init_dv = p_.cur("sDV")
    init_ac = p_.cur("sAC")

    armed = []

    class LoadW:
        def __init__(self, dram_ap, slot, q=0, tag=""):
            self.dram_ap, self.slot, self.q, self.tag = dram_ap, slot, q, tag
            self.c = NC - 1
            self.gate = []
            self.last = None
            self.mile = None

        def step(self):
            if self.c < 0:
                return
            c = self.c
            self.c -= 1
            dma = sy_dma if self.q == 0 else sc_dma
            wf = warena.ap()[:, self.slot, :, :].rearrange(
                "p c e -> p (c e)").bitcast(F32)
            stage = wf[:, 0:1024] if c >= 2 else wst0.ap()
            w = list(self.gate) + ([self.last] if self.last else [])
            if c < 2:
                w = w + buf_war.get("wst", [])
            ld = dma(stage, self.dram_ap[128 * c:128 * (c + 1), :], waits=w,
                     tag=f"{self.tag}c{c}")
            self.last = ac(lambda e, slot=self.slot, c=c, stage=stage:
                           e.activation(out=warena[:, slot, c, :], in_=stage,
                                        func=AF.Identity),
                           waits=[ld], tag=f"{self.tag}x{c}")
            if c < 2:
                buf_war["wst"] = [self.last]
            if self.c < 0:
                self.mile = self.last

        def fin(self):
            while self.c >= 0:
                self.step()
            if self in armed:
                armed.remove(self)
            return self.mile

    def arm(ldr, gate):
        ldr.gate = list(gate)
        armed.append(ldr)

    def wstep():
        for ldr in list(armed):
            ldr.step()
            if ldr.c < 0:
                armed.remove(ldr)

    slot_free_gate = lambda: p_.barrier_waits("sPE", "sAC", "sDV")

    NSTORE = 0
    carry_qa = carry_ka = None
    stats_done = {}
    stats_guard = {}

    for b in range(BL):
        if n_sub >= 4 and b > 0:
            x_ready = []
        else:
            xw = [p_.cur("sST")]
            x_mile = sy_dma(x_tm[:], x_d[b].rearrange("(t p) d -> p t d", p=P),
                            waits=xw, tag=f"ldx{b}")
            x_ready = [x_mile]

        def stats_tile(i, waits, tag):
            w = list(waits) + stats_guard.get(i // 4, [])
            s1m = dv(lambda e, i=i: e.tensor_reduce(
                stats[:, i, 0:1], x_tm[:, i, :], axis=AX.X, op=ALU.add),
                waits=w, tag=f"{tag}S1_{i}")
            s2m = ac(lambda e, i=i: e.activation(
                out=junkb.ap(), in_=x_tm[:, i, :], func=AF.Square,
                accum_out=stats[:, i, 1:2]),
                waits=w + buf_war.get("junkb", []), tag=f"{tag}S2_{i}")
            buf_war["junkb"] = [s2m]
            stats_done[i] = (s1m, s2m)

        def ln_stats(xpx, pre_waits, tag):
            for i in range(NT):
                if i not in stats_done:
                    stats_tile(i, pre_waits + [sm_ready], tag)
            mrns, oprs = [], []
            for g4 in range(4):
                sl = slice(4 * g4, 4 * g4 + 4)
                sw = [m for i in range(4 * g4, 4 * g4 + 4)
                      for m in stats_done[i]]
                m1 = dv(lambda e, sl=sl: e.tensor_scalar(
                    out=stats[:, sl, 2:3], in0=stats[:, sl, 0:1],
                    scalar1=1.0 / D, scalar2=None, op0=ALU.mult),
                    waits=sw, tag=f"{tag}m{g4}")
                dv(lambda e, sl=sl: e.tensor_scalar(
                    out=stats[:, sl, 4:5], in0=stats[:, sl, 1:2],
                    scalar1=1.0 / D, scalar2=None, op0=ALU.mult),
                    tag=f"{tag}s2n{g4}")
                s2 = dv(lambda e, sl=sl: e.tensor_tensor(
                    out=stats[:, sl, 3:4], in0=stats[:, sl, 2:3],
                    in1=stats[:, sl, 2:3], op=ALU.mult),
                    waits=[m1], tag=f"{tag}m2{g4}")
                var = dv(lambda e, sl=sl: e.tensor_tensor(
                    out=stats[:, sl, 3:4], in0=stats[:, sl, 4:5],
                    in1=stats[:, sl, 3:4], op=ALU.subtract),
                    waits=[s2], tag=f"{tag}var{g4}")
                lnv = ac(lambda e, sl=sl: e.activation(
                    out=stats[:, sl, 4:5], in_=stats[:, sl, 3:4], func=AF.Ln,
                    bias=epscol[:, 0:1]),
                    waits=[var, init_gp], tag=f"{tag}ln{g4}")
                rstd = ac(lambda e, sl=sl: e.activation(
                    out=stats[:, sl, 5:6], in_=stats[:, sl, 4:5], func=AF.Exp,
                    scale=-0.5),
                    waits=[lnv], tag=f"{tag}rstd{g4}")
                mrn = dv(lambda e, sl=sl: e.scalar_tensor_tensor(
                    out=stats[:, sl, 6:7], in0=stats[:, sl, 2:3],
                    scalar=-1.0, in1=stats[:, sl, 5:6],
                    op0=ALU.mult, op1=ALU.mult),
                    waits=[rstd], tag=f"{tag}mrn{g4}")
                opr = None
                if xpx:
                    opr = dv(lambda e, sl=sl: e.tensor_scalar(
                        out=stats[:, sl, 7:8], in0=stats[:, sl, 5:6],
                        scalar1=1.0, scalar2=None, op0=ALU.add),
                        waits=[rstd], tag=f"{tag}opr{g4}")
                settle = dv(lambda e, sl=sl: e.tensor_scalar(
                    out=stats[:, sl, 4:5], in0=stats[:, sl, 6:7],
                    scalar1=0.0, scalar2=None, op0=ALU.add),
                    waits=[mrn] + ([opr] if opr else []),
                    tag=f"{tag}settle{g4}")
                stats_guard[g4] = [settle]
                mrns.append(settle)
                oprs.append(settle if xpx else None)
            stats_done.clear()
            return mrns, oprs

        def ln_apply(i, dst, mrn, extra_waits, tag):
            if i % 2 == 0:
                return ac(lambda e, i=i, dst=dst: e.activation(
                    out=xn_tm[:, dst, :], in_=x_tm[:, i, :], func=AF.Identity,
                    scale=stats[:, i, 5:6], bias=stats[:, i, 6:7]),
                    waits=[mrn] + list(extra_waits), tag=f"{tag}xn{i}")
            return dv(lambda e, i=i, dst=dst: e.tensor_scalar(
                out=xn_tm[:, dst, :], in0=x_tm[:, i, :],
                scalar1=stats[:, i, 5:6], scalar2=stats[:, i, 6:7],
                op0=ALU.mult, op1=ALU.add),
                waits=[mrn] + list(extra_waits), tag=f"{tag}xn{i}")

        def ln_xpx(i, opr, apl, tag):
            if i % 2 == 1:
                return ac(lambda e, i=i: e.activation(
                    out=x_tm[:, i, :], in_=x_tm[:, i, :], func=AF.Identity,
                    scale=stats[:, i, 7:8], bias=stats[:, i, 6:7]),
                    waits=[opr, apl], tag=f"{tag}xpx{i}")
            return dv(lambda e, i=i: e.tensor_scalar(
                out=x_tm[:, i, :], in0=x_tm[:, i, :],
                scalar1=stats[:, i, 7:8], scalar2=stats[:, i, 6:7],
                op0=ALU.mult, op1=ALU.add),
                waits=[opr, apl], tag=f"{tag}xpx{i}")

        def fm_transpose_group(g, buf, src_tiles, xn_m, tag, guard_from=None):
            miles = []
            tm = None
            for half in range(4):
                dc0 = 2 * half
                bnk = 6 + (half % 2)
                trs = []
                for dci in range(2):
                    dc = dc0 + dci
                    for t in range(4):
                        out = pbb[bnk][:, 512 * dci + 128 * t:
                                       512 * dci + 128 * t + 128]
                        in_ = xn_tm[:, src_tiles[t], 128 * dc:128 * dc + 128]
                        trs.append((out, in_))
                w = list(xn_m)
                tm = tr_group(bnk, trs, waits=w, tag=f"{tag}tr{half}")
                eng = "vector" if half % 2 == 0 else "scalar"
                if eng == "vector":
                    def f(e, bnk=bnk, buf=buf, dc0=dc0):
                        return e.tensor_copy(xn_fm[:, buf, dc0:dc0 + 2, 4:516],
                                             pbb[bnk][:, 0:1024])
                else:
                    def f(e, bnk=bnk, buf=buf, dc0=dc0):
                        return e.copy(xn_fm[:, buf, dc0:dc0 + 2, 4:516],
                                      pbb[bnk][:, 0:1024])
                miles.append(evac(bnk, eng, f, waits=[tm], tag=f"{tag}ev{half}"))
            return miles, tm

        aw_pre = p_.barrier_waits("sPE", "sAC")
        mrns, oprs = ln_stats(xpx=True,
                              pre_waits=x_ready + [p_.cur("sDV"),
                                                   p_.cur("sAC")],
                              tag=f"b{b}ln1_")
        if carry_qa is None:
            LQA = LoadW(wqaT_d, 0, 0, f"b{b}ldwqa")
            LKA = LoadW(wkaT_d, 1, 1, f"b{b}ldwka")
            arm(LQA, slot_free_gate() if b > 0 else [])
            arm(LKA, slot_free_gate() if b > 0 else [])
        else:
            LQA, LKA = carry_qa, carry_ka
        carry_qa = carry_ka = None
        if n_sub >= 2:
            f1_ldrs = (LoadW(wrT_d, 0, 0, f"b{b}f1_ldwr"),
                       LoadW(wkfT_d, 1, 1, f"b{b}f1_ldwk"),
                       LoadW(wvfT_d, 2, 0, f"b{b}f1_ldwv"))
        else:
            f1_ldrs = None
        w_ka = LKA.fin()
        xn_m = [ln_apply(i, i, mrns[i // 4], [], tag=f"b{b}ln1_")
                for i in range(NT)]
        xpx_all = [ln_xpx(i, oprs[i // 4], xn_m[i], tag=f"b{b}ln1_")
                   for i in range(NT)]
        xpx_m = xpx_all[-2:]

        kproj_m = []
        for g in range(NG):
            fm_m, _ = fm_transpose_group(g, g % 2, [4 * g + t for t in range(4)],
                                         xn_m[4 * g + 2:4 * g + 4],
                                         tag=f"b{b}a1g{g}")
            for ec in range(NC):
                bnk = ec % 2
                mms = [(warena[:, 1, dc, 128 * ec:128 * ec + 128],
                        xn_fm[:, g % 2, dc, 4:516]) for dc in range(NC)]
                gm = mm_group(bnk, mms, waits=fm_m + [w_ka], tag=f"b{b}K{g}_{ec}")
                def f(e, bnk=bnk, ec=ec, g=g):
                    return e.activation(out=K_fm[:, ec, 512 * g:512 * g + 512],
                                        in_=pb[bnk].ap(), func=AF.Identity,
                                        bias=smalls[:, 1, ec:ec + 1])
                kproj_m.append(evac(bnk, "scalar", f, waits=[gm],
                                    tag=f"b{b}Kev{g}_{ec}"))

        if f1_ldrs:
            arm(f1_ldrs[1], slot_free_gate())
        cs_m = []
        for dh in range(2):
            mms = [(onescol[:, 0:1], xn_tm[:, kc, 512 * dh:512 * dh + 512],
                    pb[6].ap()[0:1, :]) for kc in range(NT)]
            gm = mm_group(6, mms, waits=xn_m[-2:] + [init_gp],
                          tag=f"b{b}cs{dh}")
            def f(e, dh=dh):
                return e.tensor_copy(colsum[0:1, 512 * dh:512 * dh + 512],
                                     pb[6].ap()[0:1, :])
            cs_m.append(evac(6, "vector", f, waits=[gm], tag=f"b{b}csev{dh}"))

        k_all = kproj_m[-1]

        def emit_scores(c, c4, qg_m):
            eb = c % 2
            zp = 4 * (c % 2)
            exp_m = []
            for kb in range(4):
                bnk = kb % 2
                mms = [(warena[:, 2, ec, 128 * c4:128 * c4 + 128],
                        K_fm[:, ec, 512 * kb:512 * kb + 512])
                       for ec in range(NC)]
                gm = mm_group(bnk, mms,
                              waits=qg_m + [k_all] + buf_war.get(
                                  ("ebuf", eb), []),
                              tag=f"b{b}s{c}_{kb}")
                def f(e, bnk=bnk, eb=eb, kb=kb, zp=zp):
                    return e.activation(
                        out=ebuf[:, eb, 512 * kb:512 * kb + 512],
                        in_=pb[bnk].ap(), func=AF.Exp, scale=1.0 / 32.0,
                        accum_out=zpart[:, zp + kb:zp + kb + 1])
                exp_m.append(evac(bnk, "scalar", f,
                                  waits=[gm] + buf_war.get(("zpart", eb), []),
                                  tag=f"b{b}e{c}_{kb}"))
            return exp_m

        def emit_tail(c, exp_m):
            eb = c % 2
            zp = 4 * (c % 2)
            zr = dv(lambda e, c=c, zp=zp: e.tensor_reduce(
                zbuf[:, c, 0:1], zpart[:, zp:zp + 4], axis=AX.X, op=ALU.add),
                waits=[exp_m[-1]], tag=f"b{b}Z{c}")
            buf_war[("zpart", eb)] = [zr]
            rz = dv(lambda e, c=c: e.reciprocal(zbuf[:, c, 1:2],
                                                zbuf[:, c, 0:1]),
                    waits=[zr], tag=f"b{b}rZ{c}")
            nz = dv(lambda e, c=c, eb=eb: e.tensor_scalar(
                out=nzcol[:, eb:eb + 1], in0=zbuf[:, c, 0:1],
                scalar1=-2.0 / S, scalar2=None, op0=ALU.mult),
                waits=[zr] + buf_war.get(("nzcol", eb), []),
                tag=f"b{b}nz{c}")
            ztr = tr_group(6, [(pbb[6][0:1, 0:P], nzcol[:, eb:eb + 1])],
                           waits=[nz], tag=f"b{b}ztr{c}")
            buf_war[("nzcol", eb)] = [ztr]
            zev = evac(6, "vector",
                       lambda e, eb=eb: e.tensor_copy(zrow[0:1, eb, :],
                                                      pbb[6][0:1, 0:P]),
                       waits=[ztr] + buf_war.get(("zrow", eb), []),
                       tag=f"b{b}zev{c}")
            pt_m = []
            for half in range(2):
                bnk = 2 + half
                trs = []
                for kci in range(8):
                    kc = 8 * half + kci
                    trs.append((pbb[bnk][:, 128 * kci:128 * kci + 128],
                                ebuf[:, eb, 128 * kc:128 * kc + 128]))
                tm = tr_group(bnk, trs, waits=[exp_m[-1]],
                              tag=f"b{b}pt{c}_{half}")
                def f(e, bnk=bnk, half=half):
                    return e.tensor_copy(ptbuf[:, 8 * half:8 * half + 8, :],
                                         pbb[bnk][:, 0:1024])
                pt_m.append(evac(bnk, "vector", f, waits=[tm],
                                 tag=f"b{b}ptev{c}_{half}"))
            buf_war[("ebuf", eb)] = [pt_m[-1]]
            for dh in range(2):
                bnk = 4 + dh
                mms = [(ptbuf[:, kc, :],
                        xn_tm[:, kc, 512 * dh:512 * dh + 512])
                       for kc in range(NT)]
                mms.append((zrow[0:1, eb, :],
                            colsum[0:1, 512 * dh:512 * dh + 512]))
                gm = mm_group(bnk, mms,
                              waits=pt_m + [zev] + cs_m, tag=f"b{b}pv{c}_{dh}")
                buf_war[("zrow", eb)] = [gm]
                def f(e, bnk=bnk, c=c, dh=dh):
                    return e.scalar_tensor_tensor(
                        out=x_tm[:, c, 512 * dh:512 * dh + 512],
                        in0=pb[bnk].ap(), scalar=zbuf[:, c, 1:2],
                        in1=x_tm[:, c, 512 * dh:512 * dh + 512],
                        op0=ALU.mult, op1=ALU.add)
                pv_ev = evac(bnk, "vector", f, waits=[gm, rz] + xpx_m,
                             tag=f"b{b}pvev{c}_{dh}")
            if n_sub >= 2:
                stats_tile(c, [pv_ev], f"b{b}f1_")

        pend = None
        w_qa = LQA.fin()
        for gi, g in enumerate([2, 3, 0, 1]):
            if gi >= 2:
                fm_m, _ = fm_transpose_group(g, g % 2,
                                             [4 * g + t for t in range(4)],
                                             xn_m[4 * g + 2:4 * g + 4],
                                             tag=f"b{b}a2g{g}")
            else:
                fm_m = []
            qg_m = []
            for ec in range(NC):
                bnk = ec % 2
                mms = [(warena[:, 0, dc, 128 * ec:128 * ec + 128],
                        xn_fm[:, g % 2, dc, 4:516]) for dc in range(NC)]
                gm = mm_group(bnk, mms, waits=fm_m + [w_qa], tag=f"b{b}Q{g}_{ec}")
                def f(e, bnk=bnk, ec=ec):
                    return e.activation(out=warena[:, 2, ec, 0:512],
                                        in_=pb[bnk].ap(), func=AF.Identity,
                                        bias=smalls[:, 0, ec:ec + 1])
                qg_m.append(evac(bnk, "scalar", f, waits=[gm],
                                 tag=f"b{b}Qev{g}_{ec}"))
            if gi == 3 and f1_ldrs:
                arm(f1_ldrs[0], slot_free_gate())
            for c4 in range(4):
                c = 4 * g + c4
                wstep()
                em = emit_scores(c, c4, qg_m)
                if pend is not None:
                    emit_tail(*pend)
                pend = (c, em)
        if f1_ldrs:
            arm(f1_ldrs[2], slot_free_gate())
        emit_tail(*pend)
        wstep()

        def emit_ffn(tag, ldrs, nxt, tile_hook=None, dbg_g0=False):
            if ffn_level < 0:
                return
            import os as _os0
            if _os0.environ.get("FFN_PRE2"):
                pre = p_.barrier_waits("sPE", "sAC")
            else:
                pre = p_.barrier_waits("sDV", "sAC", "sPE", "sGP", "sGD")

            xrxk = a3.ap()[:, 0:16384].bitcast(BF16).rearrange(
                "p (u c s) -> p u c s", u=2, c=NC)
            rc_fm = a3.ap()[:, 16384:32768].bitcast(F32).rearrange(
                "p (c s) -> p c s", c=NC)
            r_fm = xn_tm.ap()[:, 8:16, :].bitcast(F32)
            relu_b = ebuf.ap().rearrange("p u s -> p (u s)").rearrange(
                "p (c s) -> p c s", c=NC)

            if ffn_level == 0:
                return
            mrns2, _ = ln_stats(xpx=False, pre_waits=pre, tag=f"{tag}ln")
            w_r = ldrs[0].fin()
            w_k = ldrs[1].fin()
            w_v = ldrs[2].fin()
            gz = dv(lambda e: e.memset(xn_fm[:, 0, :, 3:4], 0.0), waits=pre,
                    tag=f"{tag}gz")
            guard_m = {0: gz}
            st_ = {"tr_prev": None, "mix_prev": None, "ab_prev": None,
                   "c_prev": None, "rc_prev": None}
            gstate = {}

            def s12(g):
                buf = g % 2
                aw = ([st_["tr_prev"]] if st_["tr_prev"] else [])
                xn_mg = [ln_apply(4 * g + t, t, mrns2[g], aw,
                                  tag=f"{tag}g{g}") for t in range(4)]
                fm_m, tr_last = fm_transpose_group(
                    g, buf, [0, 1, 2, 3], xn_mg[-2:], tag=f"{tag}g{g}")
                st_["tr_prev"] = tr_last
                if g + 1 < NG:
                    gc = dv(lambda e, buf=buf: e.tensor_copy(
                        xn_fm[:, 1 - buf, :, 3:4], xn_fm[:, buf, :, 515:516]),
                        waits=fm_m + ([st_["mix_prev"]] if st_["mix_prev"] else []),
                        tag=f"{tag}gc{g}")
                    guard_m[g + 1] = gc
                mw = fm_m + [guard_m[g]] + ([st_["ab_prev"]] if st_["ab_prev"] else [])
                mix_m = None
                for dc in range(NC):
                    wstep()
                    for u, mucol, omcol in ((0, 2, 0), (1, 3, 1)):
                        tsm = dv(lambda e, dc=dc, u=u, omcol=omcol, buf=buf:
                           e.tensor_scalar(
                               out=xrxk[:, u, dc, :],
                               in0=xn_fm[:, buf, dc, 3:515],
                               scalar1=dsm[:, omcol, dc:dc + 1], scalar2=None,
                               op0=ALU.mult),
                           waits=mw, tag=f"{tag}mx{g}_{dc}_{u}")
                        mix_m = dv(lambda e, dc=dc, u=u, mucol=mucol, buf=buf:
                                   e.scalar_tensor_tensor(
                                       out=xrxk[:, u, dc, :],
                                       in0=xn_fm[:, buf, dc, 4:516],
                                       scalar=smalls[:, mucol, dc:dc + 1],
                                       in1=xrxk[:, u, dc, :],
                                       op0=ALU.mult, op1=ALU.add),
                                   waits=[tsm], tag=f"{tag}mxs{g}_{dc}_{u}")
                st_["mix_prev"] = mix_m
                gstate[g] = {"mix_m": mix_m}

            def s3(g):
                gs = gstate[g]
                sig_m = []
                ab_m = None
                for ec in range(NC):
                    wstep()
                    bnk = ec % 2
                    mms = [(warena[:, 0, dc, 128 * ec:128 * ec + 128],
                            xrxk[:, 0, dc, :]) for dc in range(NC)]
                    gm = mm_group(bnk, mms, waits=[gs["mix_m"], w_r],
                                  tag=f"{tag}A{g}_{ec}")
                    sig_m.append(evac(bnk, "scalar",
                                      lambda e, bnk=bnk, ec=ec: e.activation(
                                          out=r_fm[:, ec, :], in_=pb[bnk].ap(),
                                          func=AF.Sigmoid),
                                      waits=[gm] + ([st_["rc_prev"]] if st_["rc_prev"] else []),
                                      tag=f"{tag}Aev{g}_{ec}"))
                for ec in range(NC):
                    bnk = ec % 2
                    mms = [(warena[:, 1, dc, 128 * ec:128 * ec + 128],
                            xrxk[:, 1, dc, :]) for dc in range(NC)]
                    gm = mm_group(bnk, mms, waits=[w_k], tag=f"{tag}B{g}_{ec}")
                    ab_m = gm
                    evac(bnk, "scalar",
                         lambda e, bnk=bnk, ec=ec: e.activation(
                             out=relu_b[:, ec, :], in_=pb[bnk].ap(),
                             func=AF.Relu),
                         waits=[gm] + ([st_["c_prev"]] if st_["c_prev"] else []),
                         tag=f"{tag}Bev{g}_{ec}")
                st_["ab_prev"] = ab_m
                if g == NG - 1 and nxt:
                    arm(nxt[0], slot_free_gate())
                    arm(nxt[1], slot_free_gate())
                relu_m = p_.cur("sAC")
                k2_m = dv(lambda e: e.tensor_tensor(
                    out=relu_b[:, :, :], in0=relu_b[:, :, :],
                    in1=relu_b[:, :, :], op=ALU.mult),
                    waits=[relu_m], tag=f"{tag}k2{g}")
                gs["k2_m"] = k2_m
                gs["sig_m"] = sig_m

            def s4(g):
                gs = gstate[g]
                rc_m = []
                cm = None
                for ec in range(NC):
                    wstep()
                    bnk = ec % 2
                    mms = [(warena[:, 2, dc, 128 * ec:128 * ec + 128],
                            relu_b[:, dc, :]) for dc in range(NC)]
                    gm = mm_group(bnk, mms, waits=[gs["k2_m"], w_v],
                                  tag=f"{tag}C{g}_{ec}")
                    cm = gm
                    rc_m.append(evac(bnk, "vector",
                                     lambda e, bnk=bnk, ec=ec: e.tensor_tensor(
                                         out=rc_fm[:, ec, :], in0=pb[bnk].ap(),
                                         in1=r_fm[:, ec, :], op=ALU.mult),
                                     waits=[gm, gs["sig_m"][ec]],
                                     tag=f"{tag}Cev{g}_{ec}"))
                st_["c_prev"] = cm
                st_["rc_prev"] = rc_m[-1]
                if g == NG - 1 and nxt and nxt[2]:
                    arm(nxt[2], slot_free_gate())
                for t in range(4):
                    ra_ev = None
                    for half in range(2):
                        bnk = 2 + half
                        trs = []
                        for eci in range(4):
                            ec = 4 * half + eci
                            trs.append((pb[bnk].ap()[:, 128 * eci:128 * eci + 128],
                                        rc_fm[:, ec, 128 * t:128 * t + 128]))
                        tm = tr_group(bnk, trs, waits=[rc_m[-1]], dtype=F32,
                                      tag=f"{tag}rt{g}_{t}_{half}")
                        def f(e, bnk=bnk, g=g, t=t, half=half):
                            return e.tensor_tensor(
                                out=x_tm[:, 4 * g + t, 512 * half:512 * half + 512],
                                in0=pb[bnk].ap(),
                                in1=x_tm[:, 4 * g + t, 512 * half:512 * half + 512],
                                op=ALU.add)
                        ra_ev = evac(bnk, "vector", f, waits=[tm],
                                     tag=f"{tag}radd{g}_{t}_{half}")
                    if tile_hook is not None:
                        tile_hook(g, t, ra_ev)

            s12(0); s3(0)
            for g in range(1, NG):
                s12(g)
                s4(g - 1)
                s3(g)
            s4(NG - 1)

        if False and dbg and b == DBG_SEQ:
            dwv = p_.barrier_waits("sDV", "sAC", "sPE")
            def cast_dump(name, src_ap, waits):
                gp_dma(dbg_d[name], src_ap, waits=waits, tag=f"dbg{name}")
            cast_dump("stats", stats[:], dwv)
            cast_dump("xn", xn_tm[:], dwv)
            cast_dump("xpx", x_tm[:], dwv)
            cast_dump("K", K_fm, dwv)
            cast_dump("csum", colsum[:], dwv)
            cast_dump("zbuf", zbuf[:], dwv)
            cast_dump("E8", ebuf[:, 0, :], dwv)
            cast_dump("q2", warena[:, 2, :, 0:512], dwv)

        def emit_attn_b(tag, ldrs, nxt):
            pre = p_.barrier_waits("sDV", "sAC", "sPE", "sGP", "sGD")
            mrns3, _ = ln_stats(xpx=False, pre_waits=pre, tag=f"{tag}ln")
            w_qb = ldrs[0].fin()
            w_kb = ldrs[1].fin()
            w_vb = ldrs[2].fin()
            a3b = a3.ap().bitcast(BF16)
            a3f = a3.ap().bitcast(F32)
            sqkv = a3b[:, 0:12288].rearrange("p (u c s) -> p u c s",
                                             u=3, c=NC)
            Mf = a3f[:, 6144:7168].rearrange("p (c s) -> p c s", c=NC)
            Maug = a3b[:, 14336:15392].rearrange("p (c s) -> p c s", c=NC)
            zf = a3f[:, 7696:7704]
            ztmp = a3f[:, 7704:7712]
            dsc = a3f[:, 7712:7760]
            vd_b = xn_tm.ap()[:, 12, 0:128]
            tmpf = xn_tm.ap()[:, 12, :].bitcast(F32)[:, 128:256]
            adbuf_a = xn_tm.ap()[:, 13, :].bitcast(F32)
            adbuf_b = ebuf.ap()[:, 1, :].bitcast(F32)[:, 0:512]
            t1f = xn_tm.ap()[:, 14, :].bitcast(F32)
            t2f = xn_tm.ap()[:, 15, :].bitcast(F32)
            Ebuf = ebuf.ap()[:, 0, :]

            mi1 = dv(lambda e: e.memset(Mf, 0.0), waits=pre, tag=f"{tag}M0")
            mi2 = dv(lambda e: e.memset(Maug, 0.0), waits=pre, tag=f"{tag}Ma0")
            mi3 = dv(lambda e: e.memset(zf, 0.0), waits=pre, tag=f"{tag}z0")
            maug_ready = [mi1, mi2, mi3]

            tr_prev = None
            proj_war = []
            for G in range(NSEGS):
                buf = G % 2
                aw = ([tr_prev] if tr_prev else [])
                xn_mg = [ln_apply(4 * G + t, t, mrns3[G], aw,
                                  tag=f"{tag}g{G}") for t in range(4)]
                fm_m, tr_last = fm_transpose_group(G, buf, [0, 1, 2, 3],
                                                   xn_mg[-2:], tag=f"{tag}g{G}")
                tr_prev = tr_last
                ev_m = []
                for u, wm, bcol in ((0, w_qb, 4), (1, w_kb, 5), (2, w_vb, 6)):
                    for ec in range(NC):
                        bnk = ec % 2
                        mms = [(warena[:, u, dc, 128 * ec:128 * ec + 128],
                                xn_fm[:, buf, dc, 4:516]) for dc in range(NC)]
                        gm = mm_group(bnk, mms, waits=fm_m + [wm] + proj_war,
                                      tag=f"{tag}p{G}u{u}e{ec}")
                        ev_m.append(evac(bnk, "scalar",
                                    lambda e, bnk=bnk, u=u, ec=ec, bcol=bcol:
                                    e.activation(out=sqkv[:, u, ec, :],
                                                 in_=pb[bnk].ap(),
                                                 func=AF.Identity,
                                                 bias=smalls[:, bcol, ec:ec + 1]),
                                    waits=[gm], tag=f"{tag}pe{G}{u}{ec}"))
                ev_last = [ev_m[-1]]
                if G == NSEGS - 1 and nxt:
                    for L_ in nxt:
                        if L_ is not None:
                            arm(L_, slot_free_gate())
                vt_m = []
                for t in range(4):
                    bnk = 2 + t % 2
                    trs = [(pbb[bnk][:, 128 * ec:128 * ec + 128],
                            sqkv[:, 2, ec, 128 * t:128 * t + 128])
                           for ec in range(NC)]
                    tm_ = tr_group(bnk, trs, waits=ev_last,
                                   tag=f"{tag}vt{G}_{t}")
                    def f(e, bnk=bnk, t=t):
                        return e.tensor_copy(xn_tm[:, 4 + t, :],
                                             pbb[bnk][:, 0:1024])
                    vt_m.append(evac(bnk, "vector", f, waits=[tm_],
                                     tag=f"{tag}vte{G}_{t}"))
                vt_last = vt_m[-1]
                num_read_m = []
                mdelta_first = {6: True, 7: True}
                mdelta_last_m = None
                def passA_g(h):
                    omgcol = dsm[:, 3, h:h + 1]
                    adbuf = adbuf_a if h % 2 == 0 else adbuf_b
                    z2m = None
                    adm_l, pte_l = [], []
                    for lc in range(4):
                        par = lc % 4
                        Eslot = Ebuf[:, 512 * par:512 * (par + 1)]
                        sb_ = (h + lc) % 2
                        ncols = 128 * (lc + 1)
                        gm = mm_group(sb_, [(sqkv[:, 0, h, 128 * lc:128 * lc + 128],
                                             sqkv[:, 1, h, 0:ncols],
                                             pb[sb_].ap()[:, 0:ncols])],
                                      waits=ev_last + buf_war.get(("E", par), []),
                                      tag=f"{tag}s{G}_{h}_{lc}")
                        ee = ac(lambda e, sb_=sb_, ncols=ncols, Eslot=Eslot:
                                e.activation(out=Eslot[:, 0:ncols],
                                             in_=pb[sb_].ap()[:, 0:ncols],
                                             func=AF.Exp,
                                             scale=1.0 / np.sqrt(DK)),
                                waits=[gm], tag=f"{tag}e_{G}_{h}_{lc}")
                        bank_war[sb_] = ee
                        mk = dv(lambda e, lc=lc, Eslot=Eslot: e.tensor_tensor(
                            out=Eslot[:, 128 * lc:128 * lc + 128],
                            in0=Eslot[:, 128 * lc:128 * lc + 128],
                            in1=tril01[:], op=ALU.mult),
                            waits=[ee, init_gp], tag=f"{tag}mk{G}_{h}_{lc}")
                        z2m = dv(lambda e, lc=lc, ncols=ncols, Eslot=Eslot:
                                 e.tensor_reduce(dsc[:, 40 + lc:41 + lc],
                                                 Eslot[:, 0:ncols],
                                                 axis=AX.X, op=ALU.add),
                                 waits=[mk], tag=f"{tag}zb2_{G}_{h}_{lc}")
                        trs = [(pbb[2][:, 128 * j:128 * j + 128],
                                Eslot[:, 128 * j:128 * j + 128])
                               for j in range(lc + 1)]
                        ptm = tr_group(2, trs, waits=[mk], tag=f"{tag}pt{G}_{h}_{lc}")
                        pte_eng = "scalar" if lc % 2 else "vector"
                        if pte_eng == "scalar":
                            def fpt(e, lc=lc):
                                return e.copy(ptbuf[:, 0:lc + 1, :],
                                              pbb[2][:, 0:128 * (lc + 1)])
                        else:
                            def fpt(e, lc=lc):
                                return e.tensor_copy(ptbuf[:, 0:lc + 1, :],
                                                     pbb[2][:, 0:128 * (lc + 1)])
                        pte = evac(2, pte_eng, fpt,
                                   waits=[ptm] + buf_war.get("ptbuf", []),
                                   tag=f"{tag}pte{G}_{h}_{lc}")
                        pte_l.append(pte)
                        buf_war[("E", par)] = [ptm]
                        for j in range(lc + 1):
                            st = (lc == 0 and j == 0)
                            sp = (lc == 3 and j == 3)
                            def fad(e, j=j, lc=lc, h=h, st=st, sp=sp):
                                return e.matmul(
                                    pb[4].ap()[:, 128 * lc:128 * lc + 128],
                                    ptbuf[:, j, :],
                                    xn_tm[:, 4 + j, 128 * h:128 * h + 128],
                                    start=st, stop=sp)
                            w = [pte] if j == 0 else []
                            if st:
                                w = w + [vt_last] + buf_war.get("bank4", [])
                            adm = p_.add("tensor", fad, waits=w,
                                         inc=("sPE" if j == lc else None),
                                         tag=f"{tag}ad{G}_{h}_{lc}_{j}")
                        adm_l.append(adm)
                        buf_war["ptbuf"] = [adm]
                        yield
                    rzb = dv(lambda e: e.reciprocal(dsc[:, 36:40],
                                                    dsc[:, 40:44]),
                             waits=[z2m], tag=f"{tag}rz_{G}_{h}")
                    sdb = dv(lambda e, omgcol=omgcol: e.tensor_scalar(
                        out=dsc[:, 44:48], in0=dsc[:, 36:40],
                        scalar1=omgcol, scalar2=None, op0=ALU.mult),
                        waits=[rzb] + buf_war.get("zb_sd", []),
                        tag=f"{tag}sd_{G}_{h}")
                    ads = None
                    for lc in range(4):
                        ads = evac(4, "scalar",
                                   lambda e, lc=lc: e.activation(
                                       out=adbuf[:, 128 * lc:128 * lc + 128],
                                       in_=pb[4].ap()[:, 128 * lc:128 * lc + 128],
                                       func=AF.Identity,
                                       scale=dsc[:, 44 + lc:45 + lc]),
                                   waits=[adm_l[3], sdb] +
                                         buf_war.get(("ad", h % 2, lc), []),
                                   tag=f"{tag}ads{G}_{h}_{lc}")
                    buf_war["zb_sd"] = [ads]
                    buf_war["bank4"] = [ads]
                    sc_pe[h] = p_.cur("sPE")

                def passB_g(h):
                    nonlocal mdelta_last_m
                    gcol = dsm[:, 2, h:h + 1]
                    adbuf = adbuf_a if h % 2 == 0 else adbuf_b
                    sc_m = sc_pe[h]
                    t12 = xn_tm.ap()[:, 14:16, :].bitcast(F32)
                    squ = sqkv[:, 0:2, h, :]
                    a1 = ac(lambda e, h=h: e.activation(
                        out=t12, in_=squ, func=AF.Relu, scale=-1.0),
                        waits=[("sPE", sc_m[1])] + buf_war.get("t1f", []),
                        tag=f"{tag}tr1_{G}_{h}")
                    a2 = ac(lambda e: e.activation(out=t12, in_=t12,
                                                   func=AF.Exp, scale=-1.0),
                            waits=[a1], tag=f"{tag}tr2_{G}_{h}")
                    el = dv(lambda e, h=h: e.scalar_tensor_tensor(
                        out=squ, in0=squ, scalar=0.0, in1=t12,
                        op0=ALU.max, op1=ALU.add),
                        waits=[a2], tag=f"{tag}tr4_{G}_{h}")
                    buf_war["t1f"] = [el]
                    el_m = [el, el]
                    yield
                    sktr = tr_group(3, [(pbb[3][:, 128 * t:128 * t + 128],
                                         sqkv[:, 1, h, 128 * t:128 * t + 128])
                                        for t in range(4)],
                                    waits=[el_m[1]], tag=f"{tag}skt{G}_{h}")
                    skte = evac(3, "vector",
                                lambda e, h=h: e.tensor_copy(
                                    xn_tm.ap()[:, 8:12, 128 * h:128 * h + 128],
                                    pbb[3][:, 0:512].rearrange(
                                        "p (t c) -> p t c", t=4)),
                                waits=[sktr], tag=f"{tag}skte{G}_{h}")
                    zt_hold[0] = dv(lambda e, h=h: e.tensor_reduce(
                        ztmp[:, h:h + 1], sqkv[:, 1, h, :], axis=AX.X,
                        op=ALU.add), waits=[el_m[1]], tag=f"{tag}zt{G}_{h}")
                    yield
                    denm = mm_group(5, [(sqkv[:, u, h, 128 * lc:128 * lc + 128],
                                         Maug[:, h, 128:129],
                                         pb[5].ap()[:, 2 * lc + u:2 * lc + u + 1])
                                        for lc in range(4) for u in range(2)],
                                    waits=maug_ready + [el_m[1]],
                                    tag=f"{tag}den{G}_{h}")
                    num_read_m.append(denm)
                    dev = evac(5, "scalar", lambda e: e.activation(
                        out=dsc[:, 16:24], in_=pb[5].ap()[:, 0:8],
                        func=AF.Identity, bias=bepscol[:, 0:1]),
                        waits=[denm] + buf_war.get("dsc16", []),
                        tag=f"{tag}dev{G}_{h}")
                    rcp = dsc[:, 24:32].rearrange("p (l u) -> p l u", u=2)
                    rbm = dv(lambda e: e.reciprocal(dsc[:, 24:32],
                                                    dsc[:, 16:24]),
                             waits=[dev], tag=f"{tag}rb{G}_{h}")
                    buf_war["dsc16"] = [rbm]
                    gqb = dv(lambda e, gcol=gcol: e.tensor_scalar(
                        out=dsc[:, 32:36], in0=rcp[:, :, 0:1],
                        scalar1=gcol, scalar2=None, op0=ALU.mult),
                        waits=[rbm], tag=f"{tag}gq{G}_{h}")
                    yield
                    c1_last = None
                    for lc in range(4):
                        nb = 3 if lc % 2 == 0 else 5
                        nq = [(sqkv[:, 0, h, 128 * lc:128 * lc + 128],
                               Maug[:, h, 0:128], pb[nb].ap()[:, 0:128]),
                              (sqkv[:, 1, h, 128 * lc:128 * lc + 128],
                               Maug[:, h, 0:128], pb[nb].ap()[:, 128:256])]
                        nm = mm_group(nb, nq, waits=maug_ready + [el_m[1], skte],
                                      tag=f"{tag}nm{G}_{h}_{lc}")
                        num_read_m.append(nm)
                        vdp = xn_tm.ap()[:, 12, 128 * (lc % 2):
                                         128 * (lc % 2) + 128]
                        vdm = dv(lambda e, h=h, lc=lc, vdp=vdp, nb=nb:
                                 e.scalar_tensor_tensor(
                                     out=vdp, in0=pb[nb].ap()[:, 128:256],
                                     scalar=rcp[:, lc, 1:2],
                                     in1=xn_tm[:, 4 + lc, 128 * h:128 * h + 128],
                                     op0=ALU.mult, op1=ALU.subtract),
                                 waits=[nm, gqb] + buf_war.get(("vd", lc % 2), []),
                                 tag=f"{tag}vd_{G}_{h}_{lc}")
                        mb = 6 + h // 4
                        def fmd(e, h=h, lc=lc, mb=mb, vdp=vdp,
                                st=mdelta_first[mb]):
                            return e.matmul(
                                pb[mb].ap()[:, 128 * (h % 4):128 * (h % 4) + 128],
                                xn_tm[:, 8 + lc, 128 * h:128 * h + 128],
                                vdp, start=st,
                                stop=(h % 4 == 3 and lc == 3))
                        mw = [vdm, skte]
                        if mdelta_first[mb] and mb in bank_war and bank_war[mb]:
                            mw.append(bank_war[mb])
                        mdelta_last_m = p_.add("tensor", fmd, waits=mw,
                                               inc="sPE",
                                               tag=f"{tag}md{G}_{h}_{lc}")
                        mdelta_first[mb] = False
                        buf_war[("vd", lc % 2)] = [mdelta_last_m]
                        c1_last = dv(lambda e, h=h, lc=lc, nb=nb:
                                     e.scalar_tensor_tensor(
                                         out=adbuf[:, 128 * lc:128 * lc + 128],
                                         in0=pb[nb].ap()[:, 0:128],
                                         scalar=dsc[:, 32 + lc:33 + lc],
                                         in1=adbuf[:, 128 * lc:128 * lc + 128],
                                         op0=ALU.mult, op1=ALU.add),
                                     waits=[vdm], tag=f"{tag}c1_{G}_{h}_{lc}")
                        bank_war[nb] = c1_last
                        yield
                    c2 = dv(lambda e, G=G, h=h, adbuf=adbuf: e.tensor_tensor(
                        out=x_tm[:, 4 * G:4 * G + 4, 128 * h:128 * h + 128],
                        in0=adbuf.rearrange("p (l c) -> p l c", l=4),
                        in1=x_tm[:, 4 * G:4 * G + 4, 128 * h:128 * h + 128],
                        op=ALU.add), waits=[c1_last],
                        tag=f"{tag}c2_{G}_{h}")
                    for lc in range(4):
                        buf_war[("ad", h % 2, lc)] = [c2]
                sc_pe = {}
                zt_hold = [None]
                for _ in passA_g(0):
                    pass
                for h in range(H):
                    wstep()
                    gens = [passB_g(h)]
                    if h + 1 < H:
                        gens.append(passA_g(h + 1))
                    while gens:
                        for g_ in list(gens):
                            try:
                                next(g_)
                            except StopIteration:
                                gens.remove(g_)
                if n_sub >= 4:
                    for lc in range(4):
                        stats_tile(4 * G + lc, buf_war[("ad", 1, lc)],
                                   f"{tag}pf_")
                zu = dv(lambda e: e.tensor_tensor(out=zf, in0=zf, in1=ztmp,
                                                  op=ALU.add),
                        waits=[zt_hold[0]] + num_read_m[-1:], tag=f"{tag}zu{G}")
                zc = dv(lambda e: e.tensor_copy(
                    Maug[:, :, 128:129],
                    zf.rearrange("p (c o) -> p c o", o=1)),
                        waits=[zu], tag=f"{tag}zc{G}")
                m1_ = dv(lambda e: e.tensor_tensor(out=Mf[:, 0:4, :],
                                                   in0=Mf[:, 0:4, :],
                                                   in1=pb[6].ap(),
                                                   op=ALU.subtract),
                         waits=[mdelta_last_m], tag=f"{tag}mu{G}a")
                m2_ = dv(lambda e: e.tensor_tensor(out=Mf[:, 4:8, :],
                                                   in0=Mf[:, 4:8, :],
                                                   in1=pb[7].ap(),
                                                   op=ALU.subtract),
                         waits=[m1_], tag=f"{tag}mu{G}b")
                bank_war[6] = m2_
                bank_war[7] = m2_
                mc = dv(lambda e: e.tensor_copy(Maug[:, :, 0:128], Mf),
                        waits=[m2_] + num_read_m[-1:], tag=f"{tag}mc{G}")
                maug_ready = [mc, zc]
                proj_war = [p_.cur("sPE"), p_.cur("sDV")]

        ab_ldrs = None
        if n_sub >= 3:
            ab_ldrs = (LoadW(wqb_d, 0, 0, f"b{b}ab_ldwq"),
                       LoadW(wkb_d, 1, 1, f"b{b}ab_ldwk"),
                       LoadW(wvb_d, 2, 0, f"b{b}ab_ldwv"))
        f2_ldrs = None
        if n_sub >= 4:
            f2_ldrs = (LoadW(wrT_d, 0, 0, f"b{b}f2_ldwr"),
                       LoadW(wkfT_d, 1, 1, f"b{b}f2_ldwk"),
                       LoadW(wvfT_d, 2, 0, f"b{b}f2_ldwv"))
        nseq = None
        if n_sub >= 4 and b + 1 < BL:
            carry_qa = LoadW(wqaT_d, 0, 0, f"b{b + 1}ldwqa")
            carry_ka = LoadW(wkaT_d, 1, 1, f"b{b + 1}ldwka")
            nseq = (carry_qa, carry_ka, None)
        def f2_hook(g, t, m, b=b):
            i = 4 * g + t
            otm = out_d[b].rearrange("(t p) d -> p t d", p=P)
            sy_dma(otm[:, i:i + 1, :], x_tm[:, i:i + 1, :], waits=[m],
                   store=True, tag=f"st{b}_{i}")
            if b + 1 < BL:
                itm = x_d[b + 1].rearrange("(t p) d -> p t d", p=P)
                ld = sy_dma(x_tm[:, i:i + 1, :], itm[:, i:i + 1, :],
                            tag=f"ldx{b + 1}_{i}")
                stats_tile(i, [ld], f"b{b + 1}ln1_")

        f1_hook = None
        if n_sub >= 3:
            def f1_hook(g, t, m):
                stats_tile(4 * g + t, [m], f"b{b}ab_")
        if n_sub >= 2:
            emit_ffn(f"b{b}f1_", f1_ldrs, ab_ldrs, tile_hook=f1_hook,
                     dbg_g0=(dbg and b == DBG_SEQ))
        if n_sub >= 3:
            emit_attn_b(f"b{b}ab_", ab_ldrs, f2_ldrs)
        if n_sub >= 4:
            emit_ffn(f"b{b}f2_", f2_ldrs, nseq, tile_hook=f2_hook)
            NSTORE += NT
        assert not armed or (n_sub >= 4 and b + 1 < BL), \
            f"unarmed/armed leftovers at b={b}: {[L.tag for L in armed]}"

        if n_sub < 4:
            st_w = p_.barrier_waits("sDV", "sAC", "sPE")
            sy_dma(out_d[b].rearrange("(t p) d -> p t d", p=P), x_tm[:],
                   waits=st_w, store=True, tag=f"st{b}")
            NSTORE += 1

    p_.add("sync", lambda e: e.nop(),
           waits=[("sST", 16 * NSTORE), p_.cur("sGD"), p_.cur("sXD")],
           tag="final")

    p_.simulate()

    with ctx:
        blk = ctx.enter_context(nc.Block())

        def make_body(eng_name):
            def body(eng):
                for (fn, w, incs, tag) in p_.ops[eng_name]:
                    for (s, v) in w:
                        eng.wait_ge(sems[s], v)
                    inst = fn(eng)
                    for (s, v, a) in incs:
                        if a:
                            inst = inst.then_inc(sems[s], v)
            return body

        blk.sync(make_body("sync"))
        blk.gpsimd(make_body("gpsimd"))
        blk.tensor(make_body("tensor"))
        blk.vector(make_body("vector"))
        blk.scalar(make_body("scalar"))

    return nc


def _host_layout(inputs):
    g = lambda k: np.asarray(inputs[k])
    smalls = np.zeros((P, 8, NC), dtype=np.float32)
    def fm(v):
        return np.ascontiguousarray(v.reshape(NC, P).T)
    smalls[:, 0, :] = fm(g("L0_attn_a_bq"))
    smalls[:, 1, :] = fm(g("L0_attn_a_bk"))
    smalls[:, 2, :] = fm(g("L0_ffn_mu_r"))
    smalls[:, 3, :] = fm(g("L0_ffn_mu_k"))
    smalls[:, 4, :] = fm(g("L0_attn_b_bq"))
    smalls[:, 5, :] = fm(g("L0_attn_b_bk"))
    smalls[:, 6, :] = fm(g("L0_attn_b_bv"))
    smalls[:, 7, :] = np.broadcast_to(g("L0_attn_b_beta")[None, :], (P, 8))
    T = lambda a: np.ascontiguousarray(a.T)
    return {
        "wqaT": T(g("L0_attn_a_Wq")), "wkaT": T(g("L0_attn_a_Wk")),
        "wrT": T(g("L0_ffn_W_r")), "wkfT": T(g("L0_ffn_W_k")),
        "wvfT": T(g("L0_ffn_W_v")),
        "wqb": np.ascontiguousarray(g("L0_attn_b_Wq")),
        "wkb": np.ascontiguousarray(g("L0_attn_b_Wk")),
        "wvb": np.ascontiguousarray(g("L0_attn_b_Wv")),
        "smalls": smalls,
    }


N_SUB = 4

def kernel(**inputs):
    x = np.ascontiguousarray(np.asarray(inputs["L0_attn_a_x"], dtype=np.float32))
    shared = _host_layout(inputs)
    nc = build_nc(n_sub=N_SUB)
    in_maps = []
    for i in range(NCORES):
        m = dict(shared)
        m["x"] = np.ascontiguousarray(x[BL * i:BL * (i + 1)])
        in_maps.append(m)
    res = run_bass_kernel_spmd(nc, in_maps, core_ids=list(range(NCORES)))
    return np.concatenate([r["out"] for r in res.results], axis=0)


if __name__ == "__main__":
    nc = build_nc()
    print("built OK")


# version 14
# speedup vs baseline: 1.2669x; 1.0112x over previous
import numpy as np
from contextlib import ExitStack

import concourse.bass as bass
import concourse.mybir as mybir
from concourse.bass_utils import run_bass_kernel_spmd

F32 = mybir.dt.float32
BF16 = mybir.dt.bfloat16
AF = mybir.ActivationFunctionType
ALU = mybir.AluOpType
AX = mybir.AxisListType

P = 128
B, S, D = 32, 2048, 1024
NCORES = 8
BL = B // NCORES
NT = S // P
NC = D // P
NG = S // 512
H, DK, DV, SEG, NSEGS = 8, 128, 128, 512, 4
LN_EPS = 1e-5
B_EPS = 1e-6

ENGS = ("sync", "gpsimd", "tensor", "vector", "scalar")


class Plan:

    def __init__(self):
        self.ops = {e: [] for e in ENGS}
        self.cnt = {s: 0 for s in ("sPE", "sDV", "sAC", "sGP", "sXD", "sGD", "sST")}
        self.seen = {e: {} for e in ENGS}
        self.eng_sem = {"tensor": "sPE", "vector": "sDV", "scalar": "sAC",
                        "gpsimd": "sGP"}

    def add(self, eng, fn, waits=(), inc=None, n=1, tag="", attach=True):
        w = []
        for wv in waits:
            if wv is None:
                continue
            (s, v) = wv
            if v <= 0:
                continue
            if self.seen[eng].get(s, 0) >= v:
                continue
            self.seen[eng][s] = v
            w.append((s, v))
        incs = []
        if inc is not None:
            self.cnt[inc] += n
            incs.append((inc, n, attach))
        self.ops[eng].append((fn, w, incs, tag))
        return (inc, self.cnt[inc]) if inc is not None else None

    def cur(self, s):
        return (s, self.cnt[s])

    def barrier_waits(self, *sems):
        return [(s, self.cnt[s]) for s in sems]

    def simulate(self):
        pos = {e: 0 for e in ENGS}
        val = {s: 0 for s in self.cnt}
        progress = True
        while progress:
            progress = False
            for e in ENGS:
                while pos[e] < len(self.ops[e]):
                    fn, w, incs, tag = self.ops[e][pos[e]]
                    if any(val[s] < v for (s, v) in w):
                        break
                    for (s, v, _a) in incs:
                        val[s] += v
                    pos[e] += 1
                    progress = True
        stuck = {e: pos[e] for e in ENGS if pos[e] < len(self.ops[e])}
        if stuck:
            msg = []
            for e, p_ in stuck.items():
                fn, w, incs, tag = self.ops[e][p_]
                msg.append(f"{e}@{p_}/{len(self.ops[e])} tag={tag} waits={w} "
                           f"sems={[(s, val[s]) for s, _ in w]}")
            raise RuntimeError("PLAN DEADLOCK:\n" + "\n".join(msg))


FFN_LEVEL = 3
DBG_SEQ = 0

def build_nc(n_sub=4, dbg=False, ffn_level=None):
    if ffn_level is None:
        ffn_level = FFN_LEVEL
    nc = bass.Bass("TRN2", target_bir_lowering=False,
                   dynamic_dma_scratch_size=8192)

    x_d = nc.dram_tensor("x", [BL, S, D], F32, kind="ExternalInput").ap()
    out_d = nc.dram_tensor("out", [BL, S, D], F32, kind="ExternalOutput").ap()
    wqaT_d = nc.dram_tensor("wqaT", [D, D], F32, kind="ExternalInput").ap()
    wkaT_d = nc.dram_tensor("wkaT", [D, D], F32, kind="ExternalInput").ap()
    wrT_d = nc.dram_tensor("wrT", [D, D], F32, kind="ExternalInput").ap()
    wkfT_d = nc.dram_tensor("wkfT", [D, D], F32, kind="ExternalInput").ap()
    wvfT_d = nc.dram_tensor("wvfT", [D, D], F32, kind="ExternalInput").ap()
    wqb_d = nc.dram_tensor("wqb", [D, D], F32, kind="ExternalInput").ap()
    wkb_d = nc.dram_tensor("wkb", [D, D], F32, kind="ExternalInput").ap()
    wvb_d = nc.dram_tensor("wvb", [D, D], F32, kind="ExternalInput").ap()
    smalls_d = nc.dram_tensor("smalls", [P, 8, NC], F32, kind="ExternalInput").ap()
    if dbg:
        dbg_d = {k: nc.dram_tensor(f"dbg_{k}", shp, F32, kind="ExternalOutput").ap()
                 for k, shp in [("stats", [P, NT, 8]), ("xn", [P, NT, D]),
                                ("xpx", [P, NT, D]), ("K", [P, NC, S]),
                                ("csum", [1, D]), ("zbuf", [P, NT, 2]),
                                ("E8", [P, S]), ("q2", [P, NC, 512]),
                                ("xrxk", [P, 2, NC, 512]), ("r", [P, NC, 512]),
                                ("k2", [P, NC, 512]), ("rc", [P, NC, 512]),
                                ("xnfm", [P, 2, NC, 516])]}

    ctx = ExitStack()
    sb = lambda *a: ctx.enter_context(nc.sbuf_tensor(*a))
    ps = lambda *a: ctx.enter_context(nc.psum_tensor(*a))

    x_tm = sb("x_tm", [P, NT, D], F32)
    xn_tm = sb("xn_tm", [P, NT, D], BF16)
    xn_fm = sb("xn_fm", [P, 2, NC, 516], BF16)
    a3 = sb("a3", [P, 32768], mybir.dt.uint8)
    warena = sb("warena", [P, 3, NC, D], BF16)
    ebuf = sb("ebuf", [P, 2, S], BF16)
    ptbuf = sb("ptbuf", [P, 17, P], BF16)
    stats = sb("stats", [P, NT, 8], F32)
    smalls = sb("smalls_s", [P, 8, NC], F32)
    dsm = sb("dsm", [P, 8, NC], F32)
    identb = sb("identb", [P, P], BF16)
    tril01 = sb("tril01", [P, P], BF16)
    identf = sb("identf", [P, P], F32)
    onescol = sb("onescol", [P, 1], BF16)
    epscol = sb("epscol", [P, 1], F32)
    bepscol = sb("bepscol", [P, 1], F32)
    zbuf = sb("zbuf", [P, NT, 2], F32)
    zpart = sb("zpart", [P, 8], F32)
    nzcol = sb("nzcol", [P, 2], BF16)
    zrow = sb("zrow", [1, 2, P], BF16)
    colsum = sb("colsum", [1, D], BF16)
    wst0 = sb("wst0", [P, 1024], F32)
    junkb = sb("junkb", [P, 1024], BF16)

    K_fm = a3.ap().bitcast(BF16).rearrange("p (c s) -> p c s", c=NC)
    junk = ebuf.ap().bitcast(F32)[:, 0, :]

    pb = [ps(f"pb{i}", [P, 512], F32) for i in range(8)]
    pbb = [pb[i].ap().bitcast(BF16) for i in range(8)]

    sems = {}
    for s in ("sPE", "sDV", "sAC", "sGP", "sXD", "sGD", "sST"):
        sems[s] = ctx.enter_context(nc.semaphore(s))

    p_ = Plan()
    bank_war = {}
    buf_war = {}

    def mm_group(bank, mms, waits=(), tag=""):
        w = list(waits)
        if bank in bank_war and bank_war[bank]:
            w.append(bank_war[bank])
        last = None
        n = len(mms)
        for i, m in enumerate(mms):
            lhsT, rhs = m[0], m[1]
            out = m[2] if len(m) > 2 else pb[bank].ap()
            st, sp = (i == 0), (i == n - 1)
            def f(e, out=out, lhsT=lhsT, rhs=rhs, st=st, sp=sp):
                return e.matmul(out, lhsT, rhs, start=st, stop=sp)
            last = p_.add("tensor", f, waits=(w if i == 0 else ()),
                          inc=("sPE" if sp else None), tag=tag)
        return last

    def tr_group(bank, trs, waits=(), dtype=BF16, tag=""):
        w = list(waits)
        if bank in bank_war and bank_war[bank]:
            w.append(bank_war[bank])
        ident = identb[:] if dtype == BF16 else identf[:]
        last = None
        for i, (out, in_) in enumerate(trs):
            def f(e, out=out, in_=in_, ident=ident):
                return e.transpose(out, in_, ident)
            last = p_.add("tensor", f, waits=(w if i == 0 else ()),
                          inc=("sPE" if i == len(trs) - 1 else None), tag=tag)
        return last

    def evac(bank, eng, fn, waits=(), tag=""):
        sem = p_.eng_sem[eng]
        mile = p_.add(eng, fn, waits=waits, inc=sem, tag=tag)
        bank_war[bank] = mile
        return mile

    def dv(fn, waits=(), tag="", inc=True):
        return p_.add("vector", fn, waits=waits, inc=("sDV" if inc else None), tag=tag)

    def ac(fn, waits=(), tag="", inc=True):
        return p_.add("scalar", fn, waits=waits, inc=("sAC" if inc else None), tag=tag)

    def gp(fn, waits=(), tag="", inc=True):
        return p_.add("gpsimd", fn, waits=waits, inc=("sGP" if inc else None), tag=tag)

    def sy_dma(out, in_, waits=(), store=False, tag=""):
        waits = list(waits) + [p_.cur("sXD"), p_.cur("sST")]
        def f(e, out=out, in_=in_):
            return e.dma_start(out=out, in_=in_)
        return p_.add("sync", f, waits=waits, inc=("sST" if store else "sXD"),
                      n=16, tag=tag)

    def hard_barrier(tag=""):
        arr = {}
        for e in ("tensor", "vector", "scalar", "gpsimd"):
            s = p_.eng_sem[e]
            def mk(s=s):
                return lambda en: en.sem_inc(sems[s], 1)
            arr[e] = p_.add(e, mk(), inc=s, tag=f"{tag}barA", attach=False)
        dmaw = [p_.cur("sGD"), p_.cur("sXD"), p_.cur("sST")]
        allw = list(arr.values()) + dmaw
        for e in ("tensor", "vector", "scalar", "gpsimd", "sync"):
            p_.add(e, (lambda en: en.wait_ge(sems["sGP"], 1)), waits=allw,
                   tag=f"{tag}barW")

    def sc_dma(out, in_, waits=(), tag=""):
        waits = list(waits) + [p_.cur("sGD")]
        def f(e, out=out, in_=in_):
            return e.dma_start(out=out, in_=in_)
        return p_.add("scalar", f, waits=waits, inc="sGD", n=16, tag=tag)

    def gp_dma(out, in_, waits=(), tag=""):
        waits = list(waits) + [p_.cur("sGD")]
        def f(e, out=out, in_=in_):
            return e.dma_start(out=out, in_=in_)
        return p_.add("gpsimd", f, waits=waits, inc="sGD", n=16, tag=tag)

    sy_dma(smalls[:], smalls_d, tag="ld_smalls")
    sm_ready = p_.cur("sXD")

    m1 = gp(lambda e: e.memset(identb[:], 0.0), tag="identb0")
    gp(lambda e: e.affine_select(out=identb[:], in_=identb[:],
                                 compare_op=ALU.not_equal, fill=1.0, base=0,
                                 pattern=[[-1, P]], channel_multiplier=1),
       waits=[m1], tag="identb")
    mt = gp(lambda e: e.memset(tril01[:], 1.0), tag="tril0")
    gp(lambda e: e.affine_select(out=tril01[:], in_=tril01[:],
                                 compare_op=ALU.is_ge, fill=0.0, base=0,
                                 pattern=[[-1, P]], channel_multiplier=1),
       waits=[mt], tag="tril")
    m2 = gp(lambda e: e.memset(identf[:], 0.0), tag="identf0")
    gp(lambda e: e.affine_select(out=identf[:], in_=identf[:],
                                 compare_op=ALU.not_equal, fill=1.0, base=0,
                                 pattern=[[-1, P]], channel_multiplier=1),
       waits=[m2], tag="identf")
    gp(lambda e: e.memset(onescol[:], 1.0), tag="ones")
    gp(lambda e: e.memset(epscol[:], LN_EPS), tag="eps")
    gp(lambda e: e.memset(bepscol[:], B_EPS), tag="beps")
    gp(lambda e: e.memset(xn_fm[:], 0.0), tag="fmz")
    init_gp = p_.cur("sGP")

    dv(lambda e: e.tensor_scalar(out=dsm[:, 0, :], in0=smalls[:, 2, :],
                                 scalar1=-1.0, scalar2=1.0, op0=ALU.mult,
                                 op1=ALU.add), waits=[sm_ready], tag="onemur")
    dv(lambda e: e.tensor_scalar(out=dsm[:, 1, :], in0=smalls[:, 3, :],
                                 scalar1=-1.0, scalar2=1.0, op0=ALU.mult,
                                 op1=ALU.add), tag="onemuk")
    ac(lambda e: e.activation(out=dsm[:, 2, :], in_=smalls[:, 7, :],
                              func=AF.Sigmoid), waits=[sm_ready], tag="gate")
    gate_done = p_.cur("sAC")
    dv(lambda e: e.tensor_scalar(out=dsm[:, 3, :], in0=dsm[:, 2, :],
                                 scalar1=-1.0, scalar2=1.0, op0=ALU.mult,
                                 op1=ALU.add), waits=[gate_done], tag="onegate")
    dv(lambda e: e.tensor_scalar(out=dsm[:, 6, :], in0=smalls[:, 4, :],
                                 scalar1=-1.0, scalar2=None, op0=ALU.mult),
       tag="negbq")
    dv(lambda e: e.tensor_scalar(out=dsm[:, 7, :], in0=smalls[:, 5, :],
                                 scalar1=-1.0, scalar2=None, op0=ALU.mult),
       tag="negbk")
    init_dv = p_.cur("sDV")
    init_ac = p_.cur("sAC")

    armed = []

    class LoadW:
        def __init__(self, dram_ap, slot, q=0, tag=""):
            self.dram_ap, self.slot, self.q, self.tag = dram_ap, slot, q, tag
            self.c = NC - 1
            self.gate = []
            self.last = None
            self.mile = None

        def step(self):
            if self.c < 0:
                return
            c = self.c
            self.c -= 1
            dma = sy_dma if self.q == 0 else sc_dma
            wf = warena.ap()[:, self.slot, :, :].rearrange(
                "p c e -> p (c e)").bitcast(F32)
            stage = wf[:, 0:1024] if c >= 2 else wst0.ap()
            w = list(self.gate) + ([self.last] if self.last else [])
            if c < 2:
                w = w + buf_war.get("wst", [])
            ld = dma(stage, self.dram_ap[128 * c:128 * (c + 1), :], waits=w,
                     tag=f"{self.tag}c{c}")
            if self.q == 0:
                self.last = ac(lambda e, slot=self.slot, c=c, stage=stage:
                               e.activation(out=warena[:, slot, c, :],
                                            in_=stage, func=AF.Identity),
                               waits=[ld], tag=f"{self.tag}x{c}")
            else:
                self.last = dv(lambda e, slot=self.slot, c=c, stage=stage:
                               e.tensor_copy(warena[:, slot, c, :], stage),
                               waits=[ld], tag=f"{self.tag}x{c}")
            if c < 2:
                buf_war["wst"] = [self.last]
            if self.c < 0:
                self.mile = self.last

        def fin(self):
            while self.c >= 0:
                self.step()
            if self in armed:
                armed.remove(self)
            return self.mile

    def arm(ldr, gate):
        ldr.gate = list(gate)
        armed.append(ldr)

    def wstep():
        for ldr in list(armed):
            ldr.step()
            if ldr.c < 0:
                armed.remove(ldr)

    slot_free_gate = lambda: p_.barrier_waits("sPE", "sAC", "sDV")

    NSTORE = 0
    carry_qa = carry_ka = None
    stats_done = {}
    stats_guard = {}

    for b in range(BL):
        if n_sub >= 4 and b > 0:
            x_ready = []
        else:
            xw = [p_.cur("sST")]
            x_mile = sy_dma(x_tm[:], x_d[b].rearrange("(t p) d -> p t d", p=P),
                            waits=xw, tag=f"ldx{b}")
            x_ready = [x_mile]

        def stats_tile(i, waits, tag):
            w = list(waits) + stats_guard.get(i // 4, [])
            s1m = dv(lambda e, i=i: e.tensor_reduce(
                stats[:, i, 0:1], x_tm[:, i, :], axis=AX.X, op=ALU.add),
                waits=w, tag=f"{tag}S1_{i}")
            s2m = ac(lambda e, i=i: e.activation(
                out=junkb.ap(), in_=x_tm[:, i, :], func=AF.Square,
                accum_out=stats[:, i, 1:2]),
                waits=w + buf_war.get("junkb", []), tag=f"{tag}S2_{i}")
            buf_war["junkb"] = [s2m]
            stats_done[i] = (s1m, s2m)

        def ln_stats(xpx, pre_waits, tag):
            for i in range(NT):
                if i not in stats_done:
                    stats_tile(i, pre_waits + [sm_ready], tag)
            mrns, oprs = [], []
            for g4 in range(4):
                sl = slice(4 * g4, 4 * g4 + 4)
                sw = [m for i in range(4 * g4, 4 * g4 + 4)
                      for m in stats_done[i]]
                m1 = dv(lambda e, sl=sl: e.tensor_scalar(
                    out=stats[:, sl, 2:3], in0=stats[:, sl, 0:1],
                    scalar1=1.0 / D, scalar2=None, op0=ALU.mult),
                    waits=sw, tag=f"{tag}m{g4}")
                dv(lambda e, sl=sl: e.tensor_scalar(
                    out=stats[:, sl, 4:5], in0=stats[:, sl, 1:2],
                    scalar1=1.0 / D, scalar2=None, op0=ALU.mult),
                    tag=f"{tag}s2n{g4}")
                s2 = dv(lambda e, sl=sl: e.tensor_tensor(
                    out=stats[:, sl, 3:4], in0=stats[:, sl, 2:3],
                    in1=stats[:, sl, 2:3], op=ALU.mult),
                    waits=[m1], tag=f"{tag}m2{g4}")
                var = dv(lambda e, sl=sl: e.tensor_tensor(
                    out=stats[:, sl, 3:4], in0=stats[:, sl, 4:5],
                    in1=stats[:, sl, 3:4], op=ALU.subtract),
                    waits=[s2], tag=f"{tag}var{g4}")
                lnv = ac(lambda e, sl=sl: e.activation(
                    out=stats[:, sl, 4:5], in_=stats[:, sl, 3:4], func=AF.Ln,
                    bias=epscol[:, 0:1]),
                    waits=[var, init_gp], tag=f"{tag}ln{g4}")
                rstd = ac(lambda e, sl=sl: e.activation(
                    out=stats[:, sl, 5:6], in_=stats[:, sl, 4:5], func=AF.Exp,
                    scale=-0.5),
                    waits=[lnv], tag=f"{tag}rstd{g4}")
                mrn = dv(lambda e, sl=sl: e.scalar_tensor_tensor(
                    out=stats[:, sl, 6:7], in0=stats[:, sl, 2:3],
                    scalar=-1.0, in1=stats[:, sl, 5:6],
                    op0=ALU.mult, op1=ALU.mult),
                    waits=[rstd], tag=f"{tag}mrn{g4}")
                opr = None
                if xpx:
                    opr = dv(lambda e, sl=sl: e.tensor_scalar(
                        out=stats[:, sl, 7:8], in0=stats[:, sl, 5:6],
                        scalar1=1.0, scalar2=None, op0=ALU.add),
                        waits=[rstd], tag=f"{tag}opr{g4}")
                settle = dv(lambda e, sl=sl: e.tensor_scalar(
                    out=stats[:, sl, 4:5], in0=stats[:, sl, 6:7],
                    scalar1=0.0, scalar2=None, op0=ALU.add),
                    waits=[mrn] + ([opr] if opr else []),
                    tag=f"{tag}settle{g4}")
                stats_guard[g4] = [settle]
                mrns.append(settle)
                oprs.append(settle if xpx else None)
            stats_done.clear()
            return mrns, oprs

        def ln_apply(i, dst, mrn, extra_waits, tag):
            if i % 2 == 0:
                return ac(lambda e, i=i, dst=dst: e.activation(
                    out=xn_tm[:, dst, :], in_=x_tm[:, i, :], func=AF.Identity,
                    scale=stats[:, i, 5:6], bias=stats[:, i, 6:7]),
                    waits=[mrn] + list(extra_waits), tag=f"{tag}xn{i}")
            return dv(lambda e, i=i, dst=dst: e.tensor_scalar(
                out=xn_tm[:, dst, :], in0=x_tm[:, i, :],
                scalar1=stats[:, i, 5:6], scalar2=stats[:, i, 6:7],
                op0=ALU.mult, op1=ALU.add),
                waits=[mrn] + list(extra_waits), tag=f"{tag}xn{i}")

        def ln_xpx(i, opr, apl, tag):
            if i % 2 == 1:
                return ac(lambda e, i=i: e.activation(
                    out=x_tm[:, i, :], in_=x_tm[:, i, :], func=AF.Identity,
                    scale=stats[:, i, 7:8], bias=stats[:, i, 6:7]),
                    waits=[opr, apl], tag=f"{tag}xpx{i}")
            return dv(lambda e, i=i: e.tensor_scalar(
                out=x_tm[:, i, :], in0=x_tm[:, i, :],
                scalar1=stats[:, i, 7:8], scalar2=stats[:, i, 6:7],
                op0=ALU.mult, op1=ALU.add),
                waits=[opr, apl], tag=f"{tag}xpx{i}")

        def fm_transpose_group(g, buf, src_tiles, xn_m, tag, guard_from=None):
            miles = []
            tm = None
            for half in range(4):
                dc0 = 2 * half
                bnk = 6 + (half % 2)
                trs = []
                for dci in range(2):
                    dc = dc0 + dci
                    for t in range(4):
                        out = pbb[bnk][:, 512 * dci + 128 * t:
                                       512 * dci + 128 * t + 128]
                        in_ = xn_tm[:, src_tiles[t], 128 * dc:128 * dc + 128]
                        trs.append((out, in_))
                w = list(xn_m)
                tm = tr_group(bnk, trs, waits=w, tag=f"{tag}tr{half}")
                eng = "vector" if half % 2 == 0 else "scalar"
                if eng == "vector":
                    def f(e, bnk=bnk, buf=buf, dc0=dc0):
                        return e.tensor_copy(xn_fm[:, buf, dc0:dc0 + 2, 4:516],
                                             pbb[bnk][:, 0:1024])
                else:
                    def f(e, bnk=bnk, buf=buf, dc0=dc0):
                        return e.copy(xn_fm[:, buf, dc0:dc0 + 2, 4:516],
                                      pbb[bnk][:, 0:1024])
                miles.append(evac(bnk, eng, f, waits=[tm], tag=f"{tag}ev{half}"))
            return miles, tm

        aw_pre = p_.barrier_waits("sPE", "sAC")
        mrns, oprs = ln_stats(xpx=True,
                              pre_waits=x_ready + [p_.cur("sDV"),
                                                   p_.cur("sAC")],
                              tag=f"b{b}ln1_")
        if carry_qa is None:
            LQA = LoadW(wqaT_d, 0, 0, f"b{b}ldwqa")
            LKA = LoadW(wkaT_d, 1, 1, f"b{b}ldwka")
            arm(LQA, slot_free_gate() if b > 0 else [])
            arm(LKA, slot_free_gate() if b > 0 else [])
        else:
            LQA, LKA = carry_qa, carry_ka
        carry_qa = carry_ka = None
        if n_sub >= 2:
            f1_ldrs = (LoadW(wrT_d, 0, 0, f"b{b}f1_ldwr"),
                       LoadW(wkfT_d, 1, 1, f"b{b}f1_ldwk"),
                       LoadW(wvfT_d, 2, 1, f"b{b}f1_ldwv"))
        else:
            f1_ldrs = None
        w_ka = LKA.fin()
        xn_m = [ln_apply(i, i, mrns[i // 4], [], tag=f"b{b}ln1_")
                for i in range(NT)]
        xpx_all = [ln_xpx(i, oprs[i // 4], xn_m[i], tag=f"b{b}ln1_")
                   for i in range(NT)]
        xpx_m = xpx_all[-2:]

        kproj_m = []
        for g in range(NG):
            fm_m, _ = fm_transpose_group(g, g % 2, [4 * g + t for t in range(4)],
                                         xn_m[4 * g + 2:4 * g + 4],
                                         tag=f"b{b}a1g{g}")
            for ec in range(NC):
                bnk = ec % 2
                mms = [(warena[:, 1, dc, 128 * ec:128 * ec + 128],
                        xn_fm[:, g % 2, dc, 4:516]) for dc in range(NC)]
                gm = mm_group(bnk, mms, waits=fm_m + [w_ka], tag=f"b{b}K{g}_{ec}")
                def f(e, bnk=bnk, ec=ec, g=g):
                    return e.activation(out=K_fm[:, ec, 512 * g:512 * g + 512],
                                        in_=pb[bnk].ap(), func=AF.Identity,
                                        bias=smalls[:, 1, ec:ec + 1])
                kproj_m.append(evac(bnk, "scalar", f, waits=[gm],
                                    tag=f"b{b}Kev{g}_{ec}"))

        if f1_ldrs:
            arm(f1_ldrs[1], slot_free_gate())
        cs_m = []
        for dh in range(2):
            mms = [(onescol[:, 0:1], xn_tm[:, kc, 512 * dh:512 * dh + 512],
                    pb[6].ap()[0:1, :]) for kc in range(NT)]
            gm = mm_group(6, mms, waits=xn_m[-2:] + [init_gp],
                          tag=f"b{b}cs{dh}")
            def f(e, dh=dh):
                return e.tensor_copy(colsum[0:1, 512 * dh:512 * dh + 512],
                                     pb[6].ap()[0:1, :])
            cs_m.append(evac(6, "vector", f, waits=[gm], tag=f"b{b}csev{dh}"))

        k_all = kproj_m[-1]

        def emit_scores(c, c4, qg_m):
            eb = c % 2
            zp = 4 * (c % 2)
            exp_m = []
            for kb in range(4):
                bnk = kb % 2
                mms = [(warena[:, 2, ec, 128 * c4:128 * c4 + 128],
                        K_fm[:, ec, 512 * kb:512 * kb + 512])
                       for ec in range(NC)]
                gm = mm_group(bnk, mms,
                              waits=qg_m + [k_all] + buf_war.get(
                                  ("ebuf", eb), []),
                              tag=f"b{b}s{c}_{kb}")
                def f(e, bnk=bnk, eb=eb, kb=kb, zp=zp):
                    return e.activation(
                        out=ebuf[:, eb, 512 * kb:512 * kb + 512],
                        in_=pb[bnk].ap(), func=AF.Exp, scale=1.0 / 32.0,
                        accum_out=zpart[:, zp + kb:zp + kb + 1])
                exp_m.append(evac(bnk, "scalar", f,
                                  waits=[gm] + buf_war.get(("zpart", eb), []),
                                  tag=f"b{b}e{c}_{kb}"))
            return exp_m

        def emit_tail(c, exp_m):
            eb = c % 2
            zp = 4 * (c % 2)
            zr = dv(lambda e, c=c, zp=zp: e.tensor_reduce(
                zbuf[:, c, 0:1], zpart[:, zp:zp + 4], axis=AX.X, op=ALU.add),
                waits=[exp_m[-1]], tag=f"b{b}Z{c}")
            buf_war[("zpart", eb)] = [zr]
            rz = dv(lambda e, c=c: e.reciprocal(zbuf[:, c, 1:2],
                                                zbuf[:, c, 0:1]),
                    waits=[zr], tag=f"b{b}rZ{c}")
            nz = dv(lambda e, c=c, eb=eb: e.tensor_scalar(
                out=nzcol[:, eb:eb + 1], in0=zbuf[:, c, 0:1],
                scalar1=-2.0 / S, scalar2=None, op0=ALU.mult),
                waits=[zr] + buf_war.get(("nzcol", eb), []),
                tag=f"b{b}nz{c}")
            ztr = tr_group(6, [(pbb[6][0:1, 0:P], nzcol[:, eb:eb + 1])],
                           waits=[nz], tag=f"b{b}ztr{c}")
            buf_war[("nzcol", eb)] = [ztr]
            zev = evac(6, "vector",
                       lambda e, eb=eb: e.tensor_copy(zrow[0:1, eb, :],
                                                      pbb[6][0:1, 0:P]),
                       waits=[ztr] + buf_war.get(("zrow", eb), []),
                       tag=f"b{b}zev{c}")
            pt_m = []
            for half in range(2):
                bnk = 2 + half
                trs = []
                for kci in range(8):
                    kc = 8 * half + kci
                    trs.append((pbb[bnk][:, 128 * kci:128 * kci + 128],
                                ebuf[:, eb, 128 * kc:128 * kc + 128]))
                tm = tr_group(bnk, trs, waits=[exp_m[-1]],
                              tag=f"b{b}pt{c}_{half}")
                def f(e, bnk=bnk, half=half):
                    return e.tensor_copy(ptbuf[:, 8 * half:8 * half + 8, :],
                                         pbb[bnk][:, 0:1024])
                pt_m.append(evac(bnk, "vector", f, waits=[tm],
                                 tag=f"b{b}ptev{c}_{half}"))
            buf_war[("ebuf", eb)] = [pt_m[-1]]
            for dh in range(2):
                bnk = 4 + dh
                mms = [(ptbuf[:, kc, :],
                        xn_tm[:, kc, 512 * dh:512 * dh + 512])
                       for kc in range(NT)]
                mms.append((zrow[0:1, eb, :],
                            colsum[0:1, 512 * dh:512 * dh + 512]))
                gm = mm_group(bnk, mms,
                              waits=pt_m + [zev] + cs_m, tag=f"b{b}pv{c}_{dh}")
                buf_war[("zrow", eb)] = [gm]
                def f(e, bnk=bnk, c=c, dh=dh):
                    return e.scalar_tensor_tensor(
                        out=x_tm[:, c, 512 * dh:512 * dh + 512],
                        in0=pb[bnk].ap(), scalar=zbuf[:, c, 1:2],
                        in1=x_tm[:, c, 512 * dh:512 * dh + 512],
                        op0=ALU.mult, op1=ALU.add)
                pv_ev = evac(bnk, "vector", f, waits=[gm, rz] + xpx_m,
                             tag=f"b{b}pvev{c}_{dh}")
            if n_sub >= 2:
                stats_tile(c, [pv_ev], f"b{b}f1_")

        pend = None
        w_qa = LQA.fin()
        for gi, g in enumerate([2, 3, 0, 1]):
            if gi >= 2:
                fm_m, _ = fm_transpose_group(g, g % 2,
                                             [4 * g + t for t in range(4)],
                                             xn_m[4 * g + 2:4 * g + 4],
                                             tag=f"b{b}a2g{g}")
            else:
                fm_m = []
            qg_m = []
            for ec in range(NC):
                bnk = ec % 2
                mms = [(warena[:, 0, dc, 128 * ec:128 * ec + 128],
                        xn_fm[:, g % 2, dc, 4:516]) for dc in range(NC)]
                gm = mm_group(bnk, mms, waits=fm_m + [w_qa], tag=f"b{b}Q{g}_{ec}")
                def f(e, bnk=bnk, ec=ec):
                    return e.activation(out=warena[:, 2, ec, 0:512],
                                        in_=pb[bnk].ap(), func=AF.Identity,
                                        bias=smalls[:, 0, ec:ec + 1])
                qg_m.append(evac(bnk, "scalar", f, waits=[gm],
                                 tag=f"b{b}Qev{g}_{ec}"))
            if gi == 3 and f1_ldrs:
                arm(f1_ldrs[0], slot_free_gate())
            for c4 in range(4):
                c = 4 * g + c4
                wstep()
                em = emit_scores(c, c4, qg_m)
                if pend is not None:
                    emit_tail(*pend)
                pend = (c, em)
        if f1_ldrs:
            arm(f1_ldrs[2], slot_free_gate())
        emit_tail(*pend)
        wstep()

        def emit_ffn(tag, ldrs, nxt, tile_hook=None, dbg_g0=False):
            if ffn_level < 0:
                return
            import os as _os0
            if _os0.environ.get("FFN_PRE2"):
                pre = p_.barrier_waits("sPE", "sAC")
            else:
                pre = p_.barrier_waits("sDV", "sAC", "sPE", "sGP", "sGD")

            xrxk = a3.ap()[:, 0:16384].bitcast(BF16).rearrange(
                "p (u c s) -> p u c s", u=2, c=NC)
            rc_fm = a3.ap()[:, 16384:32768].bitcast(F32).rearrange(
                "p (c s) -> p c s", c=NC)
            r_fm = xn_tm.ap()[:, 8:16, :].bitcast(F32)
            relu_b = ebuf.ap().rearrange("p u s -> p (u s)").rearrange(
                "p (c s) -> p c s", c=NC)

            if ffn_level == 0:
                return
            mrns2, _ = ln_stats(xpx=False, pre_waits=pre, tag=f"{tag}ln")
            w_r = ldrs[0].fin()
            w_k = ldrs[1].fin()
            w_v = ldrs[2].fin()
            gz = dv(lambda e: e.memset(xn_fm[:, 0, :, 3:4], 0.0), waits=pre,
                    tag=f"{tag}gz")
            guard_m = {0: gz}
            st_ = {"tr_prev": None, "mix_prev": None, "ab_prev": None,
                   "c_prev": None, "rc_prev": None}
            gstate = {}

            def s12(g):
                buf = g % 2
                aw = ([st_["tr_prev"]] if st_["tr_prev"] else [])
                xn_mg = [ln_apply(4 * g + t, t, mrns2[g], aw,
                                  tag=f"{tag}g{g}") for t in range(4)]
                fm_m, tr_last = fm_transpose_group(
                    g, buf, [0, 1, 2, 3], xn_mg[-2:], tag=f"{tag}g{g}")
                st_["tr_prev"] = tr_last
                if g + 1 < NG:
                    gc = dv(lambda e, buf=buf: e.tensor_copy(
                        xn_fm[:, 1 - buf, :, 3:4], xn_fm[:, buf, :, 515:516]),
                        waits=fm_m + ([st_["mix_prev"]] if st_["mix_prev"] else []),
                        tag=f"{tag}gc{g}")
                    guard_m[g + 1] = gc
                mw = fm_m + [guard_m[g]] + ([st_["ab_prev"]] if st_["ab_prev"] else [])
                mix_m = None
                for dc in range(NC):
                    wstep()
                    for u, mucol, omcol in ((0, 2, 0), (1, 3, 1)):
                        tsm = dv(lambda e, dc=dc, u=u, omcol=omcol, buf=buf:
                           e.tensor_scalar(
                               out=xrxk[:, u, dc, :],
                               in0=xn_fm[:, buf, dc, 3:515],
                               scalar1=dsm[:, omcol, dc:dc + 1], scalar2=None,
                               op0=ALU.mult),
                           waits=mw, tag=f"{tag}mx{g}_{dc}_{u}")
                        mix_m = dv(lambda e, dc=dc, u=u, mucol=mucol, buf=buf:
                                   e.scalar_tensor_tensor(
                                       out=xrxk[:, u, dc, :],
                                       in0=xn_fm[:, buf, dc, 4:516],
                                       scalar=smalls[:, mucol, dc:dc + 1],
                                       in1=xrxk[:, u, dc, :],
                                       op0=ALU.mult, op1=ALU.add),
                                   waits=[tsm], tag=f"{tag}mxs{g}_{dc}_{u}")
                st_["mix_prev"] = mix_m
                gstate[g] = {"mix_m": mix_m}

            def s3(g):
                gs = gstate[g]
                sig_m = []
                ab_m = None
                for ec in range(NC):
                    wstep()
                    bnk = ec % 2
                    mms = [(warena[:, 0, dc, 128 * ec:128 * ec + 128],
                            xrxk[:, 0, dc, :]) for dc in range(NC)]
                    gm = mm_group(bnk, mms, waits=[gs["mix_m"], w_r],
                                  tag=f"{tag}A{g}_{ec}")
                    sig_m.append(evac(bnk, "scalar",
                                      lambda e, bnk=bnk, ec=ec: e.activation(
                                          out=r_fm[:, ec, :], in_=pb[bnk].ap(),
                                          func=AF.Sigmoid),
                                      waits=[gm] + ([st_["rc_prev"]] if st_["rc_prev"] else []),
                                      tag=f"{tag}Aev{g}_{ec}"))
                for ec in range(NC):
                    bnk = ec % 2
                    mms = [(warena[:, 1, dc, 128 * ec:128 * ec + 128],
                            xrxk[:, 1, dc, :]) for dc in range(NC)]
                    gm = mm_group(bnk, mms, waits=[w_k], tag=f"{tag}B{g}_{ec}")
                    ab_m = gm
                    evac(bnk, "scalar",
                         lambda e, bnk=bnk, ec=ec: e.activation(
                             out=relu_b[:, ec, :], in_=pb[bnk].ap(),
                             func=AF.Relu),
                         waits=[gm] + ([st_["c_prev"]] if st_["c_prev"] else []),
                         tag=f"{tag}Bev{g}_{ec}")
                st_["ab_prev"] = ab_m
                if g == NG - 1 and nxt:
                    arm(nxt[0], slot_free_gate())
                    arm(nxt[1], slot_free_gate())
                relu_m = p_.cur("sAC")
                k2_m = dv(lambda e: e.tensor_tensor(
                    out=relu_b[:, :, :], in0=relu_b[:, :, :],
                    in1=relu_b[:, :, :], op=ALU.mult),
                    waits=[relu_m], tag=f"{tag}k2{g}")
                gs["k2_m"] = k2_m
                gs["sig_m"] = sig_m

            def s4(g):
                gs = gstate[g]
                rc_m = []
                cm = None
                for ec in range(NC):
                    wstep()
                    bnk = ec % 2
                    mms = [(warena[:, 2, dc, 128 * ec:128 * ec + 128],
                            relu_b[:, dc, :]) for dc in range(NC)]
                    gm = mm_group(bnk, mms, waits=[gs["k2_m"], w_v],
                                  tag=f"{tag}C{g}_{ec}")
                    cm = gm
                    rc_m.append(evac(bnk, "vector",
                                     lambda e, bnk=bnk, ec=ec: e.tensor_tensor(
                                         out=rc_fm[:, ec, :], in0=pb[bnk].ap(),
                                         in1=r_fm[:, ec, :], op=ALU.mult),
                                     waits=[gm, gs["sig_m"][ec]],
                                     tag=f"{tag}Cev{g}_{ec}"))
                st_["c_prev"] = cm
                st_["rc_prev"] = rc_m[-1]
                if g == NG - 1 and nxt and nxt[2]:
                    arm(nxt[2], slot_free_gate())
                for t in range(4):
                    ra_ev = None
                    for half in range(2):
                        bnk = 2 + half
                        trs = []
                        for eci in range(4):
                            ec = 4 * half + eci
                            trs.append((pb[bnk].ap()[:, 128 * eci:128 * eci + 128],
                                        rc_fm[:, ec, 128 * t:128 * t + 128]))
                        tm = tr_group(bnk, trs, waits=[rc_m[-1]], dtype=F32,
                                      tag=f"{tag}rt{g}_{t}_{half}")
                        def f(e, bnk=bnk, g=g, t=t, half=half):
                            return e.tensor_tensor(
                                out=x_tm[:, 4 * g + t, 512 * half:512 * half + 512],
                                in0=pb[bnk].ap(),
                                in1=x_tm[:, 4 * g + t, 512 * half:512 * half + 512],
                                op=ALU.add)
                        ra_ev = evac(bnk, "vector", f, waits=[tm],
                                     tag=f"{tag}radd{g}_{t}_{half}")
                    if tile_hook is not None:
                        tile_hook(g, t, ra_ev)

            s12(0); s3(0)
            for g in range(1, NG):
                s12(g)
                s4(g - 1)
                s3(g)
            s4(NG - 1)

        if False and dbg and b == DBG_SEQ:
            dwv = p_.barrier_waits("sDV", "sAC", "sPE")
            def cast_dump(name, src_ap, waits):
                gp_dma(dbg_d[name], src_ap, waits=waits, tag=f"dbg{name}")
            cast_dump("stats", stats[:], dwv)
            cast_dump("xn", xn_tm[:], dwv)
            cast_dump("xpx", x_tm[:], dwv)
            cast_dump("K", K_fm, dwv)
            cast_dump("csum", colsum[:], dwv)
            cast_dump("zbuf", zbuf[:], dwv)
            cast_dump("E8", ebuf[:, 0, :], dwv)
            cast_dump("q2", warena[:, 2, :, 0:512], dwv)

        def emit_attn_b(tag, ldrs, nxt):
            pre = p_.barrier_waits("sDV", "sAC", "sPE", "sGP", "sGD")
            mrns3, _ = ln_stats(xpx=False, pre_waits=pre, tag=f"{tag}ln")
            w_qb = ldrs[0].fin()
            w_kb = ldrs[1].fin()
            w_vb = ldrs[2].fin()
            a3b = a3.ap().bitcast(BF16)
            a3f = a3.ap().bitcast(F32)
            sqkv = a3b[:, 0:12288].rearrange("p (u c s) -> p u c s",
                                             u=3, c=NC)
            Mf = a3f[:, 6144:7168].rearrange("p (c s) -> p c s", c=NC)
            Maug = a3b[:, 14336:15392].rearrange("p (c s) -> p c s", c=NC)
            zf = a3f[:, 7696:7704]
            ztmp = a3f[:, 7704:7712]
            dsc = a3f[:, 7712:7760]
            vd_b = xn_tm.ap()[:, 12, 0:128]
            tmpf = xn_tm.ap()[:, 12, :].bitcast(F32)[:, 128:256]
            adbuf_a = xn_tm.ap()[:, 13, :].bitcast(F32)
            adbuf_b = ebuf.ap()[:, 1, :].bitcast(F32)[:, 0:512]
            t1f = xn_tm.ap()[:, 14, :].bitcast(F32)
            t2f = xn_tm.ap()[:, 15, :].bitcast(F32)
            Ebuf = ebuf.ap()[:, 0, :]

            mi1 = dv(lambda e: e.memset(Mf, 0.0), waits=pre, tag=f"{tag}M0")
            mi2 = dv(lambda e: e.memset(Maug, 0.0), waits=pre, tag=f"{tag}Ma0")
            mi3 = dv(lambda e: e.memset(zf, 0.0), waits=pre, tag=f"{tag}z0")
            maug_ready = [mi1, mi2, mi3]

            tr_prev = None
            proj_war = []
            for G in range(NSEGS):
                buf = G % 2
                aw = ([tr_prev] if tr_prev else [])
                xn_mg = [ln_apply(4 * G + t, t, mrns3[G], aw,
                                  tag=f"{tag}g{G}") for t in range(4)]
                fm_m, tr_last = fm_transpose_group(G, buf, [0, 1, 2, 3],
                                                   xn_mg[-2:], tag=f"{tag}g{G}")
                tr_prev = tr_last
                ev_m = []
                for u, wm, bcol in ((0, w_qb, 4), (1, w_kb, 5), (2, w_vb, 6)):
                    for ec in range(NC):
                        bnk = ec % 2
                        mms = [(warena[:, u, dc, 128 * ec:128 * ec + 128],
                                xn_fm[:, buf, dc, 4:516]) for dc in range(NC)]
                        gm = mm_group(bnk, mms, waits=fm_m + [wm] + proj_war,
                                      tag=f"{tag}p{G}u{u}e{ec}")
                        ev_m.append(evac(bnk, "scalar",
                                    lambda e, bnk=bnk, u=u, ec=ec, bcol=bcol:
                                    e.activation(out=sqkv[:, u, ec, :],
                                                 in_=pb[bnk].ap(),
                                                 func=AF.Identity,
                                                 bias=smalls[:, bcol, ec:ec + 1]),
                                    waits=[gm], tag=f"{tag}pe{G}{u}{ec}"))
                ev_last = [ev_m[-1]]
                if G == NSEGS - 1 and nxt:
                    for L_ in nxt:
                        if L_ is not None:
                            arm(L_, slot_free_gate())
                vt_m = []
                for t in range(4):
                    bnk = 2 + t % 2
                    trs = [(pbb[bnk][:, 128 * ec:128 * ec + 128],
                            sqkv[:, 2, ec, 128 * t:128 * t + 128])
                           for ec in range(NC)]
                    tm_ = tr_group(bnk, trs, waits=ev_last,
                                   tag=f"{tag}vt{G}_{t}")
                    def f(e, bnk=bnk, t=t):
                        return e.tensor_copy(xn_tm[:, 4 + t, :],
                                             pbb[bnk][:, 0:1024])
                    vt_m.append(evac(bnk, "vector", f, waits=[tm_],
                                     tag=f"{tag}vte{G}_{t}"))
                vt_last = vt_m[-1]
                num_read_m = []
                mdelta_first = {6: True, 7: True}
                mdelta_last_m = None
                def passA_g(h):
                    omgcol = dsm[:, 3, h:h + 1]
                    adbuf = adbuf_a if h % 2 == 0 else adbuf_b
                    z2m = None
                    adm_l, pte_l = [], []
                    for lc in range(4):
                        par = lc % 4
                        Eslot = Ebuf[:, 512 * par:512 * (par + 1)]
                        sb_ = (h + lc) % 2
                        ncols = 128 * (lc + 1)
                        gm = mm_group(sb_, [(sqkv[:, 0, h, 128 * lc:128 * lc + 128],
                                             sqkv[:, 1, h, 0:ncols],
                                             pb[sb_].ap()[:, 0:ncols])],
                                      waits=ev_last + buf_war.get(("E", par), []),
                                      tag=f"{tag}s{G}_{h}_{lc}")
                        ee = ac(lambda e, sb_=sb_, ncols=ncols, Eslot=Eslot:
                                e.activation(out=Eslot[:, 0:ncols],
                                             in_=pb[sb_].ap()[:, 0:ncols],
                                             func=AF.Exp,
                                             scale=1.0 / np.sqrt(DK)),
                                waits=[gm], tag=f"{tag}e_{G}_{h}_{lc}")
                        bank_war[sb_] = ee
                        mk = dv(lambda e, lc=lc, Eslot=Eslot: e.tensor_tensor(
                            out=Eslot[:, 128 * lc:128 * lc + 128],
                            in0=Eslot[:, 128 * lc:128 * lc + 128],
                            in1=tril01[:], op=ALU.mult),
                            waits=[ee, init_gp], tag=f"{tag}mk{G}_{h}_{lc}")
                        z2m = dv(lambda e, lc=lc, ncols=ncols, Eslot=Eslot:
                                 e.tensor_reduce(dsc[:, 40 + lc:41 + lc],
                                                 Eslot[:, 0:ncols],
                                                 axis=AX.X, op=ALU.add),
                                 waits=[mk], tag=f"{tag}zb2_{G}_{h}_{lc}")
                        trs = [(pbb[2][:, 128 * j:128 * j + 128],
                                Eslot[:, 128 * j:128 * j + 128])
                               for j in range(lc + 1)]
                        ptm = tr_group(2, trs, waits=[mk], tag=f"{tag}pt{G}_{h}_{lc}")
                        pte_eng = "scalar" if lc % 2 else "vector"
                        if pte_eng == "scalar":
                            def fpt(e, lc=lc):
                                return e.copy(ptbuf[:, 0:lc + 1, :],
                                              pbb[2][:, 0:128 * (lc + 1)])
                        else:
                            def fpt(e, lc=lc):
                                return e.tensor_copy(ptbuf[:, 0:lc + 1, :],
                                                     pbb[2][:, 0:128 * (lc + 1)])
                        pte = evac(2, pte_eng, fpt,
                                   waits=[ptm] + buf_war.get("ptbuf", []),
                                   tag=f"{tag}pte{G}_{h}_{lc}")
                        pte_l.append(pte)
                        buf_war[("E", par)] = [ptm]
                        for j in range(lc + 1):
                            st = (lc == 0 and j == 0)
                            sp = (lc == 3 and j == 3)
                            def fad(e, j=j, lc=lc, h=h, st=st, sp=sp):
                                return e.matmul(
                                    pb[4].ap()[:, 128 * lc:128 * lc + 128],
                                    ptbuf[:, j, :],
                                    xn_tm[:, 4 + j, 128 * h:128 * h + 128],
                                    start=st, stop=sp)
                            w = [pte] if j == 0 else []
                            if st:
                                w = w + [vt_last] + buf_war.get("bank4", [])
                            adm = p_.add("tensor", fad, waits=w,
                                         inc=("sPE" if j == lc else None),
                                         tag=f"{tag}ad{G}_{h}_{lc}_{j}")
                        adm_l.append(adm)
                        buf_war["ptbuf"] = [adm]
                        yield
                    rzb = dv(lambda e: e.reciprocal(dsc[:, 36:40],
                                                    dsc[:, 40:44]),
                             waits=[z2m], tag=f"{tag}rz_{G}_{h}")
                    sdb = dv(lambda e, omgcol=omgcol: e.tensor_scalar(
                        out=dsc[:, 44:48], in0=dsc[:, 36:40],
                        scalar1=omgcol, scalar2=None, op0=ALU.mult),
                        waits=[rzb] + buf_war.get("zb_sd", []),
                        tag=f"{tag}sd_{G}_{h}")
                    ads = None
                    for lc in range(4):
                        ads = evac(4, "scalar",
                                   lambda e, lc=lc: e.activation(
                                       out=adbuf[:, 128 * lc:128 * lc + 128],
                                       in_=pb[4].ap()[:, 128 * lc:128 * lc + 128],
                                       func=AF.Identity,
                                       scale=dsc[:, 44 + lc:45 + lc]),
                                   waits=[adm_l[3], sdb] +
                                         buf_war.get(("ad", h % 2, lc), []),
                                   tag=f"{tag}ads{G}_{h}_{lc}")
                    buf_war["zb_sd"] = [ads]
                    buf_war["bank4"] = [ads]
                    sc_pe[h] = p_.cur("sPE")

                def passB_g(h):
                    nonlocal mdelta_last_m
                    gcol = dsm[:, 2, h:h + 1]
                    adbuf = adbuf_a if h % 2 == 0 else adbuf_b
                    sc_m = sc_pe[h]
                    t12 = xn_tm.ap()[:, 14:16, :].bitcast(F32)
                    squ = sqkv[:, 0:2, h, :]
                    a1 = ac(lambda e, h=h: e.activation(
                        out=t12, in_=squ, func=AF.Relu, scale=-1.0),
                        waits=[("sPE", sc_m[1])] + buf_war.get("t1f", []),
                        tag=f"{tag}tr1_{G}_{h}")
                    a2 = ac(lambda e: e.activation(out=t12, in_=t12,
                                                   func=AF.Exp, scale=-1.0),
                            waits=[a1], tag=f"{tag}tr2_{G}_{h}")
                    el = dv(lambda e, h=h: e.scalar_tensor_tensor(
                        out=squ, in0=squ, scalar=0.0, in1=t12,
                        op0=ALU.max, op1=ALU.add),
                        waits=[a2], tag=f"{tag}tr4_{G}_{h}")
                    buf_war["t1f"] = [el]
                    el_m = [el, el]
                    yield
                    sktr = tr_group(3, [(pbb[3][:, 128 * t:128 * t + 128],
                                         sqkv[:, 1, h, 128 * t:128 * t + 128])
                                        for t in range(4)],
                                    waits=[el_m[1]], tag=f"{tag}skt{G}_{h}")
                    skte = evac(3, "vector",
                                lambda e, h=h: e.tensor_copy(
                                    xn_tm.ap()[:, 8:12, 128 * h:128 * h + 128],
                                    pbb[3][:, 0:512].rearrange(
                                        "p (t c) -> p t c", t=4)),
                                waits=[sktr], tag=f"{tag}skte{G}_{h}")
                    zt_hold[0] = dv(lambda e, h=h: e.tensor_reduce(
                        ztmp[:, h:h + 1], sqkv[:, 1, h, :], axis=AX.X,
                        op=ALU.add), waits=[el_m[1]], tag=f"{tag}zt{G}_{h}")
                    yield
                    denm = mm_group(5, [(sqkv[:, u, h, 128 * lc:128 * lc + 128],
                                         Maug[:, h, 128:129],
                                         pb[5].ap()[:, 2 * lc + u:2 * lc + u + 1])
                                        for lc in range(4) for u in range(2)],
                                    waits=maug_ready + [el_m[1]],
                                    tag=f"{tag}den{G}_{h}")
                    num_read_m.append(denm)
                    dev = evac(5, "scalar", lambda e: e.activation(
                        out=dsc[:, 16:24], in_=pb[5].ap()[:, 0:8],
                        func=AF.Identity, bias=bepscol[:, 0:1]),
                        waits=[denm] + buf_war.get("dsc16", []),
                        tag=f"{tag}dev{G}_{h}")
                    rcp = dsc[:, 24:32].rearrange("p (l u) -> p l u", u=2)
                    rbm = dv(lambda e: e.reciprocal(dsc[:, 24:32],
                                                    dsc[:, 16:24]),
                             waits=[dev], tag=f"{tag}rb{G}_{h}")
                    buf_war["dsc16"] = [rbm]
                    gqb = dv(lambda e, gcol=gcol: e.tensor_scalar(
                        out=dsc[:, 32:36], in0=rcp[:, :, 0:1],
                        scalar1=gcol, scalar2=None, op0=ALU.mult),
                        waits=[rbm], tag=f"{tag}gq{G}_{h}")
                    yield
                    c1_last = None
                    for lc in range(4):
                        nb = 3 if lc % 2 == 0 else 5
                        nq = [(sqkv[:, 0, h, 128 * lc:128 * lc + 128],
                               Maug[:, h, 0:128], pb[nb].ap()[:, 0:128]),
                              (sqkv[:, 1, h, 128 * lc:128 * lc + 128],
                               Maug[:, h, 0:128], pb[nb].ap()[:, 128:256])]
                        nm = mm_group(nb, nq, waits=maug_ready + [el_m[1], skte],
                                      tag=f"{tag}nm{G}_{h}_{lc}")
                        num_read_m.append(nm)
                        vdp = xn_tm.ap()[:, 12, 128 * (lc % 2):
                                         128 * (lc % 2) + 128]
                        vdm = dv(lambda e, h=h, lc=lc, vdp=vdp, nb=nb:
                                 e.scalar_tensor_tensor(
                                     out=vdp, in0=pb[nb].ap()[:, 128:256],
                                     scalar=rcp[:, lc, 1:2],
                                     in1=xn_tm[:, 4 + lc, 128 * h:128 * h + 128],
                                     op0=ALU.mult, op1=ALU.subtract),
                                 waits=[nm, gqb] + buf_war.get(("vd", lc % 2), []),
                                 tag=f"{tag}vd_{G}_{h}_{lc}")
                        mb = 6 + h // 4
                        def fmd(e, h=h, lc=lc, mb=mb, vdp=vdp,
                                st=mdelta_first[mb]):
                            return e.matmul(
                                pb[mb].ap()[:, 128 * (h % 4):128 * (h % 4) + 128],
                                xn_tm[:, 8 + lc, 128 * h:128 * h + 128],
                                vdp, start=st,
                                stop=(h % 4 == 3 and lc == 3))
                        mw = [vdm, skte]
                        if mdelta_first[mb] and mb in bank_war and bank_war[mb]:
                            mw.append(bank_war[mb])
                        mdelta_last_m = p_.add("tensor", fmd, waits=mw,
                                               inc="sPE",
                                               tag=f"{tag}md{G}_{h}_{lc}")
                        mdelta_first[mb] = False
                        buf_war[("vd", lc % 2)] = [mdelta_last_m]
                        c1_last = dv(lambda e, h=h, lc=lc, nb=nb:
                                     e.scalar_tensor_tensor(
                                         out=adbuf[:, 128 * lc:128 * lc + 128],
                                         in0=pb[nb].ap()[:, 0:128],
                                         scalar=dsc[:, 32 + lc:33 + lc],
                                         in1=adbuf[:, 128 * lc:128 * lc + 128],
                                         op0=ALU.mult, op1=ALU.add),
                                     waits=[vdm], tag=f"{tag}c1_{G}_{h}_{lc}")
                        bank_war[nb] = c1_last
                        yield
                    c2 = dv(lambda e, G=G, h=h, adbuf=adbuf: e.tensor_tensor(
                        out=x_tm[:, 4 * G:4 * G + 4, 128 * h:128 * h + 128],
                        in0=adbuf.rearrange("p (l c) -> p l c", l=4),
                        in1=x_tm[:, 4 * G:4 * G + 4, 128 * h:128 * h + 128],
                        op=ALU.add), waits=[c1_last],
                        tag=f"{tag}c2_{G}_{h}")
                    for lc in range(4):
                        buf_war[("ad", h % 2, lc)] = [c2]
                sc_pe = {}
                zt_hold = [None]
                for _ in passA_g(0):
                    pass
                for h in range(H):
                    wstep()
                    gens = [passB_g(h)]
                    if h + 1 < H:
                        gens.append(passA_g(h + 1))
                    while gens:
                        for g_ in list(gens):
                            try:
                                next(g_)
                            except StopIteration:
                                gens.remove(g_)
                if n_sub >= 4:
                    for lc in range(4):
                        stats_tile(4 * G + lc, buf_war[("ad", 1, lc)],
                                   f"{tag}pf_")
                zu = dv(lambda e: e.tensor_tensor(out=zf, in0=zf, in1=ztmp,
                                                  op=ALU.add),
                        waits=[zt_hold[0]] + num_read_m[-1:], tag=f"{tag}zu{G}")
                zc = dv(lambda e: e.tensor_copy(
                    Maug[:, :, 128:129],
                    zf.rearrange("p (c o) -> p c o", o=1)),
                        waits=[zu], tag=f"{tag}zc{G}")
                m1_ = dv(lambda e: e.tensor_tensor(out=Mf[:, 0:4, :],
                                                   in0=Mf[:, 0:4, :],
                                                   in1=pb[6].ap(),
                                                   op=ALU.subtract),
                         waits=[mdelta_last_m], tag=f"{tag}mu{G}a")
                m2_ = dv(lambda e: e.tensor_tensor(out=Mf[:, 4:8, :],
                                                   in0=Mf[:, 4:8, :],
                                                   in1=pb[7].ap(),
                                                   op=ALU.subtract),
                         waits=[m1_], tag=f"{tag}mu{G}b")
                bank_war[6] = m2_
                bank_war[7] = m2_
                mc = dv(lambda e: e.tensor_copy(Maug[:, :, 0:128], Mf),
                        waits=[m2_] + num_read_m[-1:], tag=f"{tag}mc{G}")
                maug_ready = [mc, zc]
                proj_war = [p_.cur("sPE"), p_.cur("sDV")]

        ab_ldrs = None
        if n_sub >= 3:
            ab_ldrs = (LoadW(wqb_d, 0, 0, f"b{b}ab_ldwq"),
                       LoadW(wkb_d, 1, 1, f"b{b}ab_ldwk"),
                       LoadW(wvb_d, 2, 1, f"b{b}ab_ldwv"))
        f2_ldrs = None
        if n_sub >= 4:
            f2_ldrs = (LoadW(wrT_d, 0, 0, f"b{b}f2_ldwr"),
                       LoadW(wkfT_d, 1, 1, f"b{b}f2_ldwk"),
                       LoadW(wvfT_d, 2, 1, f"b{b}f2_ldwv"))
        nseq = None
        if n_sub >= 4 and b + 1 < BL:
            carry_qa = LoadW(wqaT_d, 0, 0, f"b{b + 1}ldwqa")
            carry_ka = LoadW(wkaT_d, 1, 1, f"b{b + 1}ldwka")
            nseq = (carry_qa, carry_ka, None)
        def f2_hook(g, t, m, b=b):
            i = 4 * g + t
            otm = out_d[b].rearrange("(t p) d -> p t d", p=P)
            sy_dma(otm[:, i:i + 1, :], x_tm[:, i:i + 1, :], waits=[m],
                   store=True, tag=f"st{b}_{i}")
            if b + 1 < BL:
                itm = x_d[b + 1].rearrange("(t p) d -> p t d", p=P)
                ld = sy_dma(x_tm[:, i:i + 1, :], itm[:, i:i + 1, :],
                            tag=f"ldx{b + 1}_{i}")
                stats_tile(i, [ld], f"b{b + 1}ln1_")

        f1_hook = None
        if n_sub >= 3:
            def f1_hook(g, t, m):
                stats_tile(4 * g + t, [m], f"b{b}ab_")
        if n_sub >= 2:
            emit_ffn(f"b{b}f1_", f1_ldrs, ab_ldrs, tile_hook=f1_hook,
                     dbg_g0=(dbg and b == DBG_SEQ))
        if n_sub >= 3:
            emit_attn_b(f"b{b}ab_", ab_ldrs, f2_ldrs)
        if n_sub >= 4:
            emit_ffn(f"b{b}f2_", f2_ldrs, nseq, tile_hook=f2_hook)
            NSTORE += NT
        assert not armed or (n_sub >= 4 and b + 1 < BL), \
            f"unarmed/armed leftovers at b={b}: {[L.tag for L in armed]}"

        if n_sub < 4:
            st_w = p_.barrier_waits("sDV", "sAC", "sPE")
            sy_dma(out_d[b].rearrange("(t p) d -> p t d", p=P), x_tm[:],
                   waits=st_w, store=True, tag=f"st{b}")
            NSTORE += 1

    p_.add("sync", lambda e: e.nop(),
           waits=[("sST", 16 * NSTORE), p_.cur("sGD"), p_.cur("sXD")],
           tag="final")

    p_.simulate()

    with ctx:
        blk = ctx.enter_context(nc.Block())

        def make_body(eng_name):
            def body(eng):
                for (fn, w, incs, tag) in p_.ops[eng_name]:
                    for (s, v) in w:
                        eng.wait_ge(sems[s], v)
                    inst = fn(eng)
                    for (s, v, a) in incs:
                        if a:
                            inst = inst.then_inc(sems[s], v)
            return body

        blk.sync(make_body("sync"))
        blk.gpsimd(make_body("gpsimd"))
        blk.tensor(make_body("tensor"))
        blk.vector(make_body("vector"))
        blk.scalar(make_body("scalar"))

    return nc


def _host_layout(inputs):
    g = lambda k: np.asarray(inputs[k])
    smalls = np.zeros((P, 8, NC), dtype=np.float32)
    def fm(v):
        return np.ascontiguousarray(v.reshape(NC, P).T)
    smalls[:, 0, :] = fm(g("L0_attn_a_bq"))
    smalls[:, 1, :] = fm(g("L0_attn_a_bk"))
    smalls[:, 2, :] = fm(g("L0_ffn_mu_r"))
    smalls[:, 3, :] = fm(g("L0_ffn_mu_k"))
    smalls[:, 4, :] = fm(g("L0_attn_b_bq"))
    smalls[:, 5, :] = fm(g("L0_attn_b_bk"))
    smalls[:, 6, :] = fm(g("L0_attn_b_bv"))
    smalls[:, 7, :] = np.broadcast_to(g("L0_attn_b_beta")[None, :], (P, 8))
    T = lambda a: np.ascontiguousarray(a.T)
    return {
        "wqaT": T(g("L0_attn_a_Wq")), "wkaT": T(g("L0_attn_a_Wk")),
        "wrT": T(g("L0_ffn_W_r")), "wkfT": T(g("L0_ffn_W_k")),
        "wvfT": T(g("L0_ffn_W_v")),
        "wqb": np.ascontiguousarray(g("L0_attn_b_Wq")),
        "wkb": np.ascontiguousarray(g("L0_attn_b_Wk")),
        "wvb": np.ascontiguousarray(g("L0_attn_b_Wv")),
        "smalls": smalls,
    }


N_SUB = 4

def kernel(**inputs):
    x = np.ascontiguousarray(np.asarray(inputs["L0_attn_a_x"], dtype=np.float32))
    shared = _host_layout(inputs)
    nc = build_nc(n_sub=N_SUB)
    in_maps = []
    for i in range(NCORES):
        m = dict(shared)
        m["x"] = np.ascontiguousarray(x[BL * i:BL * (i + 1)])
        in_maps.append(m)
    res = run_bass_kernel_spmd(nc, in_maps, core_ids=list(range(NCORES)))
    return np.concatenate([r["out"] for r in res.results], axis=0)


if __name__ == "__main__":
    nc = build_nc()
    print("built OK")


# version 15 (final)
# speedup vs baseline: 1.3020x; 1.0277x over previous
import numpy as np
from contextlib import ExitStack

import concourse.bass as bass
import concourse.mybir as mybir
from concourse.bass_utils import run_bass_kernel_spmd

F32 = mybir.dt.float32
BF16 = mybir.dt.bfloat16
AF = mybir.ActivationFunctionType
ALU = mybir.AluOpType
AX = mybir.AxisListType

P = 128
B, S, D = 32, 2048, 1024
NCORES = 8
BL = B // NCORES
NT = S // P
NC = D // P
NG = S // 512
H, DK, DV, SEG, NSEGS = 8, 128, 128, 512, 4
LN_EPS = 1e-5
B_EPS = 1e-6

ENGS = ("sync", "gpsimd", "tensor", "vector", "scalar")


class Plan:

    def __init__(self):
        self.ops = {e: [] for e in ENGS}
        self.cnt = {s: 0 for s in ("sPE", "sDV", "sAC", "sGP", "sXD", "sGD", "sST")}
        self.seen = {e: {} for e in ENGS}
        self.eng_sem = {"tensor": "sPE", "vector": "sDV", "scalar": "sAC",
                        "gpsimd": "sGP"}

    def add(self, eng, fn, waits=(), inc=None, n=1, tag="", attach=True):
        w = []
        for wv in waits:
            if wv is None:
                continue
            (s, v) = wv
            if v <= 0:
                continue
            if self.seen[eng].get(s, 0) >= v:
                continue
            self.seen[eng][s] = v
            w.append((s, v))
        incs = []
        if inc is not None:
            self.cnt[inc] += n
            incs.append((inc, n, attach))
        self.ops[eng].append((fn, w, incs, tag))
        return (inc, self.cnt[inc]) if inc is not None else None

    def cur(self, s):
        return (s, self.cnt[s])

    def barrier_waits(self, *sems):
        return [(s, self.cnt[s]) for s in sems]

    def simulate(self):
        pos = {e: 0 for e in ENGS}
        val = {s: 0 for s in self.cnt}
        progress = True
        while progress:
            progress = False
            for e in ENGS:
                while pos[e] < len(self.ops[e]):
                    fn, w, incs, tag = self.ops[e][pos[e]]
                    if any(val[s] < v for (s, v) in w):
                        break
                    for (s, v, _a) in incs:
                        val[s] += v
                    pos[e] += 1
                    progress = True
        stuck = {e: pos[e] for e in ENGS if pos[e] < len(self.ops[e])}
        if stuck:
            msg = []
            for e, p_ in stuck.items():
                fn, w, incs, tag = self.ops[e][p_]
                msg.append(f"{e}@{p_}/{len(self.ops[e])} tag={tag} waits={w} "
                           f"sems={[(s, val[s]) for s, _ in w]}")
            raise RuntimeError("PLAN DEADLOCK:\n" + "\n".join(msg))


FFN_LEVEL = 3
DBG_SEQ = 0

def build_nc(n_sub=4, dbg=False, ffn_level=None):
    if ffn_level is None:
        ffn_level = FFN_LEVEL
    nc = bass.Bass("TRN2", target_bir_lowering=False,
                   dynamic_dma_scratch_size=8192)

    x_d = nc.dram_tensor("x", [BL, S, D], F32, kind="ExternalInput").ap()
    out_d = nc.dram_tensor("out", [BL, S, D], F32, kind="ExternalOutput").ap()
    wqaT_d = nc.dram_tensor("wqaT", [D, D], F32, kind="ExternalInput").ap()
    wkaT_d = nc.dram_tensor("wkaT", [D, D], F32, kind="ExternalInput").ap()
    wrT_d = nc.dram_tensor("wrT", [D, D], F32, kind="ExternalInput").ap()
    wkfT_d = nc.dram_tensor("wkfT", [D, D], F32, kind="ExternalInput").ap()
    wvfT_d = nc.dram_tensor("wvfT", [D, D], F32, kind="ExternalInput").ap()
    wqb_d = nc.dram_tensor("wqb", [D, D], F32, kind="ExternalInput").ap()
    wkb_d = nc.dram_tensor("wkb", [D, D], F32, kind="ExternalInput").ap()
    wvb_d = nc.dram_tensor("wvb", [D, D], F32, kind="ExternalInput").ap()
    smalls_d = nc.dram_tensor("smalls", [P, 8, NC], F32, kind="ExternalInput").ap()
    if dbg:
        dbg_d = {k: nc.dram_tensor(f"dbg_{k}", shp, F32, kind="ExternalOutput").ap()
                 for k, shp in [("stats", [P, NT, 8]), ("xn", [P, NT, D]),
                                ("xpx", [P, NT, D]), ("K", [P, NC, S]),
                                ("csum", [1, D]), ("zbuf", [P, NT, 2]),
                                ("E8", [P, S]), ("q2", [P, NC, 512]),
                                ("xrxk", [P, 2, NC, 512]), ("r", [P, NC, 512]),
                                ("k2", [P, NC, 512]), ("rc", [P, NC, 512]),
                                ("xnfm", [P, 2, NC, 516])]}

    ctx = ExitStack()
    sb = lambda *a: ctx.enter_context(nc.sbuf_tensor(*a))
    ps = lambda *a: ctx.enter_context(nc.psum_tensor(*a))

    x_tm = sb("x_tm", [P, NT, D], F32)
    xn_tm = sb("xn_tm", [P, NT, D], BF16)
    xn_fm = sb("xn_fm", [P, 2, NC, 516], BF16)
    a3 = sb("a3", [P, 32768], mybir.dt.uint8)
    warena = sb("warena", [P, 3, NC, D], BF16)
    ebuf = sb("ebuf", [P, 2, S], BF16)
    ptbuf = sb("ptbuf", [P, 17, P], BF16)
    stats = sb("stats", [P, NT, 8], F32)
    smalls = sb("smalls_s", [P, 8, NC], F32)
    dsm = sb("dsm", [P, 8, NC], F32)
    identb = sb("identb", [P, P], BF16)
    tril01 = sb("tril01", [P, P], BF16)
    identf = sb("identf", [P, P], F32)
    onescol = sb("onescol", [P, 1], BF16)
    epscol = sb("epscol", [P, 1], F32)
    bepscol = sb("bepscol", [P, 1], F32)
    zbuf = sb("zbuf", [P, NT, 2], F32)
    zpart = sb("zpart", [P, 8], F32)
    nzcol = sb("nzcol", [P, 2], BF16)
    zrow = sb("zrow", [1, 2, P], BF16)
    colsum = sb("colsum", [1, D], BF16)
    wst0 = sb("wst0", [P, 1024], F32)
    junkb = sb("junkb", [P, 1024], BF16)

    K_fm = a3.ap().bitcast(BF16).rearrange("p (c s) -> p c s", c=NC)
    junk = ebuf.ap().bitcast(F32)[:, 0, :]

    pb = [ps(f"pb{i}", [P, 512], F32) for i in range(8)]
    pbb = [pb[i].ap().bitcast(BF16) for i in range(8)]

    sems = {}
    for s in ("sPE", "sDV", "sAC", "sGP", "sXD", "sGD", "sST"):
        sems[s] = ctx.enter_context(nc.semaphore(s))

    p_ = Plan()
    bank_war = {}
    buf_war = {}

    def mm_group(bank, mms, waits=(), tag=""):
        w = list(waits)
        if bank in bank_war and bank_war[bank]:
            w.append(bank_war[bank])
        last = None
        n = len(mms)
        for i, m in enumerate(mms):
            lhsT, rhs = m[0], m[1]
            out = m[2] if len(m) > 2 else pb[bank].ap()
            st, sp = (i == 0), (i == n - 1)
            def f(e, out=out, lhsT=lhsT, rhs=rhs, st=st, sp=sp):
                return e.matmul(out, lhsT, rhs, start=st, stop=sp)
            last = p_.add("tensor", f, waits=(w if i == 0 else ()),
                          inc=("sPE" if sp else None), tag=tag)
        return last

    def tr_group(bank, trs, waits=(), dtype=BF16, tag=""):
        w = list(waits)
        if bank in bank_war and bank_war[bank]:
            w.append(bank_war[bank])
        ident = identb[:] if dtype == BF16 else identf[:]
        last = None
        for i, (out, in_) in enumerate(trs):
            def f(e, out=out, in_=in_, ident=ident):
                return e.transpose(out, in_, ident)
            last = p_.add("tensor", f, waits=(w if i == 0 else ()),
                          inc=("sPE" if i == len(trs) - 1 else None), tag=tag)
        return last

    def evac(bank, eng, fn, waits=(), tag=""):
        sem = p_.eng_sem[eng]
        mile = p_.add(eng, fn, waits=waits, inc=sem, tag=tag)
        bank_war[bank] = mile
        return mile

    def dv(fn, waits=(), tag="", inc=True):
        return p_.add("vector", fn, waits=waits, inc=("sDV" if inc else None), tag=tag)

    def ac(fn, waits=(), tag="", inc=True):
        return p_.add("scalar", fn, waits=waits, inc=("sAC" if inc else None), tag=tag)

    def gp(fn, waits=(), tag="", inc=True):
        return p_.add("gpsimd", fn, waits=waits, inc=("sGP" if inc else None), tag=tag)

    def sy_dma(out, in_, waits=(), store=False, tag=""):
        waits = list(waits) + [p_.cur("sXD"), p_.cur("sST")]
        def f(e, out=out, in_=in_):
            return e.dma_start(out=out, in_=in_)
        return p_.add("sync", f, waits=waits, inc=("sST" if store else "sXD"),
                      n=16, tag=tag)

    def hard_barrier(tag=""):
        arr = {}
        for e in ("tensor", "vector", "scalar", "gpsimd"):
            s = p_.eng_sem[e]
            def mk(s=s):
                return lambda en: en.sem_inc(sems[s], 1)
            arr[e] = p_.add(e, mk(), inc=s, tag=f"{tag}barA", attach=False)
        dmaw = [p_.cur("sGD"), p_.cur("sXD"), p_.cur("sST")]
        allw = list(arr.values()) + dmaw
        for e in ("tensor", "vector", "scalar", "gpsimd", "sync"):
            p_.add(e, (lambda en: en.wait_ge(sems["sGP"], 1)), waits=allw,
                   tag=f"{tag}barW")

    def sc_dma(out, in_, waits=(), tag=""):
        waits = list(waits) + [p_.cur("sGD")]
        def f(e, out=out, in_=in_):
            return e.dma_start(out=out, in_=in_)
        return p_.add("scalar", f, waits=waits, inc="sGD", n=16, tag=tag)

    def gp_dma(out, in_, waits=(), tag=""):
        waits = list(waits) + [p_.cur("sGD")]
        def f(e, out=out, in_=in_):
            return e.dma_start(out=out, in_=in_)
        return p_.add("gpsimd", f, waits=waits, inc="sGD", n=16, tag=tag)

    sy_dma(smalls[:], smalls_d, tag="ld_smalls")
    sm_ready = p_.cur("sXD")

    m1 = gp(lambda e: e.memset(identb[:], 0.0), tag="identb0")
    gp(lambda e: e.affine_select(out=identb[:], in_=identb[:],
                                 compare_op=ALU.not_equal, fill=1.0, base=0,
                                 pattern=[[-1, P]], channel_multiplier=1),
       waits=[m1], tag="identb")
    mt = gp(lambda e: e.memset(tril01[:], 1.0), tag="tril0")
    gp(lambda e: e.affine_select(out=tril01[:], in_=tril01[:],
                                 compare_op=ALU.is_ge, fill=0.0, base=0,
                                 pattern=[[-1, P]], channel_multiplier=1),
       waits=[mt], tag="tril")
    m2 = gp(lambda e: e.memset(identf[:], 0.0), tag="identf0")
    gp(lambda e: e.affine_select(out=identf[:], in_=identf[:],
                                 compare_op=ALU.not_equal, fill=1.0, base=0,
                                 pattern=[[-1, P]], channel_multiplier=1),
       waits=[m2], tag="identf")
    gp(lambda e: e.memset(onescol[:], 1.0), tag="ones")
    gp(lambda e: e.memset(epscol[:], LN_EPS), tag="eps")
    gp(lambda e: e.memset(bepscol[:], B_EPS), tag="beps")
    gp(lambda e: e.memset(xn_fm[:], 0.0), tag="fmz")
    init_gp = p_.cur("sGP")

    dv(lambda e: e.tensor_scalar(out=dsm[:, 0, :], in0=smalls[:, 2, :],
                                 scalar1=-1.0, scalar2=1.0, op0=ALU.mult,
                                 op1=ALU.add), waits=[sm_ready], tag="onemur")
    dv(lambda e: e.tensor_scalar(out=dsm[:, 1, :], in0=smalls[:, 3, :],
                                 scalar1=-1.0, scalar2=1.0, op0=ALU.mult,
                                 op1=ALU.add), tag="onemuk")
    ac(lambda e: e.activation(out=dsm[:, 2, :], in_=smalls[:, 7, :],
                              func=AF.Sigmoid), waits=[sm_ready], tag="gate")
    gate_done = p_.cur("sAC")
    dv(lambda e: e.tensor_scalar(out=dsm[:, 3, :], in0=dsm[:, 2, :],
                                 scalar1=-1.0, scalar2=1.0, op0=ALU.mult,
                                 op1=ALU.add), waits=[gate_done], tag="onegate")
    dv(lambda e: e.tensor_scalar(out=dsm[:, 6, :], in0=smalls[:, 4, :],
                                 scalar1=-1.0, scalar2=None, op0=ALU.mult),
       tag="negbq")
    dv(lambda e: e.tensor_scalar(out=dsm[:, 7, :], in0=smalls[:, 5, :],
                                 scalar1=-1.0, scalar2=None, op0=ALU.mult),
       tag="negbk")
    init_dv = p_.cur("sDV")
    init_ac = p_.cur("sAC")

    armed = []

    class LoadW:
        def __init__(self, dram_ap, slot, q=0, tag=""):
            self.dram_ap, self.slot, self.q, self.tag = dram_ap, slot, q, tag
            self.c = NC - 1
            self.gate = []
            self.last = None
            self.mile = None

        def step(self):
            if self.c < 0:
                return
            c = self.c
            self.c -= 1
            dma = sy_dma if self.q == 0 else sc_dma
            wf = warena.ap()[:, self.slot, :, :].rearrange(
                "p c e -> p (c e)").bitcast(F32)
            stage = wf[:, 0:1024] if c >= 2 else wst0.ap()
            w = list(self.gate) + ([self.last] if self.last else [])
            if c < 2:
                w = w + buf_war.get("wst", [])
            ld = dma(stage, self.dram_ap[128 * c:128 * (c + 1), :], waits=w,
                     tag=f"{self.tag}c{c}")
            if self.q == 0:
                self.last = ac(lambda e, slot=self.slot, c=c, stage=stage:
                               e.activation(out=warena[:, slot, c, :],
                                            in_=stage, func=AF.Identity),
                               waits=[ld], tag=f"{self.tag}x{c}")
            else:
                self.last = dv(lambda e, slot=self.slot, c=c, stage=stage:
                               e.tensor_copy(warena[:, slot, c, :], stage),
                               waits=[ld], tag=f"{self.tag}x{c}")
            if c < 2:
                buf_war["wst"] = [self.last]
            if self.c < 0:
                self.mile = self.last

        def fin(self):
            while self.c >= 0:
                self.step()
            if self in armed:
                armed.remove(self)
            return self.mile

    def arm(ldr, gate):
        ldr.gate = list(gate)
        armed.append(ldr)

    def wstep():
        for ldr in list(armed):
            ldr.step()
            if ldr.c < 0:
                armed.remove(ldr)

    slot_free_gate = lambda: p_.barrier_waits("sPE", "sAC", "sDV")

    NSTORE = 0
    carry_qa = carry_ka = None
    stats_done = {}
    stats_guard = {}

    for b in range(BL):
        if n_sub >= 4 and b > 0:
            x_ready = []
        else:
            xw = [p_.cur("sST")]
            x_mile = sy_dma(x_tm[:], x_d[b].rearrange("(t p) d -> p t d", p=P),
                            waits=xw, tag=f"ldx{b}")
            x_ready = [x_mile]

        def stats_tile(i, waits, tag):
            w = list(waits) + stats_guard.get(i // 4, [])
            s1m = dv(lambda e, i=i: e.tensor_reduce(
                stats[:, i, 0:1], x_tm[:, i, :], axis=AX.X, op=ALU.add),
                waits=w, tag=f"{tag}S1_{i}")
            s2m = ac(lambda e, i=i: e.activation(
                out=junkb.ap(), in_=x_tm[:, i, :], func=AF.Square,
                accum_out=stats[:, i, 1:2]),
                waits=w + buf_war.get("junkb", []), tag=f"{tag}S2_{i}")
            buf_war["junkb"] = [s2m]
            stats_done[i] = (s1m, s2m)

        def ln_stats(xpx, pre_waits, tag):
            for i in range(NT):
                if i not in stats_done:
                    stats_tile(i, pre_waits + [sm_ready], tag)
            mrns, oprs = [], []
            for g4 in range(4):
                sl = slice(4 * g4, 4 * g4 + 4)
                sw = [m for i in range(4 * g4, 4 * g4 + 4)
                      for m in stats_done[i]]
                m1 = dv(lambda e, sl=sl: e.tensor_scalar(
                    out=stats[:, sl, 2:3], in0=stats[:, sl, 0:1],
                    scalar1=1.0 / D, scalar2=None, op0=ALU.mult),
                    waits=sw, tag=f"{tag}m{g4}")
                dv(lambda e, sl=sl: e.tensor_scalar(
                    out=stats[:, sl, 4:5], in0=stats[:, sl, 1:2],
                    scalar1=1.0 / D, scalar2=None, op0=ALU.mult),
                    tag=f"{tag}s2n{g4}")
                s2 = dv(lambda e, sl=sl: e.tensor_tensor(
                    out=stats[:, sl, 3:4], in0=stats[:, sl, 2:3],
                    in1=stats[:, sl, 2:3], op=ALU.mult),
                    waits=[m1], tag=f"{tag}m2{g4}")
                var = dv(lambda e, sl=sl: e.tensor_tensor(
                    out=stats[:, sl, 3:4], in0=stats[:, sl, 4:5],
                    in1=stats[:, sl, 3:4], op=ALU.subtract),
                    waits=[s2], tag=f"{tag}var{g4}")
                lnv = ac(lambda e, sl=sl: e.activation(
                    out=stats[:, sl, 4:5], in_=stats[:, sl, 3:4], func=AF.Ln,
                    bias=epscol[:, 0:1]),
                    waits=[var, init_gp], tag=f"{tag}ln{g4}")
                rstd = ac(lambda e, sl=sl: e.activation(
                    out=stats[:, sl, 5:6], in_=stats[:, sl, 4:5], func=AF.Exp,
                    scale=-0.5),
                    waits=[lnv], tag=f"{tag}rstd{g4}")
                mrn = dv(lambda e, sl=sl: e.scalar_tensor_tensor(
                    out=stats[:, sl, 6:7], in0=stats[:, sl, 2:3],
                    scalar=-1.0, in1=stats[:, sl, 5:6],
                    op0=ALU.mult, op1=ALU.mult),
                    waits=[rstd], tag=f"{tag}mrn{g4}")
                opr = None
                if xpx:
                    opr = dv(lambda e, sl=sl: e.tensor_scalar(
                        out=stats[:, sl, 7:8], in0=stats[:, sl, 5:6],
                        scalar1=1.0, scalar2=None, op0=ALU.add),
                        waits=[rstd], tag=f"{tag}opr{g4}")
                settle = dv(lambda e, sl=sl: e.tensor_scalar(
                    out=stats[:, sl, 4:5], in0=stats[:, sl, 6:7],
                    scalar1=0.0, scalar2=None, op0=ALU.add),
                    waits=[mrn] + ([opr] if opr else []),
                    tag=f"{tag}settle{g4}")
                stats_guard[g4] = [settle]
                mrns.append(settle)
                oprs.append(settle if xpx else None)
            stats_done.clear()
            return mrns, oprs

        def ln_apply(i, dst, mrn, extra_waits, tag):
            if i % 2 == 0:
                return ac(lambda e, i=i, dst=dst: e.activation(
                    out=xn_tm[:, dst, :], in_=x_tm[:, i, :], func=AF.Identity,
                    scale=stats[:, i, 5:6], bias=stats[:, i, 6:7]),
                    waits=[mrn] + list(extra_waits), tag=f"{tag}xn{i}")
            return dv(lambda e, i=i, dst=dst: e.tensor_scalar(
                out=xn_tm[:, dst, :], in0=x_tm[:, i, :],
                scalar1=stats[:, i, 5:6], scalar2=stats[:, i, 6:7],
                op0=ALU.mult, op1=ALU.add),
                waits=[mrn] + list(extra_waits), tag=f"{tag}xn{i}")

        def ln_xpx(i, opr, apl, tag):
            if i % 2 == 1:
                return ac(lambda e, i=i: e.activation(
                    out=x_tm[:, i, :], in_=x_tm[:, i, :], func=AF.Identity,
                    scale=stats[:, i, 7:8], bias=stats[:, i, 6:7]),
                    waits=[opr, apl], tag=f"{tag}xpx{i}")
            return dv(lambda e, i=i: e.tensor_scalar(
                out=x_tm[:, i, :], in0=x_tm[:, i, :],
                scalar1=stats[:, i, 7:8], scalar2=stats[:, i, 6:7],
                op0=ALU.mult, op1=ALU.add),
                waits=[opr, apl], tag=f"{tag}xpx{i}")

        def fm_transpose_group(g, buf, src_tiles, xn_m, tag, guard_from=None):
            miles = []
            tm = None
            for half in range(4):
                dc0 = 2 * half
                bnk = 6 + (half % 2)
                trs = []
                for dci in range(2):
                    dc = dc0 + dci
                    for t in range(4):
                        out = pbb[bnk][:, 512 * dci + 128 * t:
                                       512 * dci + 128 * t + 128]
                        in_ = xn_tm[:, src_tiles[t], 128 * dc:128 * dc + 128]
                        trs.append((out, in_))
                w = list(xn_m)
                tm = tr_group(bnk, trs, waits=w, tag=f"{tag}tr{half}")
                eng = "vector" if half % 2 == 0 else "scalar"
                if eng == "vector":
                    def f(e, bnk=bnk, buf=buf, dc0=dc0):
                        return e.tensor_copy(xn_fm[:, buf, dc0:dc0 + 2, 4:516],
                                             pbb[bnk][:, 0:1024])
                else:
                    def f(e, bnk=bnk, buf=buf, dc0=dc0):
                        return e.copy(xn_fm[:, buf, dc0:dc0 + 2, 4:516],
                                      pbb[bnk][:, 0:1024])
                miles.append(evac(bnk, eng, f, waits=[tm], tag=f"{tag}ev{half}"))
            return miles, tm

        aw_pre = p_.barrier_waits("sPE", "sAC")
        mrns, oprs = ln_stats(xpx=True,
                              pre_waits=x_ready + [p_.cur("sDV"),
                                                   p_.cur("sAC")],
                              tag=f"b{b}ln1_")
        if carry_qa is None:
            LQA = LoadW(wqaT_d, 0, 0, f"b{b}ldwqa")
            LKA = LoadW(wkaT_d, 1, 1, f"b{b}ldwka")
            arm(LQA, slot_free_gate() if b > 0 else [])
            arm(LKA, slot_free_gate() if b > 0 else [])
        else:
            LQA, LKA = carry_qa, carry_ka
        carry_qa = carry_ka = None
        if n_sub >= 2:
            f1_ldrs = (LoadW(wrT_d, 0, 0, f"b{b}f1_ldwr"),
                       LoadW(wkfT_d, 1, 1, f"b{b}f1_ldwk"),
                       LoadW(wvfT_d, 2, 1, f"b{b}f1_ldwv"))
        else:
            f1_ldrs = None
        w_ka = LKA.fin()
        xn_m = [ln_apply(i, i, mrns[i // 4], [], tag=f"b{b}ln1_")
                for i in range(NT)]
        xpx_all = [ln_xpx(i, oprs[i // 4], xn_m[i], tag=f"b{b}ln1_")
                   for i in range(NT)]
        xpx_m = xpx_all[-2:]

        kproj_m = []
        for g in range(NG):
            fm_m, _ = fm_transpose_group(g, g % 2, [4 * g + t for t in range(4)],
                                         xn_m[4 * g + 2:4 * g + 4],
                                         tag=f"b{b}a1g{g}")
            for ec in range(NC):
                bnk = ec % 2
                mms = [(warena[:, 1, dc, 128 * ec:128 * ec + 128],
                        xn_fm[:, g % 2, dc, 4:516]) for dc in range(NC)]
                gm = mm_group(bnk, mms, waits=fm_m + [w_ka], tag=f"b{b}K{g}_{ec}")
                def f(e, bnk=bnk, ec=ec, g=g):
                    return e.activation(out=K_fm[:, ec, 512 * g:512 * g + 512],
                                        in_=pb[bnk].ap(), func=AF.Identity,
                                        bias=smalls[:, 1, ec:ec + 1])
                kproj_m.append(evac(bnk, "scalar", f, waits=[gm],
                                    tag=f"b{b}Kev{g}_{ec}"))

        if f1_ldrs:
            arm(f1_ldrs[1], slot_free_gate())
        cs_m = []
        for dh in range(2):
            mms = [(onescol[:, 0:1], xn_tm[:, kc, 512 * dh:512 * dh + 512],
                    pb[6].ap()[0:1, :]) for kc in range(NT)]
            gm = mm_group(6, mms, waits=xn_m[-2:] + [init_gp],
                          tag=f"b{b}cs{dh}")
            def f(e, dh=dh):
                return e.tensor_copy(colsum[0:1, 512 * dh:512 * dh + 512],
                                     pb[6].ap()[0:1, :])
            cs_m.append(evac(6, "vector", f, waits=[gm], tag=f"b{b}csev{dh}"))

        k_all = kproj_m[-1]

        def emit_scores(c, c4, qg_m):
            eb = c % 2
            zp = 4 * (c % 2)
            exp_m = []
            for kb in range(4):
                bnk = kb % 2
                mms = [(warena[:, 2, ec, 128 * c4:128 * c4 + 128],
                        K_fm[:, ec, 512 * kb:512 * kb + 512])
                       for ec in range(NC)]
                gm = mm_group(bnk, mms,
                              waits=qg_m + [k_all] + buf_war.get(
                                  ("ebuf", eb), []),
                              tag=f"b{b}s{c}_{kb}")
                def f(e, bnk=bnk, eb=eb, kb=kb, zp=zp):
                    return e.activation(
                        out=ebuf[:, eb, 512 * kb:512 * kb + 512],
                        in_=pb[bnk].ap(), func=AF.Exp, scale=1.0 / 32.0,
                        accum_out=zpart[:, zp + kb:zp + kb + 1])
                exp_m.append(evac(bnk, "scalar", f,
                                  waits=[gm] + buf_war.get(("zpart", eb), []),
                                  tag=f"b{b}e{c}_{kb}"))
            return exp_m

        def emit_tail(c, exp_m):
            eb = c % 2
            zp = 4 * (c % 2)
            zr = dv(lambda e, c=c, zp=zp: e.tensor_reduce(
                zbuf[:, c, 0:1], zpart[:, zp:zp + 4], axis=AX.X, op=ALU.add),
                waits=[exp_m[-1]], tag=f"b{b}Z{c}")
            buf_war[("zpart", eb)] = [zr]
            rz = dv(lambda e, c=c: e.reciprocal(zbuf[:, c, 1:2],
                                                zbuf[:, c, 0:1]),
                    waits=[zr], tag=f"b{b}rZ{c}")
            nz = dv(lambda e, c=c, eb=eb: e.tensor_scalar(
                out=nzcol[:, eb:eb + 1], in0=zbuf[:, c, 0:1],
                scalar1=-2.0 / S, scalar2=None, op0=ALU.mult),
                waits=[zr] + buf_war.get(("nzcol", eb), []),
                tag=f"b{b}nz{c}")
            ztr = tr_group(6, [(pbb[6][0:1, 0:P], nzcol[:, eb:eb + 1])],
                           waits=[nz], tag=f"b{b}ztr{c}")
            buf_war[("nzcol", eb)] = [ztr]
            zev = evac(6, "vector",
                       lambda e, eb=eb: e.tensor_copy(zrow[0:1, eb, :],
                                                      pbb[6][0:1, 0:P]),
                       waits=[ztr] + buf_war.get(("zrow", eb), []),
                       tag=f"b{b}zev{c}")
            pt_m = []
            for half in range(2):
                bnk = 2 + half
                trs = []
                for kci in range(8):
                    kc = 8 * half + kci
                    trs.append((pbb[bnk][:, 128 * kci:128 * kci + 128],
                                ebuf[:, eb, 128 * kc:128 * kc + 128]))
                tm = tr_group(bnk, trs, waits=[exp_m[-1]],
                              tag=f"b{b}pt{c}_{half}")
                def f(e, bnk=bnk, half=half):
                    return e.tensor_copy(ptbuf[:, 8 * half:8 * half + 8, :],
                                         pbb[bnk][:, 0:1024])
                pt_m.append(evac(bnk, "vector", f, waits=[tm],
                                 tag=f"b{b}ptev{c}_{half}"))
            buf_war[("ebuf", eb)] = [pt_m[-1]]
            for dh in range(2):
                bnk = 4 + dh
                mms = [(ptbuf[:, kc, :],
                        xn_tm[:, kc, 512 * dh:512 * dh + 512])
                       for kc in range(NT)]
                mms.append((zrow[0:1, eb, :],
                            colsum[0:1, 512 * dh:512 * dh + 512]))
                gm = mm_group(bnk, mms,
                              waits=pt_m + [zev] + cs_m, tag=f"b{b}pv{c}_{dh}")
                buf_war[("zrow", eb)] = [gm]
                def f(e, bnk=bnk, c=c, dh=dh):
                    return e.scalar_tensor_tensor(
                        out=x_tm[:, c, 512 * dh:512 * dh + 512],
                        in0=pb[bnk].ap(), scalar=zbuf[:, c, 1:2],
                        in1=x_tm[:, c, 512 * dh:512 * dh + 512],
                        op0=ALU.mult, op1=ALU.add)
                pv_ev = evac(bnk, "vector", f, waits=[gm, rz] + xpx_m,
                             tag=f"b{b}pvev{c}_{dh}")
            if n_sub >= 2:
                stats_tile(c, [pv_ev], f"b{b}f1_")

        pend = None
        w_qa = LQA.fin()
        for gi, g in enumerate([2, 3, 0, 1]):
            if gi >= 2:
                fm_m, _ = fm_transpose_group(g, g % 2,
                                             [4 * g + t for t in range(4)],
                                             xn_m[4 * g + 2:4 * g + 4],
                                             tag=f"b{b}a2g{g}")
            else:
                fm_m = []
            qg_m = []
            for ec in range(NC):
                bnk = ec % 2
                mms = [(warena[:, 0, dc, 128 * ec:128 * ec + 128],
                        xn_fm[:, g % 2, dc, 4:516]) for dc in range(NC)]
                gm = mm_group(bnk, mms, waits=fm_m + [w_qa], tag=f"b{b}Q{g}_{ec}")
                def f(e, bnk=bnk, ec=ec):
                    return e.activation(out=warena[:, 2, ec, 0:512],
                                        in_=pb[bnk].ap(), func=AF.Identity,
                                        bias=smalls[:, 0, ec:ec + 1])
                qg_m.append(evac(bnk, "scalar", f, waits=[gm],
                                 tag=f"b{b}Qev{g}_{ec}"))
            if gi == 3 and f1_ldrs:
                arm(f1_ldrs[0], slot_free_gate())
            for c4 in range(4):
                c = 4 * g + c4
                wstep()
                em = emit_scores(c, c4, qg_m)
                if pend is not None:
                    emit_tail(*pend)
                pend = (c, em)
        if f1_ldrs:
            arm(f1_ldrs[2], slot_free_gate())
        emit_tail(*pend)
        wstep()

        def emit_ffn(tag, ldrs, nxt, tile_hook=None, dbg_g0=False):
            if ffn_level < 0:
                return
            import os as _os0
            if _os0.environ.get("FFN_PRE2"):
                pre = p_.barrier_waits("sPE", "sAC")
            else:
                pre = p_.barrier_waits("sDV", "sAC", "sPE", "sGP", "sGD")

            xrxk = a3.ap()[:, 0:16384].bitcast(BF16).rearrange(
                "p (u c s) -> p u c s", u=2, c=NC)
            rc_fm = a3.ap()[:, 16384:32768].bitcast(F32).rearrange(
                "p (c s) -> p c s", c=NC)
            r_fm = xn_tm.ap()[:, 8:16, :].bitcast(F32)
            relu_b = ebuf.ap().rearrange("p u s -> p (u s)").rearrange(
                "p (c s) -> p c s", c=NC)

            if ffn_level == 0:
                return
            mrns2, _ = ln_stats(xpx=False, pre_waits=pre, tag=f"{tag}ln")
            w_r = ldrs[0].fin()
            w_k = ldrs[1].fin()
            w_v = ldrs[2].fin()
            gz = dv(lambda e: e.memset(xn_fm[:, 0, :, 3:4], 0.0), waits=pre,
                    tag=f"{tag}gz")
            guard_m = {0: gz}
            st_ = {"tr_prev": None, "mix_prev": None, "ab_prev": None,
                   "c_prev": None, "rc_prev": None}
            gstate = {}

            def s12_gen(g):
                buf = g % 2
                aw = ([st_["tr_prev"]] if st_["tr_prev"] else [])
                xn_mg = [ln_apply(4 * g + t, t, mrns2[g], aw,
                                  tag=f"{tag}g{g}") for t in range(4)]
                fm_m, tr_last = fm_transpose_group(
                    g, buf, [0, 1, 2, 3], xn_mg[-2:], tag=f"{tag}g{g}")
                st_["tr_prev"] = tr_last
                yield
                if g + 1 < NG:
                    gc = dv(lambda e, buf=buf: e.tensor_copy(
                        xn_fm[:, 1 - buf, :, 3:4], xn_fm[:, buf, :, 515:516]),
                        waits=fm_m + ([st_["mix_prev"]] if st_["mix_prev"] else []),
                        tag=f"{tag}gc{g}")
                    guard_m[g + 1] = gc
                mw = fm_m + [guard_m[g]] + ([st_["ab_prev"]] if st_["ab_prev"] else [])
                mix_m = None
                for dc in range(NC):
                    wstep()
                    for u, mucol, omcol in ((0, 2, 0), (1, 3, 1)):
                        tsm = dv(lambda e, dc=dc, u=u, omcol=omcol, buf=buf:
                           e.tensor_scalar(
                               out=xrxk[:, u, dc, :],
                               in0=xn_fm[:, buf, dc, 3:515],
                               scalar1=dsm[:, omcol, dc:dc + 1], scalar2=None,
                               op0=ALU.mult),
                           waits=mw, tag=f"{tag}mx{g}_{dc}_{u}")
                        mix_m = dv(lambda e, dc=dc, u=u, mucol=mucol, buf=buf:
                                   e.scalar_tensor_tensor(
                                       out=xrxk[:, u, dc, :],
                                       in0=xn_fm[:, buf, dc, 4:516],
                                       scalar=smalls[:, mucol, dc:dc + 1],
                                       in1=xrxk[:, u, dc, :],
                                       op0=ALU.mult, op1=ALU.add),
                                   waits=[tsm], tag=f"{tag}mxs{g}_{dc}_{u}")
                    if dc % 2 == 1:
                        yield
                st_["mix_prev"] = mix_m
                gstate[g] = {"mix_m": mix_m}

            def s3(g):
                gs = gstate[g]
                sig_m = []
                ab_m = None
                for ec in range(NC):
                    wstep()
                    bnk = ec % 2
                    mms = [(warena[:, 0, dc, 128 * ec:128 * ec + 128],
                            xrxk[:, 0, dc, :]) for dc in range(NC)]
                    gm = mm_group(bnk, mms, waits=[gs["mix_m"], w_r],
                                  tag=f"{tag}A{g}_{ec}")
                    sig_m.append(evac(bnk, "scalar",
                                      lambda e, bnk=bnk, ec=ec: e.activation(
                                          out=r_fm[:, ec, :], in_=pb[bnk].ap(),
                                          func=AF.Sigmoid),
                                      waits=[gm] + ([st_["rc_prev"]] if st_["rc_prev"] else []),
                                      tag=f"{tag}Aev{g}_{ec}"))
                for ec in range(NC):
                    bnk = ec % 2
                    mms = [(warena[:, 1, dc, 128 * ec:128 * ec + 128],
                            xrxk[:, 1, dc, :]) for dc in range(NC)]
                    gm = mm_group(bnk, mms, waits=[w_k], tag=f"{tag}B{g}_{ec}")
                    ab_m = gm
                    evac(bnk, "scalar",
                         lambda e, bnk=bnk, ec=ec: e.activation(
                             out=relu_b[:, ec, :], in_=pb[bnk].ap(),
                             func=AF.Relu),
                         waits=[gm] + ([st_["c_prev"]] if st_["c_prev"] else []),
                         tag=f"{tag}Bev{g}_{ec}")
                st_["ab_prev"] = ab_m
                if g == NG - 1 and nxt:
                    arm(nxt[0], slot_free_gate())
                    arm(nxt[1], slot_free_gate())
                relu_m = p_.cur("sAC")
                k2_m = dv(lambda e: e.tensor_tensor(
                    out=relu_b[:, :, :], in0=relu_b[:, :, :],
                    in1=relu_b[:, :, :], op=ALU.mult),
                    waits=[relu_m], tag=f"{tag}k2{g}")
                gs["k2_m"] = k2_m
                gs["sig_m"] = sig_m

            def s4_gen(g):
                gs = gstate[g]
                rc_m = []
                cm = None
                for ec in range(NC):
                    wstep()
                    bnk = ec % 2
                    mms = [(warena[:, 2, dc, 128 * ec:128 * ec + 128],
                            relu_b[:, dc, :]) for dc in range(NC)]
                    gm = mm_group(bnk, mms, waits=[gs["k2_m"], w_v],
                                  tag=f"{tag}C{g}_{ec}")
                    cm = gm
                    rc_m.append(evac(bnk, "vector",
                                     lambda e, bnk=bnk, ec=ec: e.tensor_tensor(
                                         out=rc_fm[:, ec, :], in0=pb[bnk].ap(),
                                         in1=r_fm[:, ec, :], op=ALU.mult),
                                     waits=[gm, gs["sig_m"][ec]],
                                     tag=f"{tag}Cev{g}_{ec}"))
                    if ec % 2 == 1:
                        yield
                st_["c_prev"] = cm
                st_["rc_prev"] = rc_m[-1]
                if g == NG - 1 and nxt and nxt[2]:
                    arm(nxt[2], slot_free_gate())
                for t in range(4):
                    ra_ev = None
                    for half in range(2):
                        bnk = 2 + half
                        trs = []
                        for eci in range(4):
                            ec = 4 * half + eci
                            trs.append((pb[bnk].ap()[:, 128 * eci:128 * eci + 128],
                                        rc_fm[:, ec, 128 * t:128 * t + 128]))
                        tm = tr_group(bnk, trs, waits=[rc_m[-1]], dtype=F32,
                                      tag=f"{tag}rt{g}_{t}_{half}")
                        def f(e, bnk=bnk, g=g, t=t, half=half):
                            return e.tensor_tensor(
                                out=x_tm[:, 4 * g + t, 512 * half:512 * half + 512],
                                in0=pb[bnk].ap(),
                                in1=x_tm[:, 4 * g + t, 512 * half:512 * half + 512],
                                op=ALU.add)
                        ra_ev = evac(bnk, "vector", f, waits=[tm],
                                     tag=f"{tag}radd{g}_{t}_{half}")
                    if tile_hook is not None:
                        tile_hook(g, t, ra_ev)
                    yield

            def drive(*gens):
                gens = [g_ for g_ in gens if g_ is not None]
                while gens:
                    for g_ in list(gens):
                        try:
                            next(g_)
                        except StopIteration:
                            gens.remove(g_)

            drive(s12_gen(0))
            s3(0)
            for g in range(1, NG):
                drive(s12_gen(g), s4_gen(g - 1))
                s3(g)
            drive(s4_gen(NG - 1))

        if False and dbg and b == DBG_SEQ:
            dwv = p_.barrier_waits("sDV", "sAC", "sPE")
            def cast_dump(name, src_ap, waits):
                gp_dma(dbg_d[name], src_ap, waits=waits, tag=f"dbg{name}")
            cast_dump("stats", stats[:], dwv)
            cast_dump("xn", xn_tm[:], dwv)
            cast_dump("xpx", x_tm[:], dwv)
            cast_dump("K", K_fm, dwv)
            cast_dump("csum", colsum[:], dwv)
            cast_dump("zbuf", zbuf[:], dwv)
            cast_dump("E8", ebuf[:, 0, :], dwv)
            cast_dump("q2", warena[:, 2, :, 0:512], dwv)

        def emit_attn_b(tag, ldrs, nxt):
            pre = p_.barrier_waits("sDV", "sAC", "sPE", "sGP", "sGD")
            mrns3, _ = ln_stats(xpx=False, pre_waits=pre, tag=f"{tag}ln")
            w_qb = ldrs[0].fin()
            w_kb = ldrs[1].fin()
            w_vb = ldrs[2].fin()
            a3b = a3.ap().bitcast(BF16)
            a3f = a3.ap().bitcast(F32)
            sqkv = a3b[:, 0:12288].rearrange("p (u c s) -> p u c s",
                                             u=3, c=NC)
            Mf = a3f[:, 6144:7168].rearrange("p (c s) -> p c s", c=NC)
            Maug = a3b[:, 14336:15392].rearrange("p (c s) -> p c s", c=NC)
            zf = a3f[:, 7696:7704]
            ztmp = a3f[:, 7704:7712]
            dsc = a3f[:, 7712:7760]
            vd_b = xn_tm.ap()[:, 12, 0:128]
            tmpf = xn_tm.ap()[:, 12, :].bitcast(F32)[:, 128:256]
            adbuf_a = xn_tm.ap()[:, 13, :].bitcast(F32)
            adbuf_b = ebuf.ap()[:, 1, :].bitcast(F32)[:, 0:512]
            t1f = xn_tm.ap()[:, 14, :].bitcast(F32)
            t2f = xn_tm.ap()[:, 15, :].bitcast(F32)
            Ebuf = ebuf.ap()[:, 0, :]

            mi1 = dv(lambda e: e.memset(Mf, 0.0), waits=pre, tag=f"{tag}M0")
            mi2 = dv(lambda e: e.memset(Maug, 0.0), waits=pre, tag=f"{tag}Ma0")
            mi3 = dv(lambda e: e.memset(zf, 0.0), waits=pre, tag=f"{tag}z0")
            maug_ready = [mi1, mi2, mi3]

            tr_prev = None
            proj_war = []
            for G in range(NSEGS):
                buf = G % 2
                aw = ([tr_prev] if tr_prev else [])
                xn_mg = [ln_apply(4 * G + t, t, mrns3[G], aw,
                                  tag=f"{tag}g{G}") for t in range(4)]
                fm_m, tr_last = fm_transpose_group(G, buf, [0, 1, 2, 3],
                                                   xn_mg[-2:], tag=f"{tag}g{G}")
                tr_prev = tr_last
                ev_m = []
                for u, wm, bcol in ((0, w_qb, 4), (1, w_kb, 5), (2, w_vb, 6)):
                    for ec in range(NC):
                        bnk = ec % 2
                        mms = [(warena[:, u, dc, 128 * ec:128 * ec + 128],
                                xn_fm[:, buf, dc, 4:516]) for dc in range(NC)]
                        gm = mm_group(bnk, mms, waits=fm_m + [wm] + proj_war,
                                      tag=f"{tag}p{G}u{u}e{ec}")
                        ev_m.append(evac(bnk, "scalar",
                                    lambda e, bnk=bnk, u=u, ec=ec, bcol=bcol:
                                    e.activation(out=sqkv[:, u, ec, :],
                                                 in_=pb[bnk].ap(),
                                                 func=AF.Identity,
                                                 bias=smalls[:, bcol, ec:ec + 1]),
                                    waits=[gm], tag=f"{tag}pe{G}{u}{ec}"))
                ev_last = [ev_m[-1]]
                if G == NSEGS - 1 and nxt:
                    for L_ in nxt:
                        if L_ is not None:
                            arm(L_, slot_free_gate())
                vt_m = []
                for t in range(4):
                    bnk = 2 + t % 2
                    trs = [(pbb[bnk][:, 128 * ec:128 * ec + 128],
                            sqkv[:, 2, ec, 128 * t:128 * t + 128])
                           for ec in range(NC)]
                    tm_ = tr_group(bnk, trs, waits=ev_last,
                                   tag=f"{tag}vt{G}_{t}")
                    def f(e, bnk=bnk, t=t):
                        return e.tensor_copy(xn_tm[:, 4 + t, :],
                                             pbb[bnk][:, 0:1024])
                    vt_m.append(evac(bnk, "vector", f, waits=[tm_],
                                     tag=f"{tag}vte{G}_{t}"))
                vt_last = vt_m[-1]
                num_read_m = []
                mdelta_first = {6: True, 7: True}
                mdelta_last_m = None
                def passA_g(h):
                    omgcol = dsm[:, 3, h:h + 1]
                    adbuf = adbuf_a if h % 2 == 0 else adbuf_b
                    z2m = None
                    adm_l, pte_l = [], []
                    for lc in range(4):
                        par = lc % 4
                        Eslot = Ebuf[:, 512 * par:512 * (par + 1)]
                        sb_ = (h + lc) % 2
                        ncols = 128 * (lc + 1)
                        gm = mm_group(sb_, [(sqkv[:, 0, h, 128 * lc:128 * lc + 128],
                                             sqkv[:, 1, h, 0:ncols],
                                             pb[sb_].ap()[:, 0:ncols])],
                                      waits=ev_last + buf_war.get(("E", par), []),
                                      tag=f"{tag}s{G}_{h}_{lc}")
                        ee = ac(lambda e, sb_=sb_, ncols=ncols, Eslot=Eslot:
                                e.activation(out=Eslot[:, 0:ncols],
                                             in_=pb[sb_].ap()[:, 0:ncols],
                                             func=AF.Exp,
                                             scale=1.0 / np.sqrt(DK)),
                                waits=[gm], tag=f"{tag}e_{G}_{h}_{lc}")
                        bank_war[sb_] = ee
                        mk = dv(lambda e, lc=lc, Eslot=Eslot: e.tensor_tensor(
                            out=Eslot[:, 128 * lc:128 * lc + 128],
                            in0=Eslot[:, 128 * lc:128 * lc + 128],
                            in1=tril01[:], op=ALU.mult),
                            waits=[ee, init_gp], tag=f"{tag}mk{G}_{h}_{lc}")
                        z2m = dv(lambda e, lc=lc, ncols=ncols, Eslot=Eslot:
                                 e.tensor_reduce(dsc[:, 40 + lc:41 + lc],
                                                 Eslot[:, 0:ncols],
                                                 axis=AX.X, op=ALU.add),
                                 waits=[mk], tag=f"{tag}zb2_{G}_{h}_{lc}")
                        trs = [(pbb[2][:, 128 * j:128 * j + 128],
                                Eslot[:, 128 * j:128 * j + 128])
                               for j in range(lc + 1)]
                        ptm = tr_group(2, trs, waits=[mk], tag=f"{tag}pt{G}_{h}_{lc}")
                        pte_eng = "scalar" if lc % 2 else "vector"
                        if pte_eng == "scalar":
                            def fpt(e, lc=lc):
                                return e.copy(ptbuf[:, 0:lc + 1, :],
                                              pbb[2][:, 0:128 * (lc + 1)])
                        else:
                            def fpt(e, lc=lc):
                                return e.tensor_copy(ptbuf[:, 0:lc + 1, :],
                                                     pbb[2][:, 0:128 * (lc + 1)])
                        pte = evac(2, pte_eng, fpt,
                                   waits=[ptm] + buf_war.get("ptbuf", []),
                                   tag=f"{tag}pte{G}_{h}_{lc}")
                        pte_l.append(pte)
                        buf_war[("E", par)] = [ptm]
                        for j in range(lc + 1):
                            st = (lc == 0 and j == 0)
                            sp = (lc == 3 and j == 3)
                            def fad(e, j=j, lc=lc, h=h, st=st, sp=sp):
                                return e.matmul(
                                    pb[4].ap()[:, 128 * lc:128 * lc + 128],
                                    ptbuf[:, j, :],
                                    xn_tm[:, 4 + j, 128 * h:128 * h + 128],
                                    start=st, stop=sp)
                            w = [pte] if j == 0 else []
                            if st:
                                w = w + [vt_last] + buf_war.get("bank4", [])
                            adm = p_.add("tensor", fad, waits=w,
                                         inc=("sPE" if j == lc else None),
                                         tag=f"{tag}ad{G}_{h}_{lc}_{j}")
                        adm_l.append(adm)
                        buf_war["ptbuf"] = [adm]
                        yield
                    rzb = dv(lambda e: e.reciprocal(dsc[:, 36:40],
                                                    dsc[:, 40:44]),
                             waits=[z2m], tag=f"{tag}rz_{G}_{h}")
                    sdb = dv(lambda e, omgcol=omgcol: e.tensor_scalar(
                        out=dsc[:, 44:48], in0=dsc[:, 36:40],
                        scalar1=omgcol, scalar2=None, op0=ALU.mult),
                        waits=[rzb] + buf_war.get("zb_sd", []),
                        tag=f"{tag}sd_{G}_{h}")
                    ads = None
                    for lc in range(4):
                        ads = evac(4, "scalar",
                                   lambda e, lc=lc: e.activation(
                                       out=adbuf[:, 128 * lc:128 * lc + 128],
                                       in_=pb[4].ap()[:, 128 * lc:128 * lc + 128],
                                       func=AF.Identity,
                                       scale=dsc[:, 44 + lc:45 + lc]),
                                   waits=[adm_l[3], sdb] +
                                         buf_war.get(("ad", h % 2, lc), []),
                                   tag=f"{tag}ads{G}_{h}_{lc}")
                    buf_war["zb_sd"] = [ads]
                    buf_war["bank4"] = [ads]
                    sc_pe[h] = p_.cur("sPE")

                def passB_g(h):
                    nonlocal mdelta_last_m
                    gcol = dsm[:, 2, h:h + 1]
                    adbuf = adbuf_a if h % 2 == 0 else adbuf_b
                    sc_m = sc_pe[h]
                    t12 = xn_tm.ap()[:, 14:16, :].bitcast(F32)
                    squ = sqkv[:, 0:2, h, :]
                    a1 = ac(lambda e, h=h: e.activation(
                        out=t12, in_=squ, func=AF.Relu, scale=-1.0),
                        waits=[("sPE", sc_m[1])] + buf_war.get("t1f", []),
                        tag=f"{tag}tr1_{G}_{h}")
                    a2 = ac(lambda e: e.activation(out=t12, in_=t12,
                                                   func=AF.Exp, scale=-1.0),
                            waits=[a1], tag=f"{tag}tr2_{G}_{h}")
                    el = dv(lambda e, h=h: e.scalar_tensor_tensor(
                        out=squ, in0=squ, scalar=0.0, in1=t12,
                        op0=ALU.max, op1=ALU.add),
                        waits=[a2], tag=f"{tag}tr4_{G}_{h}")
                    buf_war["t1f"] = [el]
                    el_m = [el, el]
                    yield
                    sktr = tr_group(3, [(pbb[3][:, 128 * t:128 * t + 128],
                                         sqkv[:, 1, h, 128 * t:128 * t + 128])
                                        for t in range(4)],
                                    waits=[el_m[1]], tag=f"{tag}skt{G}_{h}")
                    skte = evac(3, "scalar",
                                lambda e, h=h: e.copy(
                                    xn_tm.ap()[:, 8:12, 128 * h:128 * h + 128],
                                    pbb[3][:, 0:512].rearrange(
                                        "p (t c) -> p t c", t=4)),
                                waits=[sktr], tag=f"{tag}skte{G}_{h}")
                    zt_hold[0] = dv(lambda e, h=h: e.tensor_reduce(
                        ztmp[:, h:h + 1], sqkv[:, 1, h, :], axis=AX.X,
                        op=ALU.add), waits=[el_m[1]], tag=f"{tag}zt{G}_{h}")
                    yield
                    denm = mm_group(5, [(sqkv[:, u, h, 128 * lc:128 * lc + 128],
                                         Maug[:, h, 128:129],
                                         pb[5].ap()[:, 2 * lc + u:2 * lc + u + 1])
                                        for lc in range(4) for u in range(2)],
                                    waits=maug_ready + [el_m[1]],
                                    tag=f"{tag}den{G}_{h}")
                    num_read_m.append(denm)
                    dev = evac(5, "scalar", lambda e: e.activation(
                        out=dsc[:, 16:24], in_=pb[5].ap()[:, 0:8],
                        func=AF.Identity, bias=bepscol[:, 0:1]),
                        waits=[denm] + buf_war.get("dsc16", []),
                        tag=f"{tag}dev{G}_{h}")
                    rcp = dsc[:, 24:32].rearrange("p (l u) -> p l u", u=2)
                    rbm = dv(lambda e: e.reciprocal(dsc[:, 24:32],
                                                    dsc[:, 16:24]),
                             waits=[dev], tag=f"{tag}rb{G}_{h}")
                    buf_war["dsc16"] = [rbm]
                    gqb = dv(lambda e, gcol=gcol: e.tensor_scalar(
                        out=dsc[:, 32:36], in0=rcp[:, :, 0:1],
                        scalar1=gcol, scalar2=None, op0=ALU.mult),
                        waits=[rbm], tag=f"{tag}gq{G}_{h}")
                    yield
                    c1_last = None
                    for lc in range(4):
                        nb = 3 if lc % 2 == 0 else 5
                        nq = [(sqkv[:, 0, h, 128 * lc:128 * lc + 128],
                               Maug[:, h, 0:128], pb[nb].ap()[:, 0:128]),
                              (sqkv[:, 1, h, 128 * lc:128 * lc + 128],
                               Maug[:, h, 0:128], pb[nb].ap()[:, 128:256])]
                        nm = mm_group(nb, nq, waits=maug_ready + [el_m[1], skte],
                                      tag=f"{tag}nm{G}_{h}_{lc}")
                        num_read_m.append(nm)
                        vdp = xn_tm.ap()[:, 12, 128 * (lc % 2):
                                         128 * (lc % 2) + 128]
                        vdm = dv(lambda e, h=h, lc=lc, vdp=vdp, nb=nb:
                                 e.scalar_tensor_tensor(
                                     out=vdp, in0=pb[nb].ap()[:, 128:256],
                                     scalar=rcp[:, lc, 1:2],
                                     in1=xn_tm[:, 4 + lc, 128 * h:128 * h + 128],
                                     op0=ALU.mult, op1=ALU.subtract),
                                 waits=[nm, gqb] + buf_war.get(("vd", lc % 2), []),
                                 tag=f"{tag}vd_{G}_{h}_{lc}")
                        mb = 6 + h // 4
                        def fmd(e, h=h, lc=lc, mb=mb, vdp=vdp,
                                st=mdelta_first[mb]):
                            return e.matmul(
                                pb[mb].ap()[:, 128 * (h % 4):128 * (h % 4) + 128],
                                xn_tm[:, 8 + lc, 128 * h:128 * h + 128],
                                vdp, start=st,
                                stop=(h % 4 == 3 and lc == 3))
                        mw = [vdm, skte]
                        if mdelta_first[mb] and mb in bank_war and bank_war[mb]:
                            mw.append(bank_war[mb])
                        mdelta_last_m = p_.add("tensor", fmd, waits=mw,
                                               inc="sPE",
                                               tag=f"{tag}md{G}_{h}_{lc}")
                        mdelta_first[mb] = False
                        buf_war[("vd", lc % 2)] = [mdelta_last_m]
                        c1_last = dv(lambda e, h=h, lc=lc, nb=nb:
                                     e.scalar_tensor_tensor(
                                         out=adbuf[:, 128 * lc:128 * lc + 128],
                                         in0=pb[nb].ap()[:, 0:128],
                                         scalar=dsc[:, 32 + lc:33 + lc],
                                         in1=adbuf[:, 128 * lc:128 * lc + 128],
                                         op0=ALU.mult, op1=ALU.add),
                                     waits=[vdm], tag=f"{tag}c1_{G}_{h}_{lc}")
                        bank_war[nb] = c1_last
                        yield
                    c2 = dv(lambda e, G=G, h=h, adbuf=adbuf: e.tensor_tensor(
                        out=x_tm[:, 4 * G:4 * G + 4, 128 * h:128 * h + 128],
                        in0=adbuf.rearrange("p (l c) -> p l c", l=4),
                        in1=x_tm[:, 4 * G:4 * G + 4, 128 * h:128 * h + 128],
                        op=ALU.add), waits=[c1_last],
                        tag=f"{tag}c2_{G}_{h}")
                    for lc in range(4):
                        buf_war[("ad", h % 2, lc)] = [c2]
                sc_pe = {}
                zt_hold = [None]
                for _ in passA_g(0):
                    pass
                for h in range(H):
                    wstep()
                    gens = [passB_g(h)]
                    if h + 1 < H:
                        gens.append(passA_g(h + 1))
                    while gens:
                        for g_ in list(gens):
                            try:
                                next(g_)
                            except StopIteration:
                                gens.remove(g_)
                if n_sub >= 4:
                    for lc in range(4):
                        stats_tile(4 * G + lc, buf_war[("ad", 1, lc)],
                                   f"{tag}pf_")
                zu = dv(lambda e: e.tensor_tensor(out=zf, in0=zf, in1=ztmp,
                                                  op=ALU.add),
                        waits=[zt_hold[0]] + num_read_m[-1:], tag=f"{tag}zu{G}")
                zc = dv(lambda e: e.tensor_copy(
                    Maug[:, :, 128:129],
                    zf.rearrange("p (c o) -> p c o", o=1)),
                        waits=[zu], tag=f"{tag}zc{G}")
                m1_ = dv(lambda e: e.tensor_tensor(out=Mf[:, 0:4, :],
                                                   in0=Mf[:, 0:4, :],
                                                   in1=pb[6].ap(),
                                                   op=ALU.subtract),
                         waits=[mdelta_last_m], tag=f"{tag}mu{G}a")
                m2_ = dv(lambda e: e.tensor_tensor(out=Mf[:, 4:8, :],
                                                   in0=Mf[:, 4:8, :],
                                                   in1=pb[7].ap(),
                                                   op=ALU.subtract),
                         waits=[m1_], tag=f"{tag}mu{G}b")
                bank_war[6] = m2_
                bank_war[7] = m2_
                mc = dv(lambda e: e.tensor_copy(Maug[:, :, 0:128], Mf),
                        waits=[m2_] + num_read_m[-1:], tag=f"{tag}mc{G}")
                maug_ready = [mc, zc]
                proj_war = [p_.cur("sPE"), p_.cur("sDV")]

        ab_ldrs = None
        if n_sub >= 3:
            ab_ldrs = (LoadW(wqb_d, 0, 0, f"b{b}ab_ldwq"),
                       LoadW(wkb_d, 1, 1, f"b{b}ab_ldwk"),
                       LoadW(wvb_d, 2, 1, f"b{b}ab_ldwv"))
        f2_ldrs = None
        if n_sub >= 4:
            f2_ldrs = (LoadW(wrT_d, 0, 0, f"b{b}f2_ldwr"),
                       LoadW(wkfT_d, 1, 1, f"b{b}f2_ldwk"),
                       LoadW(wvfT_d, 2, 1, f"b{b}f2_ldwv"))
        nseq = None
        if n_sub >= 4 and b + 1 < BL:
            carry_qa = LoadW(wqaT_d, 0, 0, f"b{b + 1}ldwqa")
            carry_ka = LoadW(wkaT_d, 1, 1, f"b{b + 1}ldwka")
            nseq = (carry_qa, carry_ka, None)
        def f2_hook(g, t, m, b=b):
            i = 4 * g + t
            otm = out_d[b].rearrange("(t p) d -> p t d", p=P)
            sy_dma(otm[:, i:i + 1, :], x_tm[:, i:i + 1, :], waits=[m],
                   store=True, tag=f"st{b}_{i}")
            if b + 1 < BL:
                itm = x_d[b + 1].rearrange("(t p) d -> p t d", p=P)
                ld = sy_dma(x_tm[:, i:i + 1, :], itm[:, i:i + 1, :],
                            tag=f"ldx{b + 1}_{i}")
                stats_tile(i, [ld], f"b{b + 1}ln1_")

        f1_hook = None
        if n_sub >= 3:
            def f1_hook(g, t, m):
                stats_tile(4 * g + t, [m], f"b{b}ab_")
        if n_sub >= 2:
            emit_ffn(f"b{b}f1_", f1_ldrs, ab_ldrs, tile_hook=f1_hook,
                     dbg_g0=(dbg and b == DBG_SEQ))
        if n_sub >= 3:
            emit_attn_b(f"b{b}ab_", ab_ldrs, f2_ldrs)
        if n_sub >= 4:
            emit_ffn(f"b{b}f2_", f2_ldrs, nseq, tile_hook=f2_hook)
            NSTORE += NT
        assert not armed or (n_sub >= 4 and b + 1 < BL), \
            f"unarmed/armed leftovers at b={b}: {[L.tag for L in armed]}"

        if n_sub < 4:
            st_w = p_.barrier_waits("sDV", "sAC", "sPE")
            sy_dma(out_d[b].rearrange("(t p) d -> p t d", p=P), x_tm[:],
                   waits=st_w, store=True, tag=f"st{b}")
            NSTORE += 1

    p_.add("sync", lambda e: e.nop(),
           waits=[("sST", 16 * NSTORE), p_.cur("sGD"), p_.cur("sXD")],
           tag="final")

    p_.simulate()

    with ctx:
        blk = ctx.enter_context(nc.Block())

        def make_body(eng_name):
            def body(eng):
                for (fn, w, incs, tag) in p_.ops[eng_name]:
                    for (s, v) in w:
                        eng.wait_ge(sems[s], v)
                    inst = fn(eng)
                    for (s, v, a) in incs:
                        if a:
                            inst = inst.then_inc(sems[s], v)
            return body

        blk.sync(make_body("sync"))
        blk.gpsimd(make_body("gpsimd"))
        blk.tensor(make_body("tensor"))
        blk.vector(make_body("vector"))
        blk.scalar(make_body("scalar"))

    return nc


def _host_layout(inputs):
    g = lambda k: np.asarray(inputs[k])
    smalls = np.zeros((P, 8, NC), dtype=np.float32)
    def fm(v):
        return np.ascontiguousarray(v.reshape(NC, P).T)
    smalls[:, 0, :] = fm(g("L0_attn_a_bq"))
    smalls[:, 1, :] = fm(g("L0_attn_a_bk"))
    smalls[:, 2, :] = fm(g("L0_ffn_mu_r"))
    smalls[:, 3, :] = fm(g("L0_ffn_mu_k"))
    smalls[:, 4, :] = fm(g("L0_attn_b_bq"))
    smalls[:, 5, :] = fm(g("L0_attn_b_bk"))
    smalls[:, 6, :] = fm(g("L0_attn_b_bv"))
    smalls[:, 7, :] = np.broadcast_to(g("L0_attn_b_beta")[None, :], (P, 8))
    T = lambda a: np.ascontiguousarray(a.T)
    return {
        "wqaT": T(g("L0_attn_a_Wq")), "wkaT": T(g("L0_attn_a_Wk")),
        "wrT": T(g("L0_ffn_W_r")), "wkfT": T(g("L0_ffn_W_k")),
        "wvfT": T(g("L0_ffn_W_v")),
        "wqb": np.ascontiguousarray(g("L0_attn_b_Wq")),
        "wkb": np.ascontiguousarray(g("L0_attn_b_Wk")),
        "wvb": np.ascontiguousarray(g("L0_attn_b_Wv")),
        "smalls": smalls,
    }


N_SUB = 4

def kernel(**inputs):
    x = np.ascontiguousarray(np.asarray(inputs["L0_attn_a_x"], dtype=np.float32))
    shared = _host_layout(inputs)
    nc = build_nc(n_sub=N_SUB)
    in_maps = []
    for i in range(NCORES):
        m = dict(shared)
        m["x"] = np.ascontiguousarray(x[BL * i:BL * (i + 1)])
        in_maps.append(m)
    res = run_bass_kernel_spmd(nc, in_maps, core_ids=list(range(NCORES)))
    return np.concatenate([r["out"] for r in res.results], axis=0)


if __name__ == "__main__":
    nc = build_nc()
    print("built OK")
